# version 48
# baseline (speedup 1.0000x reference)
import contextlib
import numpy as np
import concourse.bass as bass
import concourse.mybir as mybir
from concourse.bass_utils import run_bass_kernel_spmd

F32 = mybir.dt.float32
BF16 = mybir.dt.bfloat16
AF = mybir.ActivationFunctionType
OP = mybir.AluOpType
AX = mybir.AxisListType
LN2 = float(np.log(2.0))

SKIP_SELF_WAITS = False
ENGINES = ("sync", "scalar", "vector", "tensor", "gpsimd")
ESEM = {"sync": "s_ld", "scalar": "s_sc", "vector": "s_ve",
        "tensor": "s_pe", "gpsimd": "s_po"}
EDSEM = {"sync": "s_ld", "scalar": "s_scd", "gpsimd": "s_pod"}


class Buf:
    __slots__ = ("t", "w", "rs", "name")

    def __init__(self, t, name=""):
        self.t = t
        self.w = None
        self.rs = []
        self.name = name


class Prog:
    NDMA = 32

    def __init__(self, nc):
        self.nc = nc
        self.plans = {e: [] for e in ENGINES}
        self.sems = {}
        self.lastw = {e: {} for e in ENGINES}
        self.fence_toks = []
        self.n_ops = 0
        self.dma_rr = {"hw": 0, "sw": 0}

    def next_dma_sem(self, kind):
        name = f"d{kind}{self.dma_rr[kind] % self.NDMA}"
        self.dma_rr[kind] += 1
        return name

    def sem(self, name):
        if name not in self.sems:
            self.sems[name] = [self.nc.alloc_semaphore(name), 0]
        return self.sems[name]

    def fence(self):
        self.fence_toks = [(n, rec[1]) for n, rec in self.sems.items()
                           if rec[1] > 0]

    def op(self, eng, fn, reads=(), writes=(), dma=False, cc=False,
           extra_waits=()):
        if cc:
            sname = "s_cc"
            inc = 1
        elif dma:
            sname = self.next_dma_sem("sw" if eng == "gpsimd" else "hw")
            inc = 16
        else:
            sname = ESEM[eng]
            inc = 1
        rec = self.sem(sname)
        waits = {}

        def addw(tok):
            if tok is None:
                return
            s, v = tok
            if waits.get(s, 0) < v:
                waits[s] = v
        for b in reads:
            addw(b.w)
        for b in writes:
            addw(b.w)
            for t in b.rs:
                addw(t)
        for t in extra_waits:
            addw(t)
        for t in self.fence_toks:
            addw(t)
        rec[1] += inc
        mytok = (sname, rec[1])
        for b in writes:
            b.w = mytok
            b.rs = []
        for b in reads:
            b.rs.append(mytok)
        if SKIP_SELF_WAITS:
            own = ESEM[eng]
            waits.pop(own, None)
        wl = sorted(waits.items())
        prog = self

        def emit(wl=wl, fn=fn, sname=sname, inc=inc, eng=eng):
            e = getattr(prog.nc, eng)
            for s, v in wl:
                if prog.lastw[eng].get(s, 0) < v:
                    e.wait_ge(prog.sems[s][0], v)
                    prog.lastw[eng][s] = v
            inst = fn()
            inst.then_inc(prog.sems[sname][0], inc)
        self.plans[eng].append(emit)
        self.n_ops += 1
        return mytok

    def gpdma_group(self, items):
        sname = self.next_dma_sem("sw")
        rec = self.sem(sname)
        waits = {}

        def addw(tok):
            if tok is None:
                return
            sx, v = tok
            if waits.get(sx, 0) < v:
                waits[sx] = v
        allw = []
        for fn, writes in items:
            for b in writes:
                addw(b.w)
                for t in b.rs:
                    addw(t)
                allw.append(b)
        for t in self.fence_toks:
            addw(t)
        rec[1] += 16 * len(items)
        otok = (sname, rec[1])
        for b in allw:
            b.w = otok
            b.rs = []
        wl = sorted(waits.items())
        prog = self

        def emit(wl=wl, items=items, sname=sname):
            e = prog.nc.gpsimd
            for sx, v in wl:
                if prog.lastw["gpsimd"].get(sx, 0) < v:
                    e.wait_ge(prog.sems[sx][0], v)
                    prog.lastw["gpsimd"][sx] = v
            for fn, _w in items:
                fn().then_inc(prog.sems[sname][0], 16)
        self.plans["gpsimd"].append(emit)
        self.n_ops += len(items)
        return otok

    def finish(self, final_tokens):
        waits = {}
        for s, v in final_tokens:
            waits[s] = max(waits.get(s, 0), v)

        def emit():
            for s, v in sorted(waits.items()):
                self.nc.sync.wait_ge(self.sems[s][0], v)
        self.plans["sync"].append(emit)

    def run_block(self):
        nc = self.nc
        with nc.Block() as block:
            @block.sync
            def _(_e):
                for f in self.plans["sync"]:
                    f()

            @block.scalar
            def _(_e):
                for f in self.plans["scalar"]:
                    f()

            @block.vector
            def _(_e):
                for f in self.plans["vector"]:
                    f()

            @block.tensor
            def _(_e):
                for f in self.plans["tensor"]:
                    f()

            @block.gpsimd
            def _(_e):
                for f in self.plans["gpsimd"]:
                    f()


def qblocks_for_parity(NB, parity):
    out = []
    for k in range(NB // 2):
        if (k % 2) == parity:
            out.append(k)
            out.append(NB - 1 - k)
    return sorted(out, key=lambda j: ((j + 2) // 2, j))


def even_klen(j):
    return 128 * (((j + 1) + 1) // 2 * 2)


def build_program(S, H, NH, NKV, I, n_cores=8, eps_rms=1e-6, eps_ln=1e-5):
    D = 128
    HC = H // 128
    NB = S // 128
    NQ = NB // 2
    TQ = NQ * 128
    NTB = TQ // 128
    GR = NH // NKV
    KVW = NKV * D
    OCQ = (NH * D) // 512
    IQ = min(512, I)
    NIQ = I // IQ

    NT_WG = I // 128
    NT_WD = (I // H) * (H // 128)
    TOT_T = 2 * NT_WG + NT_WD
    TPC = TOT_T // n_cores
    assert TPC * n_cores == TOT_T and TPC % 2 == 0
    WU0 = NT_WG * 128
    WD0 = 2 * NT_WG * 128

    KLENS = [even_klen(j) for j in qblocks_for_parity(NB, 0)]
    assert KLENS == [even_klen(j) for j in qblocks_for_parity(NB, 1)]

    nc = bass.Bass("TRN2", target_bir_lowering=False,
                   detect_race_conditions=not SKIP_SELF_WAITS)
    p = Prog(nc)

    x_d = nc.dram_tensor("x", [S, H], F32, kind="ExternalInput").ap()
    xq_d = nc.dram_tensor("xqrows", [TQ, H], F32, kind="ExternalInput").ap()
    qpos_d = nc.dram_tensor("qpos", [1, TQ], F32, kind="ExternalInput").ap()
    wq_d = nc.dram_tensor("wq", [H, NH * D], F32, kind="ExternalInput").ap()
    wk_d = nc.dram_tensor("wk", [H, KVW], F32, kind="ExternalInput").ap()
    wv_d = nc.dram_tensor("wv", [H, KVW], F32, kind="ExternalInput").ap()
    wo_d = nc.dram_tensor("wo", [NH * D, H], F32, kind="ExternalInput").ap()
    gq_d = nc.dram_tensor("gq", [1, D], F32, kind="ExternalInput").ap()
    gk_d = nc.dram_tensor("gk", [1, D], F32, kind="ExternalInput").ap()
    wsh_d = nc.dram_tensor("wshard", [TPC * 128, H], F32,
                           kind="ExternalInput").ap()
    out_d = nc.dram_tensor("out", [TQ, H], F32, kind="ExternalOutput").ap()
    qsh_d = nc.dram_tensor("qshard", [TPC * 128, H], BF16,
                           kind="Internal").ap()
    qfull_d = nc.dram_tensor("qfull", [TOT_T * 128, H], BF16,
                             kind="Internal").ap()
    qsh_b = Buf(None, "qshard")
    qfull_b = Buf(None, "qfull")

    stack = contextlib.ExitStack()

    def sb(name, shape, dt, st=None):
        t = (st or stack).enter_context(nc.sbuf_tensor(name, shape, dt))
        return Buf(t, name)

    def ps(name, shape, dt):
        t = stack.enter_context(nc.psum_tensor(name, shape, dt))
        return Buf(t, name)

    cpool = sb("cpool", [128, 16], F32)
    CVALS = {}

    def cap(val):
        val = float(val)
        if val not in CVALS:
            CVALS[val] = len(CVALS)
            assert len(CVALS) <= 16
        return cpool.t[:, CVALS[val]:CVALS[val] + 1]

    ident = sb("ident", [128, 128], BF16)
    gq_b = sb("gq_t", [128, D], F32)
    gk_b = sb("gk_t", [128, D], F32)
    qpos_b = sb("qpos_t", [128, NQ], F32)
    x1 = sb("x1_all", [128, NQ * H], BF16)
    lnstat = sb("lnstat", [128, 8], F32)
    lnstat1 = sb("lnstat1", [128, 8], F32)
    lnstat2 = [lnstat, lnstat1]
    qm0 = {n: sb("qm_" + n, [128, 128], F32) for n in
           ("amax", "z", "rem", "fl", "invs", "back")}

    psA = [ps(f"psA{i}", [128, 512], F32) for i in (0, 1)]
    psB = [ps(f"psB{i}", [128, 512], F32) for i in (0, 1)]
    psT = [ps(f"psT{i}", [128, 512], BF16) for i in (0, 1)]
    psC = [ps(f"psC{i}", [128, 512], F32) for i in (0, 1)]

    def transpose_to(dst_buf, dst_ap, src_buf, src_ap, slot):
        tp = psT[slot % 2]
        p.op("tensor", lambda tp=tp, src_ap=src_ap: nc.tensor.transpose(
            tp.t[:, 0:128], src_ap, ident.t[:, :]),
            reads=[src_buf, ident], writes=[tp])
        p.op("vector", lambda tp=tp, dst_ap=dst_ap: nc.vector.tensor_copy(
            dst_ap, tp.t[:, 0:128]), reads=[tp], writes=[dst_buf])

    def ap3(t, off, stride, n):
        F = int(t.shape[1])
        return bass.AP(t, off, [[F, 128], [stride, n], [1, 128]])

    def transpose_batch(dst_buf, dst_t, dst_off, dst_stride, src_buf, src_t,
                        src_off, n, slot, src_stride=128):
        tp = psT[slot % 2]

        def f_tr(tp=tp, n=n, src_t=src_t, src_off=src_off,
                 src_stride=src_stride):
            last = None
            for i in range(n):
                Fs = int(src_t.shape[1])
                sap = bass.AP(src_t, src_off + i * src_stride,
                              [[Fs, 128], [1, 128]])
                last = nc.tensor.transpose(
                    tp.t[:, i * 128:(i + 1) * 128], sap, ident.t[:, :])
            return last
        p.op("tensor", f_tr, reads=[src_buf, ident], writes=[tp])

        def f_ev(tp=tp, n=n, dst_t=dst_t, dst_off=dst_off,
                 dst_stride=dst_stride):
            return nc.vector.tensor_copy(
                ap3(dst_t, dst_off, dst_stride, n), tp.t[:, :n * 128])
        p.op("vector", f_ev, reads=[tp], writes=[dst_buf])

    def g_maps(src, F, kind, qm):
        nb = F // 32
        src3 = src.t[:, :F].rearrange("p (n b) -> p n b", b=32)
        bias_i = cap(((2 if kind == 'w' else 5) + 64) * LN2)
        bias_b = cap(-((3 if kind == 'w' else 5) + 64) * LN2)
        am, z, rem, fl = qm["amax"], qm["z"], qm["rem"], qm["fl"]
        invs, back = qm["invs"], qm["back"]
        yield p.op("vector", lambda: nc.vector.tensor_reduce(
            am.t[:, :nb], src3, AX.X, OP.max, apply_absolute_value=True),
            reads=[src], writes=[am])
        yield p.op("vector", lambda: nc.vector.tensor_scalar_max(
            am.t[:, :nb], am.t[:, :nb], float(2.0 ** -60)),
            reads=[am], writes=[am])
        yield p.op("scalar", lambda: nc.scalar.activation(
            z.t[:, :nb], am.t[:, :nb], AF.Ln),
            reads=[am, cpool], writes=[z])
        yield p.op("vector", lambda: nc.vector.tensor_scalar(
            rem.t[:, :nb], z.t[:, :nb], 1.0 / LN2, 64.0, OP.mult, OP.add),
            reads=[z], writes=[rem])
        yield p.op("vector", lambda: nc.vector.tensor_copy(
            z.t[:, :nb], rem.t[:, :nb]), reads=[rem], writes=[z])
        for k in (64.0, 32.0, 16.0, 8.0, 4.0, 2.0, 1.0):
            yield p.op("vector", lambda k=k: nc.vector.tensor_single_scalar(
                fl.t[:, :nb], rem.t[:, :nb], k, OP.is_ge),
                reads=[rem], writes=[fl])
            yield p.op("vector", lambda k=k: nc.vector.scalar_tensor_tensor(
                rem.t[:, :nb], fl.t[:, :nb], -k, rem.t[:, :nb],
                OP.mult, OP.add), reads=[fl, rem], writes=[rem])
        yield p.op("vector", lambda: nc.vector.tensor_sub(
            fl.t[:, :nb], z.t[:, :nb], rem.t[:, :nb]),
            reads=[z, rem], writes=[fl])
        yield p.op("scalar", lambda: nc.scalar.activation(
            invs.t[:, :nb], fl.t[:, :nb], AF.Exp,
            scale=cap(-LN2), bias=bias_i), reads=[fl, cpool], writes=[invs])
        yield p.op("scalar", lambda: nc.scalar.activation(
            back.t[:, :nb], fl.t[:, :nb], AF.Exp,
            scale=cap(LN2), bias=bias_b), reads=[fl, cpool], writes=[back])

    def g_quant(scr, src, out_buf, out_ap, F, kind, qm=None,
                so=0, nbo=0, maps=True):
        if qm is None:
            qm = qm0
        nb = F // 32
        if maps:
            yield from g_maps(src, F, kind, qm)
        ibc = qm["invs"].t[:, nbo:nbo + nb].broadcast_to([128, nb, 32])
        bbc = qm["back"].t[:, nbo:nbo + nb].broadcast_to([128, nb, 32])
        s3 = src.t[:, so:so + F].rearrange("p (n b) -> p n b", b=32)
        y, u, a, b, c, d = (scr[k] for k in "yuabcd")
        y3 = y.t[:, :F].rearrange("p (n b) -> p n b", b=32)
        o3 = out_ap.rearrange("p (n b) -> p n b", b=32)
        d3 = d.t[:, :F].rearrange("p (n b) -> p n b", b=32)
        V = "vector"
        yield p.op(V, lambda: nc.vector.tensor_tensor(y3, s3, ibc, OP.mult),
             reads=[src, qm["invs"]], writes=[y])
        if kind == 'w':
            yield p.op("scalar", lambda: nc.scalar.activation(
                u.t[:, :F], y.t[:, :F], AF.Abs),
                reads=[y, cpool], writes=[u])
            yield p.op(V, lambda: nc.vector.tensor_single_scalar(
                a.t[:, :F], u.t[:, :F], 0.25, OP.is_ge),
                reads=[u], writes=[a])
            for t in (0.75, 1.25, 1.75):
                yield p.op(V, lambda t=t: nc.vector.scalar_tensor_tensor(
                    a.t[:, :F], u.t[:, :F], t, a.t[:, :F], OP.is_ge, OP.add),
                    reads=[u, a], writes=[a])
            yield p.op(V, lambda: nc.vector.tensor_single_scalar(
                b.t[:, :F], u.t[:, :F], 2.5, OP.is_ge),
                reads=[u], writes=[b])
            yield p.op(V, lambda: nc.vector.scalar_tensor_tensor(
                b.t[:, :F], u.t[:, :F], 3.5, b.t[:, :F], OP.is_ge, OP.add),
                reads=[u, b], writes=[b])
            yield p.op(V, lambda: nc.vector.tensor_single_scalar(
                c.t[:, :F], u.t[:, :F], 5.0, OP.is_ge),
                reads=[u], writes=[c])
            yield p.op(V, lambda: nc.vector.scalar_tensor_tensor(
                a.t[:, :F], b.t[:, :F], 2.0, a.t[:, :F], OP.mult, OP.add),
                reads=[b, a], writes=[a])
            yield p.op(V, lambda: nc.vector.scalar_tensor_tensor(
                a.t[:, :F], c.t[:, :F], 4.0, a.t[:, :F], OP.mult, OP.add),
                reads=[c, a], writes=[a])
            qmag = a
        else:
            yield p.op("scalar", lambda: nc.scalar.activation(
                u.t[:, :F], y.t[:, :F], AF.Abs),
                reads=[y, cpool], writes=[u])
            yield p.op(V, lambda: nc.vector.tensor_single_scalar(
                u.t[:, :F], u.t[:, :F], 61.9, OP.min),
                reads=[u], writes=[u])
            yield p.op(V, lambda: nc.vector.tensor_single_scalar(
                a.t[:, :F], u.t[:, :F], 16.0, OP.is_ge),
                reads=[u], writes=[a])
            yield p.op(V, lambda: nc.vector.tensor_single_scalar(
                b.t[:, :F], u.t[:, :F], 32.0, OP.is_ge),
                reads=[u], writes=[b])
            yield p.op(V, lambda: nc.vector.tensor_scalar(
                c.t[:, :F], a.t[:, :F], -0.5, 1.0, OP.mult, OP.add),
                reads=[a], writes=[c])
            yield p.op(V, lambda: nc.vector.scalar_tensor_tensor(
                c.t[:, :F], b.t[:, :F], -0.25, c.t[:, :F], OP.mult, OP.add),
                reads=[b, c], writes=[c])
            yield p.op(V, lambda: nc.vector.scalar_tensor_tensor(
                b.t[:, :F], b.t[:, :F], 2.0, a.t[:, :F], OP.mult, OP.add),
                reads=[a, b], writes=[b])
            yield p.op(V, lambda: nc.vector.tensor_mul(
                d.t[:, :F], c.t[:, :F], u.t[:, :F]),
                reads=[c, u], writes=[d])
            yield p.op(V, lambda: nc.vector.tensor_single_scalar(
                u.t[:, :F], d.t[:, :F], 8.0, OP.is_ge),
                reads=[d], writes=[u])
            yield p.op(V, lambda: nc.vector.scalar_tensor_tensor(
                d.t[:, :F], u.t[:, :F], -8.0, d.t[:, :F], OP.mult, OP.add),
                reads=[u, d], writes=[d])
            yield p.op(V, lambda: nc.vector.tensor_single_scalar(
                c.t[:, :F], d.t[:, :F], 4.0, OP.is_ge),
                reads=[d], writes=[c])
            yield p.op(V, lambda: nc.vector.scalar_tensor_tensor(
                d.t[:, :F], c.t[:, :F], -4.0, d.t[:, :F], OP.mult, OP.add),
                reads=[c, d], writes=[d])
            yield p.op(V, lambda: nc.vector.tensor_single_scalar(
                a.t[:, :F], d.t[:, :F], 0.5, OP.is_ge),
                reads=[d], writes=[a])
            for t in (1.5, 2.5, 3.5):
                yield p.op(V, lambda t=t: nc.vector.scalar_tensor_tensor(
                    a.t[:, :F], d.t[:, :F], t, a.t[:, :F], OP.is_ge, OP.add),
                    reads=[d, a], writes=[a])
            yield p.op(V, lambda: nc.vector.scalar_tensor_tensor(
                a.t[:, :F], c.t[:, :F], 4.0, a.t[:, :F], OP.mult, OP.add),
                reads=[c, a], writes=[a])
            yield p.op(V, lambda: nc.vector.scalar_tensor_tensor(
                a.t[:, :F], u.t[:, :F], 8.0, a.t[:, :F], OP.mult, OP.add),
                reads=[u, a], writes=[a])
            yield p.op(V, lambda: nc.vector.scalar_tensor_tensor(
                a.t[:, :F], b.t[:, :F], 1.0, a.t[:, :F], OP.add, OP.mult),
                reads=[b, a], writes=[a])
            qmag = a
        yield p.op("scalar", lambda: nc.scalar.activation(
            c.t[:, :F], y.t[:, :F], AF.Sign), reads=[y, cpool], writes=[c])
        yield p.op(V, lambda qmag=qmag: nc.vector.tensor_mul(
            d.t[:, :F], qmag.t[:, :F], c.t[:, :F]),
            reads=[qmag, c], writes=[d])
        yield p.op(V, lambda: nc.vector.tensor_tensor(o3, d3, bbc, OP.mult),
             reads=[d, qm["back"]], writes=[out_buf])

    def drain(*gens):
        gens = list(gens)
        while gens:
            keep = []
            for g in gens:
                try:
                    next(g)
                    keep.append(g)
                except StopIteration:
                    pass
            gens = keep

    def drain_skew(gens, skew=2):
        gens = list(gens)
        n = len(gens)
        live = []
        for j, g in enumerate(gens):
            ok = True
            for _ in range((n - 1 - j) * skew):
                try:
                    next(g)
                except StopIteration:
                    ok = False
                    break
            if ok:
                live.append(g)
        drain(*live)

    def emit_maps(src, F, kind, qm):
        drain(g_maps(src, F, kind, qm))

    def emit_quant(scr, src, out_buf, out_ap, F, kind, qm=None):
        drain(g_quant(scr, src, out_buf, out_ap, F, kind, qm=qm))

    def emit_quant_pool(scr, src, out_buf, out_ap, F, qm):
        nb = F // 32
        emit_maps(src, F, 'act', qm)
        ibc = qm["invs"].t[:, :nb].broadcast_to([128, nb, 32])
        bbc = qm["back"].t[:, :nb].broadcast_to([128, nb, 32])
        s3 = src.t[:, :F].rearrange("p (n b) -> p n b", b=32)
        y, u, a, b, c, d, r, xx = (scr[k] for k in "yuabcdrx")
        y3 = y.t[:, :F].rearrange("p (n b) -> p n b", b=32)
        o3 = out_ap.rearrange("p (n b) -> p n b", b=32)
        r3 = r.t[:, :F].rearrange("p (n b) -> p n b", b=32)
        G = "gpsimd"

        def po(fn, reads, writes):
            p.op(G, fn, reads=reads, writes=writes)
        po(lambda: nc.gpsimd.tensor_tensor(y3, s3, ibc, OP.mult),
           [src, qm["invs"]], [y])
        po(lambda: nc.gpsimd.tensor_single_scalar(
            u.t[:, :F], y.t[:, :F], 0.0, OP.is_ge), [y], [u])
        po(lambda: nc.gpsimd.tensor_scalar(
            u.t[:, :F], u.t[:, :F], 2.0, -1.0, OP.mult, OP.add), [u], [u])
        po(lambda: nc.gpsimd.tensor_tensor(
            u.t[:, :F], y.t[:, :F], u.t[:, :F], OP.mult), [y, u], [u])
        po(lambda: nc.gpsimd.tensor_single_scalar(
            u.t[:, :F], u.t[:, :F], 61.9, OP.min), [u], [u])
        po(lambda: nc.gpsimd.tensor_single_scalar(
            a.t[:, :F], u.t[:, :F], 16.0, OP.is_ge), [u], [a])
        po(lambda: nc.gpsimd.tensor_single_scalar(
            b.t[:, :F], u.t[:, :F], 32.0, OP.is_ge), [u], [b])
        po(lambda: nc.gpsimd.tensor_scalar(
            c.t[:, :F], a.t[:, :F], -0.5, 1.0, OP.mult, OP.add), [a], [c])
        po(lambda: nc.gpsimd.tensor_single_scalar(
            d.t[:, :F], b.t[:, :F], -0.25, OP.mult), [b], [d])
        po(lambda: nc.gpsimd.tensor_tensor(
            c.t[:, :F], c.t[:, :F], d.t[:, :F], OP.add), [c, d], [c])
        po(lambda: nc.gpsimd.tensor_tensor(
            d.t[:, :F], c.t[:, :F], u.t[:, :F], OP.mult), [c, u], [d])
        po(lambda: nc.gpsimd.tensor_single_scalar(
            xx.t[:, :F], d.t[:, :F], 8.0, OP.is_ge), [d], [xx])
        po(lambda: nc.gpsimd.tensor_single_scalar(
            c.t[:, :F], xx.t[:, :F], -8.0, OP.mult), [xx], [c])
        po(lambda: nc.gpsimd.tensor_tensor(
            d.t[:, :F], d.t[:, :F], c.t[:, :F], OP.add), [d, c], [d])
        po(lambda: nc.gpsimd.tensor_single_scalar(
            c.t[:, :F], d.t[:, :F], 4.0, OP.is_ge), [d], [c])
        po(lambda: nc.gpsimd.tensor_single_scalar(
            u.t[:, :F], c.t[:, :F], -4.0, OP.mult), [c], [u])
        po(lambda: nc.gpsimd.tensor_tensor(
            d.t[:, :F], d.t[:, :F], u.t[:, :F], OP.add), [d, u], [d])
        po(lambda: nc.gpsimd.tensor_single_scalar(
            r.t[:, :F], d.t[:, :F], 0.5, OP.is_ge), [d], [r])
        for t in (1.5, 2.5, 3.5):
            po(lambda t=t: nc.gpsimd.tensor_single_scalar(
                u.t[:, :F], d.t[:, :F], t, OP.is_ge), [d], [u])
            po(lambda: nc.gpsimd.tensor_tensor(
                r.t[:, :F], r.t[:, :F], u.t[:, :F], OP.add), [r, u], [r])
        po(lambda: nc.gpsimd.tensor_single_scalar(
            u.t[:, :F], c.t[:, :F], 4.0, OP.mult), [c], [u])
        po(lambda: nc.gpsimd.tensor_tensor(
            r.t[:, :F], r.t[:, :F], u.t[:, :F], OP.add), [r, u], [r])
        po(lambda: nc.gpsimd.tensor_single_scalar(
            u.t[:, :F], xx.t[:, :F], 8.0, OP.mult), [xx], [u])
        po(lambda: nc.gpsimd.tensor_tensor(
            r.t[:, :F], r.t[:, :F], u.t[:, :F], OP.add), [r, u], [r])
        po(lambda: nc.gpsimd.tensor_single_scalar(
            u.t[:, :F], b.t[:, :F], 2.0, OP.mult), [b], [u])
        po(lambda: nc.gpsimd.tensor_tensor(
            u.t[:, :F], u.t[:, :F], a.t[:, :F], OP.add), [u, a], [u])
        po(lambda: nc.gpsimd.tensor_single_scalar(
            u.t[:, :F], u.t[:, :F], 1.0, OP.add), [u], [u])
        po(lambda: nc.gpsimd.tensor_tensor(
            r.t[:, :F], r.t[:, :F], u.t[:, :F], OP.mult), [r, u], [r])
        po(lambda: nc.gpsimd.tensor_single_scalar(
            u.t[:, :F], y.t[:, :F], 0.0, OP.is_ge), [y], [u])
        po(lambda: nc.gpsimd.tensor_scalar(
            u.t[:, :F], u.t[:, :F], 2.0, -1.0, OP.mult, OP.add), [u], [u])
        po(lambda: nc.gpsimd.tensor_tensor(
            r.t[:, :F], r.t[:, :F], u.t[:, :F], OP.mult), [r, u], [r])
        po(lambda: nc.gpsimd.tensor_tensor(o3, r3, bbc, OP.mult),
           [r, qm["back"]], [out_buf])

    def g_layernorm(src, scratch, out_buf, out_ap, ls):
        yield p.op("vector", lambda: nc.vector.reduce_sum(
            ls.t[:, 0:1], src.t[:, :H], AX.X), reads=[src], writes=[ls])
        yield p.op("scalar", lambda: nc.scalar.activation(
            scratch.t[:, :H], src.t[:, :H], AF.Square,
            accum_out=ls.t[:, 1:2]),
            reads=[src, cpool], writes=[scratch, ls])
        yield p.op("vector", lambda: nc.vector.tensor_single_scalar(
            ls.t[:, 0:1], ls.t[:, 0:1], 1.0 / H, OP.mult),
            reads=[ls], writes=[ls])
        yield p.op("vector", lambda: nc.vector.tensor_mul(
            ls.t[:, 2:3], ls.t[:, 0:1], ls.t[:, 0:1]),
            reads=[ls], writes=[ls])
        yield p.op("vector", lambda: nc.vector.scalar_tensor_tensor(
            ls.t[:, 1:2], ls.t[:, 1:2], 1.0 / H, ls.t[:, 2:3],
            OP.mult, OP.subtract), reads=[ls], writes=[ls])
        yield p.op("scalar", lambda: nc.scalar.activation(
            ls.t[:, 3:4], ls.t[:, 1:2], AF.Ln, bias=cap(eps_ln)),
            reads=[ls, cpool], writes=[ls])
        yield p.op("scalar", lambda: nc.scalar.activation(
            ls.t[:, 3:4], ls.t[:, 3:4], AF.Exp, scale=cap(-0.5)),
            reads=[ls, cpool], writes=[ls])
        yield p.op("vector", lambda: nc.vector.scalar_tensor_tensor(
            ls.t[:, 4:5], ls.t[:, 0:1], -1.0, ls.t[:, 3:4],
            OP.mult, OP.mult), reads=[ls], writes=[ls])
        yield p.op("scalar", lambda out_ap=out_ap: nc.scalar.activation(
            out_ap, src.t[:, :H], AF.Identity,
            scale=ls.t[:, 3:4], bias=ls.t[:, 4:5]),
            reads=[src, ls, cpool], writes=[out_buf])

    def emit_layernorm(src, scratch, out_buf, out_ap):
        drain(g_layernorm(src, scratch, out_buf, out_ap, lnstat))

    for v in [-LN2, LN2, eps_ln, eps_rms, -0.5, -1.0, 1.0,
              (2 + 64) * LN2, -(3 + 64) * LN2,
              (5 + 64) * LN2, -(5 + 64) * LN2, 1.0 / D]:
        cap(v)

    def f_consts():
        last = None
        for v, col in sorted(CVALS.items(), key=lambda t: t[1]):
            last = nc.gpsimd.memset(cpool.t[:, col:col + 1], float(v))
        return last
    p.op("gpsimd", f_consts, writes=[cpool])

    p.op("gpsimd", lambda: nc.gpsimd.memset(ident.t[:, :], 1.0),
         writes=[ident])
    p.op("gpsimd", lambda: nc.gpsimd.affine_select(
        out=ident.t[:, :], in_=ident.t[:, :], compare_op=OP.is_equal,
        fill=0.0, base=0, pattern=[[-1, 128]], channel_multiplier=1),
        reads=[ident], writes=[ident])

    gq_src = bass.AP(gq_d.tensor, 0, [[0, 128], [1, D]])
    gk_src = bass.AP(gk_d.tensor, 0, [[0, 128], [1, D]])
    p.op("gpsimd", lambda: nc.gpsimd.dma_start(out=gq_b.t[:, :], in_=gq_src),
         writes=[gq_b], dma=True)
    p.op("gpsimd", lambda: nc.gpsimd.dma_start(out=gk_b.t[:, :], in_=gk_src),
         writes=[gk_b], dma=True)
    p.op("vector", lambda: nc.vector.tensor_scalar_mul(
        gq_b.t[:, :], gq_b.t[:, :], float(D) ** -0.5),
        reads=[gq_b], writes=[gq_b])
    for q in range(NQ):
        src = bass.AP(qpos_d.tensor, q * 128, [[1, 128], [128 * NQ, 1]])
        p.op("gpsimd", lambda q=q, src=src: nc.gpsimd.dma_start(
            out=qpos_b.t[:, q:q + 1], in_=src), writes=[qpos_b], dma=True)

    with contextlib.ExitStack() as wst:
        def wsb(name, shape, dt):
            t = wst.enter_context(nc.sbuf_tensor(name, shape, dt))
            return Buf(t, name)
        WW = min(1024, H)
        NWC = H // WW
        NWS = 4
        scrW4 = [{k: wsb(f"scrW{i}_" + k, [128, WW], F32)
                  for k in "yuabcd"} for i in range(NWS)]
        win = [wsb(f"win{i}", [128, WW], F32) for i in range(NWS)]
        wout = [wsb(f"wout{i}", [128, WW], BF16) for i in range(NWS)]
        qmW4 = [qm0] + [
            {n: wsb(f"qmW{i}_" + n, [128, max(32, WW // 32)], F32) for n in
             ("amax", "z", "rem", "fl", "invs", "back")}
            for i in (1, 2, 3)]

        def w_chain(ci):
            rr, hh = divmod(ci, NWC)
            j = ci % NWS
            bi, bo = win[j], wout[j]
            p.op("sync", lambda rr=rr, hh=hh, bi=bi:
                 nc.sync.dma_start(
                     out=bi.t[:, :WW],
                     in_=wsh_d[rr * 128:(rr + 1) * 128,
                               hh * WW:(hh + 1) * WW]),
                 writes=[bi], dma=True)
            yield
            yield from g_quant(scrW4[j], bi, bo, bo.t[:, :WW], WW,
                               'w', qm=qmW4[j])
            p.op("sync", lambda rr=rr, hh=hh, bo=bo:
                 nc.sync.dma_start(
                     out=qsh_d[rr * 128:(rr + 1) * 128,
                               hh * WW:(hh + 1) * WW],
                     in_=bo.t[:, :WW]),
                 reads=[bo], writes=[qsh_b], dma=True)
        NWT = TPC * NWC
        for c0 in range(0, NWT, NWS):
            drain_skew([w_chain(c) for c in range(c0, min(c0 + NWS, NWT))],
                       skew=3)
        p.fence()
        p.op("gpsimd", lambda: nc.gpsimd.collective_compute(
            "AllGather", OP.bypass, [list(range(n_cores))],
            [qsh_d.opt()], [qfull_d.opt()]),
            reads=[qsh_b], writes=[qfull_b], cc=True)

    with contextlib.ExitStack() as ast:
        def asb(name, shape, dt):
            t = ast.enter_context(nc.sbuf_tensor(name, shape, dt))
            return Buf(t, name)
        sqd = asb("sqd", [128, 512], F32)
        sqd1 = asb("sqd1", [128, 512], F32)
        attn_all = asb("attn_all", [128, NQ * NH * 128], BF16)
        ssq2 = [asb(f"ssq{i}", [128, 8], F32) for i in (0, 1)]
        rstd2 = [asb(f"rstd{i}", [128, 8], F32) for i in (0, 1)]
        knorm2 = [asb(f"knorm{i}", [128, 512], BF16) for i in (0, 1)]
        sqd2 = [sqd, sqd1]
        kvq_st = ast.enter_context(contextlib.ExitStack())

        def kvqsb(name, shape, dt):
            return Buf(kvq_st.enter_context(nc.sbuf_tensor(name, shape, dt)),
                       name)
        kfm = kvqsb("kfm", [128, NKV * S], BF16)
        vtm = kvqsb("vtm", [128, (S // 128) * KVW], BF16)
        qfm = kvqsb("qfm", [128, NH * TQ], BF16)

        def g_rms(psrc, nheads, gtile, out_buf, out_ap, sl2=0):
            ssq = ssq2[sl2 % 2]
            rstd = rstd2[sl2 % 2]
            sqd_ = sqd2[sl2 % 2]
            W = nheads * 128
            yield p.op("scalar", lambda W=W, sqd_=sqd_: nc.scalar.activation(
                sqd_.t[:, :W], psrc.t[:, :W], AF.Square),
                reads=[psrc, cpool], writes=[sqd_])
            yield p.op("vector", lambda nheads=nheads, W=W, sqd_=sqd_, ssq=ssq:
                 nc.vector.tensor_reduce(
                     ssq.t[:, :nheads],
                     sqd_.t[:, :W].rearrange("p (h d) -> p h d", d=128),
                     AX.X, OP.add),
                 reads=[sqd_], writes=[ssq])
            yield p.op("scalar", lambda nheads=nheads, ssq=ssq, rstd=rstd:
                 nc.scalar.activation(
                     rstd.t[:, :nheads], ssq.t[:, :nheads], AF.Ln,
                     scale=cap(1.0 / D), bias=cap(eps_rms)),
                 reads=[ssq, cpool], writes=[rstd])
            yield p.op("scalar", lambda nheads=nheads, rstd=rstd:
                 nc.scalar.activation(
                     rstd.t[:, :nheads], rstd.t[:, :nheads], AF.Exp,
                     scale=cap(-0.5)), reads=[rstd, cpool], writes=[rstd])
            yield p.op("vector", lambda nheads=nheads, W=W, rstd=rstd, sqd_=sqd_:
                 nc.vector.tensor_tensor(
                     sqd_.t[:, :W].rearrange("p (h d) -> p h d", d=128),
                     psrc.t[:, :W].rearrange("p (h d) -> p h d", d=128),
                     rstd.t[:, :nheads].broadcast_to([128, nheads, 128]),
                     OP.mult),
                 reads=[psrc, rstd], writes=[sqd_])
            yield p.op("vector", lambda nheads=nheads, W=W, out_ap=out_ap,
                 sqd_=sqd_: nc.vector.tensor_tensor(
                     out_ap[:, :W].rearrange("p (h d) -> p h d", d=128),
                     sqd_.t[:, :W].rearrange("p (h d) -> p h d", d=128),
                     bass.AP(gtile.t, 0, [[int(gtile.t.shape[1]), 128],
                                          [0, nheads], [1, 128]]),
                     OP.mult),
                 reads=[sqd_, gtile], writes=[out_buf])

        def rms_head_cols(psrc, nheads, gtile, out_buf, out_ap, sl2=0):
            drain(g_rms(psrc, nheads, gtile, out_buf, out_ap, sl2))

        with contextlib.ExitStack() as qst:
            def qsb(name, shape, dt):
                return Buf(qst.enter_context(nc.sbuf_tensor(name, shape, dt)),
                           name)
            xqfm = qsb("xqfm", [128, HC * TQ], BF16)
            wqs = qsb("wqs", [128, HC * 512], BF16)
            xtm = [qsb("xtmq0", [128, H], BF16)] * 2
            xst = [qsb(f"xstq{i}", [128, H], F32) for i in (0, 1)]
            wst = xst
            for tc in range(NTB):
                bx = xtm[tc % 2]
                bs = xst[tc % 2]
                p.op("sync", lambda tc=tc, bs=bs: nc.sync.dma_start(
                    out=bs.t[:, :], in_=xq_d[tc * 128:(tc + 1) * 128, :]),
                    writes=[bs], dma=True)
                p.op("vector", lambda bx=bx, bs=bs: nc.vector.tensor_copy(
                    bx.t[:, :], bs.t[:, :]), reads=[bs], writes=[bx])
                for hc0 in range(0, HC, 4):
                    n = min(4, HC - hc0)
                    transpose_batch(xqfm, xqfm.t, hc0 * TQ + tc * 128, TQ,
                                    bx, bx.t, hc0 * 128, n, hc0 // 4)
            HHC = max(1, HC // 4)
            for oc in range(OCQ):
                for g2 in range(HC // HHC):
                    ws = wst[g2 % 2]
                    src = wq_d[g2 * HHC * 128:(g2 + 1) * HHC * 128,
                               oc * 512:(oc + 1) * 512]
                    src = src.rearrange("(h p) c -> p h c", p=128)
                    p.op("sync", lambda src=src, ws=ws, HHC=HHC:
                         nc.sync.dma_start(
                             out=ws.t[:, :HHC * 512].rearrange(
                                 "p (h c) -> p h c", c=512),
                             in_=src),
                         writes=[ws], dma=True)
                    p.op("vector", lambda g2=g2, ws=ws, HHC=HHC:
                         nc.vector.tensor_copy(
                             wqs.t[:, g2 * HHC * 512:(g2 + 1) * HHC * 512],
                             ws.t[:, :HHC * 512]),
                         reads=[ws], writes=[wqs])
                def q_chain(oc, tc):
                    pp = psA[tc % 2]
                    def f_q(tc=tc, pp=pp):
                        last = None
                        for hc in range(HC):
                            last = nc.tensor.matmul(
                                pp.t[:, :512],
                                xqfm.t[:, hc * TQ + tc * 128:
                                       hc * TQ + (tc + 1) * 128],
                                wqs.t[:, hc * 512:(hc + 1) * 512],
                                start=(hc == 0), stop=(hc == HC - 1))
                        return last
                    p.op("tensor", f_q, reads=[xqfm, wqs], writes=[pp])
                    yield
                    knorm = knorm2[tc % 2]
                    yield from g_rms(pp, 512 // 128, gq_b, knorm, knorm.t,
                                     tc)
                    transpose_batch(qfm, qfm.t, (oc * 4) * TQ + tc * 128,
                                    TQ, knorm, knorm.t, 0, 512 // 128, tc)
                for tc0 in range(0, NTB, 2):
                    gens = [q_chain(oc, tc0)]
                    if tc0 + 1 < NTB:
                        gens.append(q_chain(oc, tc0 + 1))
                    drain(*gens)
            p.fence()

        with contextlib.ExitStack() as kst:
            def ksb(name, shape, dt):
                return Buf(kst.enter_context(nc.sbuf_tensor(name, shape, dt)),
                           name)
            wkv = ksb("wkv", [128, HC * KVW * 2], BF16)
            xtm = [ksb(f"xtmk{i}", [128, H], BF16) for i in (0, 1)]
            xfmc = [ksb(f"xfmc{i}", [128, HC * 128], BF16) for i in (0, 1)]
            xst = [ksb(f"xstk{i}", [128, H], F32) for i in (0, 1)]
            nkvg = max(1, (H // KVW) // 2)
            for part, wsrc in ((0, wk_d), (1, wv_d)):
                for h0 in range(0, HC, nkvg):
                    n = min(nkvg, HC - h0)
                    ws = xst[(h0 // nkvg + part) % 2]
                    src = wsrc[h0 * 128:(h0 + n) * 128, :]
                    src = src.rearrange("(h p) c -> p h c", p=128)
                    off = part * HC * KVW + h0 * KVW
                    p.op("sync", lambda src=src, ws=ws, n=n:
                         nc.sync.dma_start(
                             out=ws.t[:, :n * KVW].rearrange(
                                 "p (h c) -> p h c", c=KVW),
                             in_=src),
                         writes=[ws], dma=True)
                    p.op("vector", lambda off=off, ws=ws, n=n:
                         nc.vector.tensor_copy(
                             wkv.t[:, off:off + n * KVW], ws.t[:, :n * KVW]),
                         reads=[ws], writes=[wkv])
            def kv_chain(tc):
                bx = xtm[tc % 2]
                fx = xfmc[tc % 2]
                bs = xst[tc % 2]
                p.op("sync", lambda tc=tc, bs=bs: nc.sync.dma_start(
                    out=bs.t[:, :], in_=x_d[tc * 128:(tc + 1) * 128, :]),
                    writes=[bs], dma=True)
                p.op("vector", lambda bx=bx, bs=bs: nc.vector.tensor_copy(
                    bx.t[:, :], bs.t[:, :]), reads=[bs], writes=[bx])
                yield
                for hc0 in range(0, HC, 4):
                    n = min(4, HC - hc0)
                    transpose_batch(fx, fx.t, hc0 * 128, 128,
                                    bx, bx.t, hc0 * 128, n, hc0 // 4 + tc)
                    yield
                pp = psA[tc % 2]
                def f_k(fx=fx, pp=pp):
                    last = None
                    for hc in range(HC):
                        last = nc.tensor.matmul(
                            pp.t[:, :KVW],
                            fx.t[:, hc * 128:(hc + 1) * 128],
                            wkv.t[:, hc * KVW:(hc + 1) * KVW],
                            start=(hc == 0), stop=(hc == HC - 1))
                    return last
                p.op("tensor", f_k, reads=[fx, wkv], writes=[pp])
                yield
                knorm = knorm2[tc % 2]
                yield from g_rms(pp, NKV, gk_b, knorm, knorm.t, tc)
                for h0 in range(0, NKV, 4):
                    n = min(4, NKV - h0)
                    transpose_batch(kfm, kfm.t, h0 * S + tc * 128, S,
                                    knorm, knorm.t, h0 * 128, n,
                                    h0 // 4 + tc)
                    yield
                pv = psB[tc % 2]
                def f_v(fx=fx, pv=pv):
                    last = None
                    for hc in range(HC):
                        last = nc.tensor.matmul(
                            pv.t[:, :KVW],
                            fx.t[:, hc * 128:(hc + 1) * 128],
                            wkv.t[:, HC * KVW + hc * KVW:
                                  HC * KVW + (hc + 1) * KVW],
                            start=(hc == 0), stop=(hc == HC - 1))
                    return last
                p.op("tensor", f_v, reads=[fx, wkv], writes=[pv])
                yield
                p.op("vector", lambda tc=tc, pv=pv: nc.vector.tensor_copy(
                    vtm.t[:, tc * KVW:(tc + 1) * KVW], pv.t[:, :KVW]),
                    reads=[pv], writes=[vtm])
            for tc0 in range(0, S // 128, 2):
                gens = [kv_chain(tc0)]
                if tc0 + 1 < S // 128:
                    gens.append(kv_chain(tc0 + 1))
                drain(*gens)
            p.fence()

        def ssb(name, shape, dt):
            return Buf(kvq_st.enter_context(nc.sbuf_tensor(name, shape, dt)),
                       name)
        kpos_b = ssb("kpos_t", [128, S], F32)
        p.op("gpsimd", lambda: nc.gpsimd.iota(
            kpos_b.t[:, :], [[1, S]], base=0, channel_multiplier=0,
            allow_small_or_imprecise_dtypes=True), writes=[kpos_b])
        sc_f2 = [ssb(f"sc_f{i}", [128, S], F32) for i in (0, 1)]
        af32_2 = [ssb(f"af32{i}", [128, S], F32) for i in (0, 1)]
        stat2 = [ssb(f"stat2_{i}", [128, 4], F32) for i in (0, 1)]
        msk = ssb("msk", [128, S], F32)

        with contextlib.ExitStack() as tst:
            def tsb(name, shape, dt):
                return Buf(tst.enter_context(nc.sbuf_tensor(name, shape, dt)),
                           name)
            abf2 = [tsb(f"abf{i}", [128, S], BF16) for i in (0, 1)]
            atbJ = tsb("atbJ", [128, 2 * S], BF16)
            for qi in range(NQ):
                kl = KLENS[qi]
                nkc = kl // 128
                p.op("vector", lambda qi=qi, kl=kl: nc.vector.tensor_scalar(
                    msk.t[:, :kl], kpos_b.t[:, :kl], qpos_b.t[:, qi:qi + 1],
                    -1e30, OP.is_gt, OP.mult),
                    reads=[kpos_b, qpos_b], writes=[msk])
                def h_chain(qi, h, kl, nkc):
                    kv = h // GR
                    sl = h % 2
                    sc_f = sc_f2[sl]
                    af32 = af32_2[sl]
                    stat = stat2[sl]
                    abf = abf2[sl]
                    psS = psA if sl == 0 else psC
                    for kc0 in range(0, kl, 512):
                        w = min(512, kl - kc0)
                        pp = psS[(kc0 // 512) % 2]
                        p.op("tensor",
                             lambda qi=qi, h=h, kv=kv, kc0=kc0, w=w, pp=pp:
                             nc.tensor.matmul(
                                 pp.t[:, :w],
                                 qfm.t[:, h * TQ + qi * 128:
                                       h * TQ + (qi + 1) * 128],
                                 kfm.t[:, kv * S + kc0:kv * S + kc0 + w],
                                 start=True, stop=True),
                             reads=[qfm, kfm], writes=[pp])
                        p.op("vector", lambda kc0=kc0, w=w, pp=pp, sc_f=sc_f:
                             nc.vector.tensor_add(
                                 sc_f.t[:, kc0:kc0 + w], pp.t[:, :w],
                                 msk.t[:, kc0:kc0 + w]),
                             reads=[pp, msk], writes=[sc_f])
                        yield
                    p.op("vector", lambda kl=kl, sc_f=sc_f, stat=stat:
                         nc.vector.reduce_max(
                             stat.t[:, 0:1], sc_f.t[:, :kl], AX.X,
                             negate=True),
                         reads=[sc_f], writes=[stat])
                    yield
                    p.op("scalar", lambda kl=kl, sc_f=sc_f, af32=af32,
                         stat=stat: nc.scalar.activation(
                             af32.t[:, :kl], sc_f.t[:, :kl], AF.Exp,
                             bias=stat.t[:, 0:1], accum_out=stat.t[:, 1:2]),
                         reads=[sc_f, stat, cpool], writes=[af32, stat])
                    yield
                    p.op("vector", lambda stat=stat: nc.vector.reciprocal(
                        stat.t[:, 2:3], stat.t[:, 1:2]),
                        reads=[stat], writes=[stat])
                    yield
                    p.op("scalar", lambda kl=kl, af32=af32, stat=stat,
                         abf=abf: nc.scalar.activation(
                             abf.t[:, :kl], af32.t[:, :kl], AF.Identity,
                             scale=stat.t[:, 2:3]),
                         reads=[af32, stat, cpool], writes=[abf])
                    yield
                    for c0 in range(0, nkc, 4):
                        n = min(4, nkc - c0)
                        transpose_batch(atbJ, atbJ.t,
                                        (2 * c0 + sl) * 128, 256,
                                        abf, abf.t, c0 * 128, n,
                                        c0 // 4 + sl)
                        yield
                for h0 in range(0, NH, 2):
                    drain(h_chain(qi, h0, kl, nkc),
                          h_chain(qi, h0 + 1, kl, nkc))
                    kv = h0 // GR
                    pa = psB[(h0 // 2) % 2]
                    def f_av(kv=kv, nkc=nkc, pa=pa):
                        last = None
                        for c in range(nkc):
                            last = nc.tensor.matmul(
                                pa.t[:, :256],
                                vtm.t[:, c * KVW + kv * 128:
                                      c * KVW + (kv + 1) * 128],
                                atbJ.t[:, 2 * c * 128:(2 * c + 2) * 128],
                                start=(c == 0), stop=(c == nkc - 1))
                        return last
                    p.op("tensor", f_av, reads=[vtm, atbJ], writes=[pa])
                    dsl = attn_all.t[:, (qi * NH + h0) * 128:
                                     (qi * NH + h0 + 2) * 128]
                    p.op("vector", lambda dsl=dsl, pa=pa:
                         nc.vector.tensor_copy(dsl, pa.t[:, :256]),
                         reads=[pa], writes=[attn_all])
            p.fence()

        kvq_st.close()

        with contextlib.ExitStack() as ost:
            def osb(name, shape, dt):
                return Buf(ost.enter_context(nc.sbuf_tensor(name, shape, dt)),
                           name)
            scw2 = [osb(f"sc_f2{i}", [128, H], F32) for i in (0, 1)]
            afw2 = [osb(f"af32_2{i}", [128, H], F32) for i in (0, 1)]
            xres2 = [osb(f"xres{i}", [128, H], F32) for i in (0, 1)]
            wos = osb("wos", [128, (NH * D // 128) * H], BF16)
            items = []
            for h0 in range(0, NH * D // 128, 2):
                n = min(2, NH * D // 128 - h0)
                src = wo_d[h0 * 128:(h0 + n) * 128, :]
                src = src.rearrange("(h p) c -> p h c", p=128)
                items.append((lambda src=src, h0=h0, n=n:
                              nc.gpsimd.dma_start(
                                  out=wos.t[:, h0 * H:(h0 + n) * H]
                                  .rearrange("p (h c) -> p h c", c=H),
                                  in_=src), [wos]))
            p.gpdma_group(items)
            def wo_chain(qi):
                scw = scw2[qi % 2]
                afw = afw2[qi % 2]
                xres = xres2[qi % 2]
                psS = psA if qi % 2 == 0 else psC
                p.op("sync", lambda qi=qi, xres=xres: nc.sync.dma_start(
                    out=xres.t[:, :H], in_=xq_d[qi * 128:(qi + 1) * 128, :]),
                    writes=[xres], dma=True)
                yield
                for oc in range(H // 512):
                    pp = psS[oc % 2]
                    def f_wo(qi=qi, oc=oc, pp=pp):
                        last = None
                        for h in range(NH):
                            last = nc.tensor.matmul(
                                pp.t[:, :512],
                                attn_all.t[:, (qi * NH + h) * 128:
                                           (qi * NH + h + 1) * 128],
                                wos.t[:, h * H + oc * 512:
                                      h * H + (oc + 1) * 512],
                                start=(h == 0), stop=(h == NH - 1))
                        return last
                    p.op("tensor", f_wo, reads=[attn_all, wos], writes=[pp])
                    yield
                    p.op("vector", lambda oc=oc, pp=pp, scw=scw, xres=xres:
                         nc.vector.tensor_add(
                             scw.t[:, oc * 512:(oc + 1) * 512], pp.t[:, :512],
                             xres.t[:, oc * 512:(oc + 1) * 512]),
                         reads=[pp, xres], writes=[scw])
                    yield
                yield from g_layernorm(scw, afw, x1,
                                       x1.t[:, qi * H:(qi + 1) * H],
                                       lnstat2[qi % 2])
            for qi0 in range(0, NQ, 2):
                gens = [wo_chain(qi0)]
                if qi0 + 1 < NQ:
                    gens.append(wo_chain(qi0 + 1))
                drain(*gens)
            p.fence()

    st_toks = []
    with contextlib.ExitStack() as fst:
        def fsb(name, shape, dt):
            return Buf(fst.enter_context(nc.sbuf_tensor(name, shape, dt)),
                       name)
        xqfm2 = fsb("xqfm2", [128, HC * TQ], BF16)
        interq = fsb("interq", [128, (IQ // 128) * TQ], BF16)
        x2acc = fsb("x2acc", [128, NTB * H], BF16)
        with contextlib.ExitStack() as f1st:
            def f1sb(name, shape, dt):
                return Buf(f1st.enter_context(
                    nc.sbuf_tensor(name, shape, dt)), name)
            NSET = min(4, NTB)
            JW = NSET * 512
            JH = JW // 2
            scrJ2 = [{k: f1sb(f"scrJ{i}_" + k, [128, JH], F32)
                      for k in "yuabcd"} for i in (0, 1)]
            HCH = HC // 2
            wb2 = [f1sb(f"wb{i}", [128, HCH * 512], BF16) for i in (0, 1)]
            ICH = max(1, (IQ // 128) // 2)
            wd2 = [f1sb(f"wdh{i}", [128, ICH * 512], BF16) for i in (0, 1)]
            s_sb = f1sb("s_sb", [128, NTB * 512], BF16)
            vjoint = f1sb("vjoint", [128, JW], F32)
            qmFB = {n: f1sb("qmFB_" + n, [128, NSET * 16], F32)
                    for n in ("amax", "z", "rem", "fl", "invs", "back")}
            qmF2 = [qm0, qmFB]
            e_sb2 = [f1sb(f"e_sb{i}", [128, 512], F32) for i in (0, 1)]
            l_sb2 = [f1sb(f"l_sb{i}", [128, 512], F32) for i in (0, 1)]
            xqa2j = [f1sb(f"xqaj{i}", [128, JH], BF16) for i in (0, 1)]
            ps4 = [psA[0], psA[1], psC[0], psC[1]]


            xw = min(H, JW)

            def x1_half(tb, sub, qmf, j, hw):
                xqa = xqa2j[j]
                yield from g_quant(scrJ2[j], vjoint, xqa,
                                   xqa.t[:, :hw], hw, 'act', qm=qmf,
                                   so=j * hw, nbo=j * (hw // 32),
                                   maps=False)
                hc0 = sub * (xw // 128) + j * (hw // 128)
                for cb in range(0, hw // 128, 4):
                    n = min(4, hw // 128 - cb)
                    transpose_batch(
                        xqfm2, xqfm2.t, (hc0 + cb) * TQ + tb * 128, TQ,
                        xqa, xqa.t, cb * 128, n, cb // 4 + j)
                    yield

            ci = 0
            for tb in range(NTB):
                for sub in range(H // xw):
                    qmf = qmF2[ci % 2]
                    p.op("vector", lambda tb=tb, sub=sub:
                         nc.vector.tensor_copy(
                             vjoint.t[:, :xw],
                             x1.t[:, tb * H + sub * xw:
                                  tb * H + (sub + 1) * xw]),
                         reads=[x1], writes=[vjoint])
                    drain(g_maps(vjoint, xw, 'act', qmf))
                    hw = xw // 2 if xw >= 1024 else xw
                    if hw < xw:
                        drain_skew([x1_half(tb, sub, qmf, 0, hw),
                                    x1_half(tb, sub, qmf, 1, hw)], skew=3)
                    else:
                        drain(x1_half(tb, sub, qmf, 0, hw))
                    ci += 1

            n5 = IQ // 512

            def ld_w(row0, half):
                eng = "sync"
                for hci in range(HCH):
                    hc = half * HCH + hci
                    p.op(eng, lambda row0=row0, hc=hc, hci=hci, eng=eng,
                         wb=wb2[half]:
                         getattr(nc, eng).dma_start_transpose(
                             out=wb.t[:, hci * 512:(hci + 1) * 512],
                             in_=qfull_d[row0:row0 + 512,
                                         hc * 128:(hc + 1) * 128]),
                         reads=[qfull_b], writes=[wb2[half]], dma=True)

            def mm_half(tb, pp, half, start, stop):
                def f(tb=tb, pp=pp, half=half, start=start, stop=stop):
                    last = None
                    for hci in range(HCH):
                        hc = half * HCH + hci
                        last = nc.tensor.matmul(
                            pp.t[:, :512],
                            xqfm2.t[:, hc * TQ + tb * 128:
                                    hc * TQ + (tb + 1) * 128],
                            wb2[half].t[:, hci * 512:(hci + 1) * 512],
                            start=(start and hci == 0),
                            stop=(stop and hci == HCH - 1))
                    return last
                p.op("tensor", f, reads=[xqfm2, wb2[half]], writes=[pp])

            assert n5 == 1
            NI = NIQ

            def gate_phase(i512):
                def gate_chain(tb):
                    pp = ps4[tb % 4]
                    mm_half(tb, pp, 0, True, False)
                    yield
                    mm_half(tb, pp, 1, False, True)
                    yield
                    yield
                    e_sb = e_sb2[tb % 2]
                    l_sb = l_sb2[tb % 2]
                    p.op("scalar", lambda pp=pp, e_sb=e_sb:
                         nc.scalar.activation(
                             e_sb.t[:, :512], pp.t[:, :512], AF.Exp,
                             scale=cap(-1.0)),
                         reads=[pp, cpool], writes=[e_sb])
                    yield
                    p.op("scalar", lambda e_sb=e_sb, l_sb=l_sb:
                         nc.scalar.activation(
                             l_sb.t[:, :512], e_sb.t[:, :512], AF.Ln,
                             bias=cap(1.0)),
                         reads=[e_sb, cpool], writes=[l_sb])
                    yield
                    p.op("scalar", lambda e_sb=e_sb, l_sb=l_sb:
                         nc.scalar.activation(
                             e_sb.t[:, :512], l_sb.t[:, :512], AF.Exp,
                             scale=cap(-1.0)),
                         reads=[l_sb, cpool], writes=[e_sb])
                    yield
                    p.op("vector", lambda tb=tb, pp=pp, e_sb=e_sb:
                         nc.vector.tensor_mul(
                             s_sb.t[:, tb * 512:(tb + 1) * 512],
                             e_sb.t[:, :512], pp.t[:, :512]),
                         reads=[e_sb, pp], writes=[s_sb])
                for w0 in range(0, NTB, 4):
                    wg = [gate_chain(tb)
                          for tb in range(w0, min(w0 + 4, NTB))]
                    for c in wg:
                        next(c)
                    if w0 + 4 >= NTB:
                        ld_w(WU0 + i512 * 512, 0)
                    for c in wg:
                        next(c)
                    if w0 + 4 >= NTB:
                        ld_w(WU0 + i512 * 512, 1)
                    for j0 in range(0, len(wg), 2):
                        drain(*wg[j0:j0 + 2])

            def ld_wd(iq, oc, half):
                for ici in range(ICH):
                    ic = half * ICH + ici
                    col = iq * IQ + ic * 128
                    cc, col_in = col // H, col % H
                    row0 = WD0 + cc * H + oc * 512
                    p.op("sync", lambda row0=row0, col_in=col_in,
                         ici=ici, wdb=wd2[half]:
                         nc.sync.dma_start_transpose(
                             out=wdb.t[:, ici * 512:(ici + 1) * 512],
                             in_=qfull_d[row0:row0 + 512,
                                         col_in:col_in + 128]),
                         reads=[qfull_b], writes=[wd2[half]], dma=True)

            def up_phase(i512):
                def up_chain(tb):
                    j = tb % 4
                    pp = ps4[j]
                    mm_half(tb, pp, 0, True, False)
                    yield
                    mm_half(tb, pp, 1, False, True)
                    yield
                    yield
                    p.op("vector", lambda tb=tb, pp=pp, j=j:
                         nc.vector.tensor_mul(
                             vjoint.t[:, j * 512:(j + 1) * 512],
                             s_sb.t[:, tb * 512:(tb + 1) * 512],
                             pp.t[:, :512]),
                         reads=[s_sb, pp], writes=[vjoint])

                def half_q(wave, qmf, j, hw):
                    xqa = xqa2j[j]
                    yield from g_quant(scrJ2[j], vjoint, xqa,
                                       xqa.t[:, :hw], hw, 'act',
                                       qm=qmf, so=j * hw,
                                       nbo=j * (hw // 32), maps=False)
                    nt = hw // 512
                    for jj in range(nt):
                        tb = wave[j * nt + jj]
                        transpose_batch(interq, interq.t,
                                        tb * 128, TQ,
                                        xqa, xqa.t, jj * 512, 4,
                                        jj + j)
                        yield

                ni = i512 + 1
                for w0 in range(0, NTB, 4):
                    wave = range(w0, min(w0 + 4, NTB))
                    wu_ = [up_chain(tb) for tb in wave]
                    for c in wu_:
                        next(c)
                    if w0 + 4 >= NTB and ni < NI:
                        ld_w(ni * 512, 0)
                    for c in wu_:
                        next(c)
                    if w0 + 4 >= NTB and ni < NI:
                        ld_w(ni * 512, 1)
                    drain(*wu_)
                    qmf = qmF2[(w0 // 4) % 2]
                    wn = len(wave)
                    drain(g_maps(vjoint, wn * 512, 'act', qmf))
                    if wn >= 2:
                        hw = wn * 512 // 2
                        drain_skew([half_q(wave, qmf, 0, hw),
                                    half_q(wave, qmf, 1, hw)],
                                   skew=3)
                    else:
                        drain(half_q(wave, qmf, 0, wn * 512))
                ld_wd(i512, 0, 0)
                ld_wd(i512, 0, 1)

            def down_phase(iq):
                def down_chain(oc, tb):
                    pp = ps4[tb % 4]
                    for half in range(2):
                        def f(tb=tb, pp=pp, half=half):
                            last = None
                            for ici in range(ICH):
                                ic = half * ICH + ici
                                last = nc.tensor.matmul(
                                    pp.t[:, :512],
                                    interq.t[:, ic * TQ + tb * 128:
                                             ic * TQ + (tb + 1) * 128],
                                    wd2[half].t[:, ici * 512:
                                                (ici + 1) * 512],
                                    start=(half == 0 and ici == 0),
                                    stop=(half == 1 and ici == ICH - 1))
                            return last
                        p.op("tensor", f, reads=[interq, wd2[half]],
                             writes=[pp])
                        yield
                    dsl = x2acc.t[:, tb * H + oc * 512:
                                  tb * H + (oc + 1) * 512]
                    if iq == 0:
                        p.op("vector", lambda dsl=dsl, pp=pp:
                             nc.vector.tensor_copy(dsl, pp.t[:, :512]),
                             reads=[pp], writes=[x2acc])
                    else:
                        p.op("vector", lambda dsl=dsl, pp=pp:
                             nc.vector.tensor_add(dsl, dsl,
                                                  pp.t[:, :512]),
                             reads=[pp, x2acc], writes=[x2acc])
                for oc in range(H // 512):
                    for w0 in range(0, NTB, 4):
                        wd_ = [down_chain(oc, tb)
                               for tb in range(w0, min(w0 + 4, NTB))]
                        for c in wd_:
                            next(c)
                        if w0 + 4 >= NTB and oc + 1 < H // 512:
                            ld_wd(iq, oc + 1, 0)
                        for c in wd_:
                            next(c)
                        if w0 + 4 >= NTB and oc + 1 < H // 512:
                            ld_wd(iq, oc + 1, 1)
                        drain(*wd_)

            ld_w(0, 0)
            ld_w(0, 1)
            gate_phase(0)
            for i in range(NI):
                up_phase(i)
                if i + 1 < NI:
                    gate_phase(i + 1)
                down_phase(i)
            p.fence()

        with contextlib.ExitStack() as f2st:
            def f2sb(name, shape, dt):
                return Buf(f2st.enter_context(
                    nc.sbuf_tensor(name, shape, dt)), name)
            x2f2 = [f2sb(f"x2f{i}", [128, H], F32) for i in (0, 1)]
            lnscr2 = [f2sb(f"lnscr{i}", [128, H], F32) for i in (0, 1)]
            oln2 = [f2sb(f"oln{i}", [128, H], F32) for i in (0, 1)]

            def fin_chain(tb):
                x2f = x2f2[tb % 2]
                lnscr = lnscr2[tb % 2]
                oln = oln2[tb % 2]
                p.op("vector", lambda tb=tb, x2f=x2f: nc.vector.tensor_add(
                    x2f.t[:, :H], x1.t[:, tb * H:(tb + 1) * H],
                    x2acc.t[:, tb * H:(tb + 1) * H]),
                    reads=[x1, x2acc], writes=[x2f])
                yield
                yield from g_layernorm(x2f, lnscr, oln, oln.t[:, :H],
                                       lnstat2[tb % 2])
                tok = p.op("sync", lambda tb=tb, oln=oln: nc.sync.dma_start(
                    out=out_d[tb * 128:(tb + 1) * 128, :], in_=oln.t[:, :H]),
                    reads=[oln], dma=True)
                st_toks.append(tok)
            for tb0 in range(0, NTB, 2):
                gens = [fin_chain(tb0)]
                if tb0 + 1 < NTB:
                    gens.append(fin_chain(tb0 + 1))
                drain(*gens)
            p.fence()

    p.finish(st_toks)
    p.run_block()
    stack.close()
    return nc



_NC_CACHE = {}


def build_for(S, H, NH, NKV, I, n_cores=8):
    key = (S, H, NH, NKV, I, n_cores)
    if key not in _NC_CACHE:
        _NC_CACHE[key] = build_program(S, H, NH, NKV, I, n_cores=n_cores)
    return _NC_CACHE[key]


def weight_shards(wg, wu, wd, H, n_cores):
    I = wg.shape[0]
    tiles = [wg, wu]
    for cc in range(I // H):
        tiles.append(np.ascontiguousarray(wd[:, cc * H:(cc + 1) * H]))
    full = np.concatenate(tiles, axis=0)
    rows_pc = full.shape[0] // n_cores
    return [np.ascontiguousarray(full[c * rows_pc:(c + 1) * rows_pc])
            for c in range(n_cores)]


def kernel(L0_attn_x, L0_attn_Wq, L0_attn_Wk, L0_attn_Wv, L0_attn_Wo,
           L0_attn_gq, L0_attn_gk, L0_ffn_w_gate, L0_ffn_w_up, L0_ffn_w_down):
    x = np.ascontiguousarray(L0_attn_x, dtype=np.float32)
    B, S, H = x.shape
    NB = S // 128
    I = L0_ffn_w_gate.shape[0]
    n_cores = 2 * B
    nc = build_for(S, H, 16, 4, I, n_cores)
    shards = weight_shards(
        np.ascontiguousarray(L0_ffn_w_gate, dtype=np.float32),
        np.ascontiguousarray(L0_ffn_w_up, dtype=np.float32),
        np.ascontiguousarray(L0_ffn_w_down, dtype=np.float32), H, n_cores)
    in_maps = []
    metas = []
    for c in range(n_cores):
        b, par = c // 2, c % 2
        js = qblocks_for_parity(NB, par)
        rows = np.concatenate([np.arange(j * 128, (j + 1) * 128)
                               for j in js])
        in_maps.append({
            "x": x[b],
            "xqrows": np.ascontiguousarray(x[b][rows]),
            "qpos": np.ascontiguousarray(rows.astype(np.float32)[None, :]),
            "wq": np.ascontiguousarray(L0_attn_Wq, dtype=np.float32),
            "wk": np.ascontiguousarray(L0_attn_Wk, dtype=np.float32),
            "wv": np.ascontiguousarray(L0_attn_Wv, dtype=np.float32),
            "wo": np.ascontiguousarray(L0_attn_Wo, dtype=np.float32),
            "gq": np.ascontiguousarray(L0_attn_gq, dtype=np.float32)[None, :],
            "gk": np.ascontiguousarray(L0_attn_gk, dtype=np.float32)[None, :],
            "wshard": shards[c],
        })
        metas.append((b, js))
    res = run_bass_kernel_spmd(nc, in_maps, core_ids=list(range(len(in_maps))))
    out = np.empty((B, S, H), dtype=np.float32)
    for c, r in enumerate(res.results):
        b, js = metas[c]
        o = np.asarray(r["out"])
        for qi, j in enumerate(js):
            out[b, j * 128:(j + 1) * 128] = o[qi * 128:(qi + 1) * 128]
    return out


# version 49
# speedup vs baseline: 1.0689x; 1.0689x over previous
import contextlib
import numpy as np
import concourse.bass as bass
import concourse.mybir as mybir
from concourse.bass_utils import run_bass_kernel_spmd

F32 = mybir.dt.float32
BF16 = mybir.dt.bfloat16
AF = mybir.ActivationFunctionType
OP = mybir.AluOpType
AX = mybir.AxisListType
LN2 = float(np.log(2.0))

SKIP_SELF_WAITS = False
ENGINES = ("sync", "scalar", "vector", "tensor", "gpsimd")
ESEM = {"sync": "s_ld", "scalar": "s_sc", "vector": "s_ve",
        "tensor": "s_pe", "gpsimd": "s_po"}
EDSEM = {"sync": "s_ld", "scalar": "s_scd", "gpsimd": "s_pod"}


class Buf:
    __slots__ = ("t", "w", "rs", "name")

    def __init__(self, t, name=""):
        self.t = t
        self.w = None
        self.rs = []
        self.name = name


class Prog:
    NDMA = 32

    def __init__(self, nc):
        self.nc = nc
        self.plans = {e: [] for e in ENGINES}
        self.sems = {}
        self.lastw = {e: {} for e in ENGINES}
        self.fence_toks = []
        self.n_ops = 0
        self.dma_rr = {"hw": 0, "sw": 0}

    def next_dma_sem(self, kind):
        name = f"d{kind}{self.dma_rr[kind] % self.NDMA}"
        self.dma_rr[kind] += 1
        return name

    def sem(self, name):
        if name not in self.sems:
            self.sems[name] = [self.nc.alloc_semaphore(name), 0]
        return self.sems[name]

    def fence(self, exclude=()):
        self.fence_toks = [(n, rec[1]) for n, rec in self.sems.items()
                           if rec[1] > 0 and n not in exclude]

    def op(self, eng, fn, reads=(), writes=(), dma=False, cc=False,
           extra_waits=()):
        if cc:
            sname = "s_cc"
            inc = 1
        elif dma:
            sname = self.next_dma_sem("sw" if eng == "gpsimd" else "hw")
            inc = 16
        else:
            sname = ESEM[eng]
            inc = 1
        rec = self.sem(sname)
        waits = {}

        def addw(tok):
            if tok is None:
                return
            s, v = tok
            if waits.get(s, 0) < v:
                waits[s] = v
        for b in reads:
            addw(b.w)
        for b in writes:
            addw(b.w)
            for t in b.rs:
                addw(t)
        for t in extra_waits:
            addw(t)
        for t in self.fence_toks:
            addw(t)
        rec[1] += inc
        mytok = (sname, rec[1])
        for b in writes:
            b.w = mytok
            b.rs = []
        for b in reads:
            b.rs.append(mytok)
        if SKIP_SELF_WAITS:
            own = ESEM[eng]
            waits.pop(own, None)
        wl = sorted(waits.items())
        prog = self

        def emit(wl=wl, fn=fn, sname=sname, inc=inc, eng=eng):
            e = getattr(prog.nc, eng)
            for s, v in wl:
                if prog.lastw[eng].get(s, 0) < v:
                    e.wait_ge(prog.sems[s][0], v)
                    prog.lastw[eng][s] = v
            inst = fn()
            inst.then_inc(prog.sems[sname][0], inc)
        self.plans[eng].append(emit)
        self.n_ops += 1
        return mytok

    def gpdma_group(self, items):
        sname = self.next_dma_sem("sw")
        rec = self.sem(sname)
        waits = {}

        def addw(tok):
            if tok is None:
                return
            sx, v = tok
            if waits.get(sx, 0) < v:
                waits[sx] = v
        allw = []
        for fn, writes in items:
            for b in writes:
                addw(b.w)
                for t in b.rs:
                    addw(t)
                allw.append(b)
        for t in self.fence_toks:
            addw(t)
        rec[1] += 16 * len(items)
        otok = (sname, rec[1])
        for b in allw:
            b.w = otok
            b.rs = []
        wl = sorted(waits.items())
        prog = self

        def emit(wl=wl, items=items, sname=sname):
            e = prog.nc.gpsimd
            for sx, v in wl:
                if prog.lastw["gpsimd"].get(sx, 0) < v:
                    e.wait_ge(prog.sems[sx][0], v)
                    prog.lastw["gpsimd"][sx] = v
            for fn, _w in items:
                fn().then_inc(prog.sems[sname][0], 16)
        self.plans["gpsimd"].append(emit)
        self.n_ops += len(items)
        return otok

    def finish(self, final_tokens):
        waits = {}
        for s, v in final_tokens:
            waits[s] = max(waits.get(s, 0), v)

        def emit():
            for s, v in sorted(waits.items()):
                self.nc.sync.wait_ge(self.sems[s][0], v)
        self.plans["sync"].append(emit)

    def run_block(self):
        nc = self.nc
        with nc.Block() as block:
            @block.sync
            def _(_e):
                for f in self.plans["sync"]:
                    f()

            @block.scalar
            def _(_e):
                for f in self.plans["scalar"]:
                    f()

            @block.vector
            def _(_e):
                for f in self.plans["vector"]:
                    f()

            @block.tensor
            def _(_e):
                for f in self.plans["tensor"]:
                    f()

            @block.gpsimd
            def _(_e):
                for f in self.plans["gpsimd"]:
                    f()


def qblocks_for_parity(NB, parity):
    out = []
    for k in range(NB // 2):
        if (k % 2) == parity:
            out.append(k)
            out.append(NB - 1 - k)
    return sorted(out, key=lambda j: ((j + 2) // 2, j))


def even_klen(j):
    return 128 * (((j + 1) + 1) // 2 * 2)


def build_program(S, H, NH, NKV, I, n_cores=8, eps_rms=1e-6, eps_ln=1e-5):
    D = 128
    HC = H // 128
    NB = S // 128
    NQ = NB // 2
    TQ = NQ * 128
    NTB = TQ // 128
    GR = NH // NKV
    KVW = NKV * D
    OCQ = (NH * D) // 512
    IQ = min(512, I)
    NIQ = I // IQ

    NT_WG = I // 128
    NT_WD = (I // H) * (H // 128)
    TOT_T = 2 * NT_WG + NT_WD
    TPC = TOT_T // n_cores
    assert TPC * n_cores == TOT_T and TPC % 2 == 0
    WU0 = NT_WG * 128
    WD0 = 2 * NT_WG * 128

    KLENS = [even_klen(j) for j in qblocks_for_parity(NB, 0)]
    assert KLENS == [even_klen(j) for j in qblocks_for_parity(NB, 1)]

    nc = bass.Bass("TRN2", target_bir_lowering=False,
                   detect_race_conditions=not SKIP_SELF_WAITS)
    p = Prog(nc)

    x_d = nc.dram_tensor("x", [S, H], F32, kind="ExternalInput").ap()
    xq_d = nc.dram_tensor("xqrows", [TQ, H], F32, kind="ExternalInput").ap()
    qpos_d = nc.dram_tensor("qpos", [1, TQ], F32, kind="ExternalInput").ap()
    wq_d = nc.dram_tensor("wq", [H, NH * D], F32, kind="ExternalInput").ap()
    wk_d = nc.dram_tensor("wk", [H, KVW], F32, kind="ExternalInput").ap()
    wv_d = nc.dram_tensor("wv", [H, KVW], F32, kind="ExternalInput").ap()
    wo_d = nc.dram_tensor("wo", [NH * D, H], F32, kind="ExternalInput").ap()
    gq_d = nc.dram_tensor("gq", [1, D], F32, kind="ExternalInput").ap()
    gk_d = nc.dram_tensor("gk", [1, D], F32, kind="ExternalInput").ap()
    wsh_d = nc.dram_tensor("wshard", [TPC * 128, H], F32,
                           kind="ExternalInput").ap()
    out_d = nc.dram_tensor("out", [TQ, H], F32, kind="ExternalOutput").ap()
    qsh_d = nc.dram_tensor("qshard", [TPC * 128, H], BF16,
                           kind="Internal").ap()
    qfull_d = nc.dram_tensor("qfull", [TOT_T * 128, H], BF16,
                             kind="Internal").ap()
    qsh_b = Buf(None, "qshard")
    qfull_b = Buf(None, "qfull")

    stack = contextlib.ExitStack()

    def sb(name, shape, dt, st=None):
        t = (st or stack).enter_context(nc.sbuf_tensor(name, shape, dt))
        return Buf(t, name)

    def ps(name, shape, dt):
        t = stack.enter_context(nc.psum_tensor(name, shape, dt))
        return Buf(t, name)

    cpool = sb("cpool", [128, 16], F32)
    CVALS = {}

    def cap(val):
        val = float(val)
        if val not in CVALS:
            CVALS[val] = len(CVALS)
            assert len(CVALS) <= 16
        return cpool.t[:, CVALS[val]:CVALS[val] + 1]

    ident = sb("ident", [128, 128], BF16)
    gq_b = sb("gq_t", [128, D], F32)
    gk_b = sb("gk_t", [128, D], F32)
    qpos_b = sb("qpos_t", [128, NQ], F32)
    x1 = sb("x1_all", [128, NQ * H], BF16)
    lnstat = sb("lnstat", [128, 8], F32)
    lnstat1 = sb("lnstat1", [128, 8], F32)
    lnstat2 = [lnstat, lnstat1]
    qm0 = {n: sb("qm_" + n, [128, 128], F32) for n in
           ("amax", "z", "rem", "fl", "invs", "back")}

    psA = [ps(f"psA{i}", [128, 512], F32) for i in (0, 1)]
    psB = [ps(f"psB{i}", [128, 512], F32) for i in (0, 1)]
    psT = [ps(f"psT{i}", [128, 512], BF16) for i in (0, 1)]
    psC = [ps(f"psC{i}", [128, 512], F32) for i in (0, 1)]

    def transpose_to(dst_buf, dst_ap, src_buf, src_ap, slot):
        tp = psT[slot % 2]
        p.op("tensor", lambda tp=tp, src_ap=src_ap: nc.tensor.transpose(
            tp.t[:, 0:128], src_ap, ident.t[:, :]),
            reads=[src_buf, ident], writes=[tp])
        p.op("vector", lambda tp=tp, dst_ap=dst_ap: nc.vector.tensor_copy(
            dst_ap, tp.t[:, 0:128]), reads=[tp], writes=[dst_buf])

    def ap3(t, off, stride, n):
        F = int(t.shape[1])
        return bass.AP(t, off, [[F, 128], [stride, n], [1, 128]])

    def transpose_batch(dst_buf, dst_t, dst_off, dst_stride, src_buf, src_t,
                        src_off, n, slot, src_stride=128):
        tp = psT[slot % 2]

        def f_tr(tp=tp, n=n, src_t=src_t, src_off=src_off,
                 src_stride=src_stride):
            last = None
            for i in range(n):
                Fs = int(src_t.shape[1])
                sap = bass.AP(src_t, src_off + i * src_stride,
                              [[Fs, 128], [1, 128]])
                last = nc.tensor.transpose(
                    tp.t[:, i * 128:(i + 1) * 128], sap, ident.t[:, :])
            return last
        p.op("tensor", f_tr, reads=[src_buf, ident], writes=[tp])

        def f_ev(tp=tp, n=n, dst_t=dst_t, dst_off=dst_off,
                 dst_stride=dst_stride):
            return nc.vector.tensor_copy(
                ap3(dst_t, dst_off, dst_stride, n), tp.t[:, :n * 128])
        p.op("vector", f_ev, reads=[tp], writes=[dst_buf])

    def g_maps(src, F, kind, qm):
        nb = F // 32
        src3 = src.t[:, :F].rearrange("p (n b) -> p n b", b=32)
        bias_i = cap(((2 if kind == 'w' else 5) + 64) * LN2)
        bias_b = cap(-((3 if kind == 'w' else 5) + 64) * LN2)
        am, z, rem, fl = qm["amax"], qm["z"], qm["rem"], qm["fl"]
        invs, back = qm["invs"], qm["back"]
        yield p.op("vector", lambda: nc.vector.tensor_reduce(
            am.t[:, :nb], src3, AX.X, OP.max, apply_absolute_value=True),
            reads=[src], writes=[am])
        yield p.op("vector", lambda: nc.vector.tensor_scalar_max(
            am.t[:, :nb], am.t[:, :nb], float(2.0 ** -60)),
            reads=[am], writes=[am])
        yield p.op("scalar", lambda: nc.scalar.activation(
            z.t[:, :nb], am.t[:, :nb], AF.Ln),
            reads=[am, cpool], writes=[z])
        yield p.op("vector", lambda: nc.vector.tensor_scalar(
            rem.t[:, :nb], z.t[:, :nb], 1.0 / LN2, 64.0, OP.mult, OP.add),
            reads=[z], writes=[rem])
        yield p.op("vector", lambda: nc.vector.tensor_copy(
            z.t[:, :nb], rem.t[:, :nb]), reads=[rem], writes=[z])
        for k in (64.0, 32.0, 16.0, 8.0, 4.0, 2.0, 1.0):
            yield p.op("vector", lambda k=k: nc.vector.tensor_single_scalar(
                fl.t[:, :nb], rem.t[:, :nb], k, OP.is_ge),
                reads=[rem], writes=[fl])
            yield p.op("vector", lambda k=k: nc.vector.scalar_tensor_tensor(
                rem.t[:, :nb], fl.t[:, :nb], -k, rem.t[:, :nb],
                OP.mult, OP.add), reads=[fl, rem], writes=[rem])
        yield p.op("vector", lambda: nc.vector.tensor_sub(
            fl.t[:, :nb], z.t[:, :nb], rem.t[:, :nb]),
            reads=[z, rem], writes=[fl])
        yield p.op("scalar", lambda: nc.scalar.activation(
            invs.t[:, :nb], fl.t[:, :nb], AF.Exp,
            scale=cap(-LN2), bias=bias_i), reads=[fl, cpool], writes=[invs])
        yield p.op("scalar", lambda: nc.scalar.activation(
            back.t[:, :nb], fl.t[:, :nb], AF.Exp,
            scale=cap(LN2), bias=bias_b), reads=[fl, cpool], writes=[back])

    def g_quant(scr, src, out_buf, out_ap, F, kind, qm=None,
                so=0, nbo=0, maps=True):
        if qm is None:
            qm = qm0
        nb = F // 32
        if maps:
            yield from g_maps(src, F, kind, qm)
        ibc = qm["invs"].t[:, nbo:nbo + nb].broadcast_to([128, nb, 32])
        bbc = qm["back"].t[:, nbo:nbo + nb].broadcast_to([128, nb, 32])
        s3 = src.t[:, so:so + F].rearrange("p (n b) -> p n b", b=32)
        y, u, a, b, c, d = (scr[k] for k in "yuabcd")
        y3 = y.t[:, :F].rearrange("p (n b) -> p n b", b=32)
        o3 = out_ap.rearrange("p (n b) -> p n b", b=32)
        d3 = d.t[:, :F].rearrange("p (n b) -> p n b", b=32)
        V = "vector"
        yield p.op(V, lambda: nc.vector.tensor_tensor(y3, s3, ibc, OP.mult),
             reads=[src, qm["invs"]], writes=[y])
        if kind == 'w':
            yield p.op("scalar", lambda: nc.scalar.activation(
                u.t[:, :F], y.t[:, :F], AF.Abs),
                reads=[y, cpool], writes=[u])
            yield p.op(V, lambda: nc.vector.tensor_single_scalar(
                a.t[:, :F], u.t[:, :F], 0.25, OP.is_ge),
                reads=[u], writes=[a])
            for t in (0.75, 1.25, 1.75):
                yield p.op(V, lambda t=t: nc.vector.scalar_tensor_tensor(
                    a.t[:, :F], u.t[:, :F], t, a.t[:, :F], OP.is_ge, OP.add),
                    reads=[u, a], writes=[a])
            yield p.op(V, lambda: nc.vector.tensor_single_scalar(
                b.t[:, :F], u.t[:, :F], 2.5, OP.is_ge),
                reads=[u], writes=[b])
            yield p.op(V, lambda: nc.vector.scalar_tensor_tensor(
                b.t[:, :F], u.t[:, :F], 3.5, b.t[:, :F], OP.is_ge, OP.add),
                reads=[u, b], writes=[b])
            yield p.op(V, lambda: nc.vector.tensor_single_scalar(
                c.t[:, :F], u.t[:, :F], 5.0, OP.is_ge),
                reads=[u], writes=[c])
            yield p.op(V, lambda: nc.vector.scalar_tensor_tensor(
                a.t[:, :F], b.t[:, :F], 2.0, a.t[:, :F], OP.mult, OP.add),
                reads=[b, a], writes=[a])
            yield p.op(V, lambda: nc.vector.scalar_tensor_tensor(
                a.t[:, :F], c.t[:, :F], 4.0, a.t[:, :F], OP.mult, OP.add),
                reads=[c, a], writes=[a])
            qmag = a
        else:
            yield p.op("scalar", lambda: nc.scalar.activation(
                u.t[:, :F], y.t[:, :F], AF.Abs),
                reads=[y, cpool], writes=[u])
            yield p.op(V, lambda: nc.vector.tensor_single_scalar(
                u.t[:, :F], u.t[:, :F], 61.9, OP.min),
                reads=[u], writes=[u])
            yield p.op(V, lambda: nc.vector.tensor_single_scalar(
                a.t[:, :F], u.t[:, :F], 16.0, OP.is_ge),
                reads=[u], writes=[a])
            yield p.op(V, lambda: nc.vector.tensor_single_scalar(
                b.t[:, :F], u.t[:, :F], 32.0, OP.is_ge),
                reads=[u], writes=[b])
            yield p.op(V, lambda: nc.vector.tensor_scalar(
                c.t[:, :F], a.t[:, :F], -0.5, 1.0, OP.mult, OP.add),
                reads=[a], writes=[c])
            yield p.op(V, lambda: nc.vector.scalar_tensor_tensor(
                c.t[:, :F], b.t[:, :F], -0.25, c.t[:, :F], OP.mult, OP.add),
                reads=[b, c], writes=[c])
            yield p.op(V, lambda: nc.vector.scalar_tensor_tensor(
                b.t[:, :F], b.t[:, :F], 2.0, a.t[:, :F], OP.mult, OP.add),
                reads=[a, b], writes=[b])
            yield p.op(V, lambda: nc.vector.tensor_mul(
                d.t[:, :F], c.t[:, :F], u.t[:, :F]),
                reads=[c, u], writes=[d])
            yield p.op(V, lambda: nc.vector.tensor_single_scalar(
                u.t[:, :F], d.t[:, :F], 8.0, OP.is_ge),
                reads=[d], writes=[u])
            yield p.op(V, lambda: nc.vector.scalar_tensor_tensor(
                d.t[:, :F], u.t[:, :F], -8.0, d.t[:, :F], OP.mult, OP.add),
                reads=[u, d], writes=[d])
            yield p.op(V, lambda: nc.vector.tensor_single_scalar(
                c.t[:, :F], d.t[:, :F], 4.0, OP.is_ge),
                reads=[d], writes=[c])
            yield p.op(V, lambda: nc.vector.scalar_tensor_tensor(
                d.t[:, :F], c.t[:, :F], -4.0, d.t[:, :F], OP.mult, OP.add),
                reads=[c, d], writes=[d])
            yield p.op(V, lambda: nc.vector.tensor_single_scalar(
                a.t[:, :F], d.t[:, :F], 0.5, OP.is_ge),
                reads=[d], writes=[a])
            for t in (1.5, 2.5, 3.5):
                yield p.op(V, lambda t=t: nc.vector.scalar_tensor_tensor(
                    a.t[:, :F], d.t[:, :F], t, a.t[:, :F], OP.is_ge, OP.add),
                    reads=[d, a], writes=[a])
            yield p.op(V, lambda: nc.vector.scalar_tensor_tensor(
                a.t[:, :F], c.t[:, :F], 4.0, a.t[:, :F], OP.mult, OP.add),
                reads=[c, a], writes=[a])
            yield p.op(V, lambda: nc.vector.scalar_tensor_tensor(
                a.t[:, :F], u.t[:, :F], 8.0, a.t[:, :F], OP.mult, OP.add),
                reads=[u, a], writes=[a])
            yield p.op(V, lambda: nc.vector.scalar_tensor_tensor(
                a.t[:, :F], b.t[:, :F], 1.0, a.t[:, :F], OP.add, OP.mult),
                reads=[b, a], writes=[a])
            qmag = a
        yield p.op("scalar", lambda: nc.scalar.activation(
            c.t[:, :F], y.t[:, :F], AF.Sign), reads=[y, cpool], writes=[c])
        yield p.op(V, lambda qmag=qmag: nc.vector.tensor_mul(
            d.t[:, :F], qmag.t[:, :F], c.t[:, :F]),
            reads=[qmag, c], writes=[d])
        yield p.op(V, lambda: nc.vector.tensor_tensor(o3, d3, bbc, OP.mult),
             reads=[d, qm["back"]], writes=[out_buf])

    def drain(*gens):
        gens = list(gens)
        while gens:
            keep = []
            for g in gens:
                try:
                    next(g)
                    keep.append(g)
                except StopIteration:
                    pass
            gens = keep

    def drain_skew(gens, skew=2):
        gens = list(gens)
        n = len(gens)
        live = []
        for j, g in enumerate(gens):
            ok = True
            for _ in range((n - 1 - j) * skew):
                try:
                    next(g)
                except StopIteration:
                    ok = False
                    break
            if ok:
                live.append(g)
        drain(*live)

    def emit_maps(src, F, kind, qm):
        drain(g_maps(src, F, kind, qm))

    def emit_quant(scr, src, out_buf, out_ap, F, kind, qm=None):
        drain(g_quant(scr, src, out_buf, out_ap, F, kind, qm=qm))

    def emit_quant_pool(scr, src, out_buf, out_ap, F, qm):
        nb = F // 32
        emit_maps(src, F, 'act', qm)
        ibc = qm["invs"].t[:, :nb].broadcast_to([128, nb, 32])
        bbc = qm["back"].t[:, :nb].broadcast_to([128, nb, 32])
        s3 = src.t[:, :F].rearrange("p (n b) -> p n b", b=32)
        y, u, a, b, c, d, r, xx = (scr[k] for k in "yuabcdrx")
        y3 = y.t[:, :F].rearrange("p (n b) -> p n b", b=32)
        o3 = out_ap.rearrange("p (n b) -> p n b", b=32)
        r3 = r.t[:, :F].rearrange("p (n b) -> p n b", b=32)
        G = "gpsimd"

        def po(fn, reads, writes):
            p.op(G, fn, reads=reads, writes=writes)
        po(lambda: nc.gpsimd.tensor_tensor(y3, s3, ibc, OP.mult),
           [src, qm["invs"]], [y])
        po(lambda: nc.gpsimd.tensor_single_scalar(
            u.t[:, :F], y.t[:, :F], 0.0, OP.is_ge), [y], [u])
        po(lambda: nc.gpsimd.tensor_scalar(
            u.t[:, :F], u.t[:, :F], 2.0, -1.0, OP.mult, OP.add), [u], [u])
        po(lambda: nc.gpsimd.tensor_tensor(
            u.t[:, :F], y.t[:, :F], u.t[:, :F], OP.mult), [y, u], [u])
        po(lambda: nc.gpsimd.tensor_single_scalar(
            u.t[:, :F], u.t[:, :F], 61.9, OP.min), [u], [u])
        po(lambda: nc.gpsimd.tensor_single_scalar(
            a.t[:, :F], u.t[:, :F], 16.0, OP.is_ge), [u], [a])
        po(lambda: nc.gpsimd.tensor_single_scalar(
            b.t[:, :F], u.t[:, :F], 32.0, OP.is_ge), [u], [b])
        po(lambda: nc.gpsimd.tensor_scalar(
            c.t[:, :F], a.t[:, :F], -0.5, 1.0, OP.mult, OP.add), [a], [c])
        po(lambda: nc.gpsimd.tensor_single_scalar(
            d.t[:, :F], b.t[:, :F], -0.25, OP.mult), [b], [d])
        po(lambda: nc.gpsimd.tensor_tensor(
            c.t[:, :F], c.t[:, :F], d.t[:, :F], OP.add), [c, d], [c])
        po(lambda: nc.gpsimd.tensor_tensor(
            d.t[:, :F], c.t[:, :F], u.t[:, :F], OP.mult), [c, u], [d])
        po(lambda: nc.gpsimd.tensor_single_scalar(
            xx.t[:, :F], d.t[:, :F], 8.0, OP.is_ge), [d], [xx])
        po(lambda: nc.gpsimd.tensor_single_scalar(
            c.t[:, :F], xx.t[:, :F], -8.0, OP.mult), [xx], [c])
        po(lambda: nc.gpsimd.tensor_tensor(
            d.t[:, :F], d.t[:, :F], c.t[:, :F], OP.add), [d, c], [d])
        po(lambda: nc.gpsimd.tensor_single_scalar(
            c.t[:, :F], d.t[:, :F], 4.0, OP.is_ge), [d], [c])
        po(lambda: nc.gpsimd.tensor_single_scalar(
            u.t[:, :F], c.t[:, :F], -4.0, OP.mult), [c], [u])
        po(lambda: nc.gpsimd.tensor_tensor(
            d.t[:, :F], d.t[:, :F], u.t[:, :F], OP.add), [d, u], [d])
        po(lambda: nc.gpsimd.tensor_single_scalar(
            r.t[:, :F], d.t[:, :F], 0.5, OP.is_ge), [d], [r])
        for t in (1.5, 2.5, 3.5):
            po(lambda t=t: nc.gpsimd.tensor_single_scalar(
                u.t[:, :F], d.t[:, :F], t, OP.is_ge), [d], [u])
            po(lambda: nc.gpsimd.tensor_tensor(
                r.t[:, :F], r.t[:, :F], u.t[:, :F], OP.add), [r, u], [r])
        po(lambda: nc.gpsimd.tensor_single_scalar(
            u.t[:, :F], c.t[:, :F], 4.0, OP.mult), [c], [u])
        po(lambda: nc.gpsimd.tensor_tensor(
            r.t[:, :F], r.t[:, :F], u.t[:, :F], OP.add), [r, u], [r])
        po(lambda: nc.gpsimd.tensor_single_scalar(
            u.t[:, :F], xx.t[:, :F], 8.0, OP.mult), [xx], [u])
        po(lambda: nc.gpsimd.tensor_tensor(
            r.t[:, :F], r.t[:, :F], u.t[:, :F], OP.add), [r, u], [r])
        po(lambda: nc.gpsimd.tensor_single_scalar(
            u.t[:, :F], b.t[:, :F], 2.0, OP.mult), [b], [u])
        po(lambda: nc.gpsimd.tensor_tensor(
            u.t[:, :F], u.t[:, :F], a.t[:, :F], OP.add), [u, a], [u])
        po(lambda: nc.gpsimd.tensor_single_scalar(
            u.t[:, :F], u.t[:, :F], 1.0, OP.add), [u], [u])
        po(lambda: nc.gpsimd.tensor_tensor(
            r.t[:, :F], r.t[:, :F], u.t[:, :F], OP.mult), [r, u], [r])
        po(lambda: nc.gpsimd.tensor_single_scalar(
            u.t[:, :F], y.t[:, :F], 0.0, OP.is_ge), [y], [u])
        po(lambda: nc.gpsimd.tensor_scalar(
            u.t[:, :F], u.t[:, :F], 2.0, -1.0, OP.mult, OP.add), [u], [u])
        po(lambda: nc.gpsimd.tensor_tensor(
            r.t[:, :F], r.t[:, :F], u.t[:, :F], OP.mult), [r, u], [r])
        po(lambda: nc.gpsimd.tensor_tensor(o3, r3, bbc, OP.mult),
           [r, qm["back"]], [out_buf])

    def g_layernorm(src, scratch, out_buf, out_ap, ls):
        yield p.op("vector", lambda: nc.vector.reduce_sum(
            ls.t[:, 0:1], src.t[:, :H], AX.X), reads=[src], writes=[ls])
        yield p.op("scalar", lambda: nc.scalar.activation(
            scratch.t[:, :H], src.t[:, :H], AF.Square,
            accum_out=ls.t[:, 1:2]),
            reads=[src, cpool], writes=[scratch, ls])
        yield p.op("vector", lambda: nc.vector.tensor_single_scalar(
            ls.t[:, 0:1], ls.t[:, 0:1], 1.0 / H, OP.mult),
            reads=[ls], writes=[ls])
        yield p.op("vector", lambda: nc.vector.tensor_mul(
            ls.t[:, 2:3], ls.t[:, 0:1], ls.t[:, 0:1]),
            reads=[ls], writes=[ls])
        yield p.op("vector", lambda: nc.vector.scalar_tensor_tensor(
            ls.t[:, 1:2], ls.t[:, 1:2], 1.0 / H, ls.t[:, 2:3],
            OP.mult, OP.subtract), reads=[ls], writes=[ls])
        yield p.op("scalar", lambda: nc.scalar.activation(
            ls.t[:, 3:4], ls.t[:, 1:2], AF.Ln, bias=cap(eps_ln)),
            reads=[ls, cpool], writes=[ls])
        yield p.op("scalar", lambda: nc.scalar.activation(
            ls.t[:, 3:4], ls.t[:, 3:4], AF.Exp, scale=cap(-0.5)),
            reads=[ls, cpool], writes=[ls])
        yield p.op("vector", lambda: nc.vector.scalar_tensor_tensor(
            ls.t[:, 4:5], ls.t[:, 0:1], -1.0, ls.t[:, 3:4],
            OP.mult, OP.mult), reads=[ls], writes=[ls])
        yield p.op("scalar", lambda out_ap=out_ap: nc.scalar.activation(
            out_ap, src.t[:, :H], AF.Identity,
            scale=ls.t[:, 3:4], bias=ls.t[:, 4:5]),
            reads=[src, ls, cpool], writes=[out_buf])

    def emit_layernorm(src, scratch, out_buf, out_ap):
        drain(g_layernorm(src, scratch, out_buf, out_ap, lnstat))

    for v in [-LN2, LN2, eps_ln, eps_rms, -0.5, -1.0, 1.0,
              (2 + 64) * LN2, -(3 + 64) * LN2,
              (5 + 64) * LN2, -(5 + 64) * LN2, 1.0 / D]:
        cap(v)

    def f_consts():
        last = None
        for v, col in sorted(CVALS.items(), key=lambda t: t[1]):
            last = nc.gpsimd.memset(cpool.t[:, col:col + 1], float(v))
        return last
    p.op("gpsimd", f_consts, writes=[cpool])

    p.op("gpsimd", lambda: nc.gpsimd.memset(ident.t[:, :], 1.0),
         writes=[ident])
    p.op("gpsimd", lambda: nc.gpsimd.affine_select(
        out=ident.t[:, :], in_=ident.t[:, :], compare_op=OP.is_equal,
        fill=0.0, base=0, pattern=[[-1, 128]], channel_multiplier=1),
        reads=[ident], writes=[ident])

    gq_src = bass.AP(gq_d.tensor, 0, [[0, 128], [1, D]])
    gk_src = bass.AP(gk_d.tensor, 0, [[0, 128], [1, D]])
    p.op("gpsimd", lambda: nc.gpsimd.dma_start(out=gq_b.t[:, :], in_=gq_src),
         writes=[gq_b], dma=True)
    p.op("gpsimd", lambda: nc.gpsimd.dma_start(out=gk_b.t[:, :], in_=gk_src),
         writes=[gk_b], dma=True)
    p.op("vector", lambda: nc.vector.tensor_scalar_mul(
        gq_b.t[:, :], gq_b.t[:, :], float(D) ** -0.5),
        reads=[gq_b], writes=[gq_b])
    for q in range(NQ):
        src = bass.AP(qpos_d.tensor, q * 128, [[1, 128], [128 * NQ, 1]])
        p.op("gpsimd", lambda q=q, src=src: nc.gpsimd.dma_start(
            out=qpos_b.t[:, q:q + 1], in_=src), writes=[qpos_b], dma=True)

    with contextlib.ExitStack() as wst:
        def wsb(name, shape, dt):
            t = wst.enter_context(nc.sbuf_tensor(name, shape, dt))
            return Buf(t, name)
        WW = min(1024, H)
        NWC = H // WW
        NWS = 4
        scrW4 = [{k: wsb(f"scrW{i}_" + k, [128, WW], F32)
                  for k in "yuabcd"} for i in range(NWS)]
        win = [wsb(f"win{i}", [128, WW], F32) for i in range(NWS)]
        wout = [wsb(f"wout{i}", [128, WW], BF16) for i in range(NWS)]
        qmW4 = [qm0] + [
            {n: wsb(f"qmW{i}_" + n, [128, max(32, WW // 32)], F32) for n in
             ("amax", "z", "rem", "fl", "invs", "back")}
            for i in (1, 2, 3)]

        def w_chain(ci):
            rr, hh = divmod(ci, NWC)
            j = ci % NWS
            bi, bo = win[j], wout[j]
            p.op("sync", lambda rr=rr, hh=hh, bi=bi:
                 nc.sync.dma_start(
                     out=bi.t[:, :WW],
                     in_=wsh_d[rr * 128:(rr + 1) * 128,
                               hh * WW:(hh + 1) * WW]),
                 writes=[bi], dma=True)
            yield
            yield from g_quant(scrW4[j], bi, bo, bo.t[:, :WW], WW,
                               'w', qm=qmW4[j])
            p.op("sync", lambda rr=rr, hh=hh, bo=bo:
                 nc.sync.dma_start(
                     out=qsh_d[rr * 128:(rr + 1) * 128,
                               hh * WW:(hh + 1) * WW],
                     in_=bo.t[:, :WW]),
                 reads=[bo], writes=[qsh_b], dma=True)
        NWT = TPC * NWC
        for c0 in range(0, NWT, NWS):
            drain_skew([w_chain(c) for c in range(c0, min(c0 + NWS, NWT))],
                       skew=3)
        p.fence()

    with contextlib.ExitStack() as ast:
        def asb(name, shape, dt):
            t = ast.enter_context(nc.sbuf_tensor(name, shape, dt))
            return Buf(t, name)
        sqd = asb("sqd", [128, 512], F32)
        sqd1 = asb("sqd1", [128, 512], F32)
        attn_all = asb("attn_all", [128, NQ * NH * 128], BF16)
        ssq2 = [asb(f"ssq{i}", [128, 8], F32) for i in (0, 1)]
        rstd2 = [asb(f"rstd{i}", [128, 8], F32) for i in (0, 1)]
        knorm2 = [asb(f"knorm{i}", [128, 512], BF16) for i in (0, 1)]
        sqd2 = [sqd, sqd1]
        kvq_st = ast.enter_context(contextlib.ExitStack())

        def kvqsb(name, shape, dt):
            return Buf(kvq_st.enter_context(nc.sbuf_tensor(name, shape, dt)),
                       name)
        kfm = kvqsb("kfm", [128, NKV * S], BF16)
        vtm = kvqsb("vtm", [128, (S // 128) * KVW], BF16)
        qfm = kvqsb("qfm", [128, NH * TQ], BF16)

        def g_rms(psrc, nheads, gtile, out_buf, out_ap, sl2=0):
            ssq = ssq2[sl2 % 2]
            rstd = rstd2[sl2 % 2]
            sqd_ = sqd2[sl2 % 2]
            W = nheads * 128
            yield p.op("scalar", lambda W=W, sqd_=sqd_: nc.scalar.activation(
                sqd_.t[:, :W], psrc.t[:, :W], AF.Square),
                reads=[psrc, cpool], writes=[sqd_])
            yield p.op("vector", lambda nheads=nheads, W=W, sqd_=sqd_, ssq=ssq:
                 nc.vector.tensor_reduce(
                     ssq.t[:, :nheads],
                     sqd_.t[:, :W].rearrange("p (h d) -> p h d", d=128),
                     AX.X, OP.add),
                 reads=[sqd_], writes=[ssq])
            yield p.op("scalar", lambda nheads=nheads, ssq=ssq, rstd=rstd:
                 nc.scalar.activation(
                     rstd.t[:, :nheads], ssq.t[:, :nheads], AF.Ln,
                     scale=cap(1.0 / D), bias=cap(eps_rms)),
                 reads=[ssq, cpool], writes=[rstd])
            yield p.op("scalar", lambda nheads=nheads, rstd=rstd:
                 nc.scalar.activation(
                     rstd.t[:, :nheads], rstd.t[:, :nheads], AF.Exp,
                     scale=cap(-0.5)), reads=[rstd, cpool], writes=[rstd])
            yield p.op("vector", lambda nheads=nheads, W=W, rstd=rstd, sqd_=sqd_:
                 nc.vector.tensor_tensor(
                     sqd_.t[:, :W].rearrange("p (h d) -> p h d", d=128),
                     psrc.t[:, :W].rearrange("p (h d) -> p h d", d=128),
                     rstd.t[:, :nheads].broadcast_to([128, nheads, 128]),
                     OP.mult),
                 reads=[psrc, rstd], writes=[sqd_])
            yield p.op("vector", lambda nheads=nheads, W=W, out_ap=out_ap,
                 sqd_=sqd_: nc.vector.tensor_tensor(
                     out_ap[:, :W].rearrange("p (h d) -> p h d", d=128),
                     sqd_.t[:, :W].rearrange("p (h d) -> p h d", d=128),
                     bass.AP(gtile.t, 0, [[int(gtile.t.shape[1]), 128],
                                          [0, nheads], [1, 128]]),
                     OP.mult),
                 reads=[sqd_, gtile], writes=[out_buf])

        def rms_head_cols(psrc, nheads, gtile, out_buf, out_ap, sl2=0):
            drain(g_rms(psrc, nheads, gtile, out_buf, out_ap, sl2))

        with contextlib.ExitStack() as qst:
            def qsb(name, shape, dt):
                return Buf(qst.enter_context(nc.sbuf_tensor(name, shape, dt)),
                           name)
            xqfm = qsb("xqfm", [128, HC * TQ], BF16)
            wqs = qsb("wqs", [128, HC * 512], BF16)
            xtm = [qsb("xtmq0", [128, H], BF16)] * 2
            for tc in range(NTB):
                bx = xtm[tc % 2]
                p.op("gpsimd", lambda tc=tc, bx=bx: nc.gpsimd.dma_start(
                    out=bx.t[:, :], in_=xq_d[tc * 128:(tc + 1) * 128, :]),
                    writes=[bx], dma=True)
                for hc0 in range(0, HC, 4):
                    n = min(4, HC - hc0)
                    transpose_batch(xqfm, xqfm.t, hc0 * TQ + tc * 128, TQ,
                                    bx, bx.t, hc0 * 128, n, hc0 // 4)
            HHC = max(1, HC // 4)
            for oc in range(OCQ):
                items = []
                for g2 in range(HC // HHC):
                    src = wq_d[g2 * HHC * 128:(g2 + 1) * HHC * 128,
                               oc * 512:(oc + 1) * 512]
                    src = src.rearrange("(h p) c -> p h c", p=128)
                    items.append((lambda src=src, g2=g2, HHC=HHC:
                                  nc.gpsimd.dma_start(
                                      out=wqs.t[:, g2 * HHC * 512:
                                                (g2 + 1) * HHC * 512]
                                      .rearrange("p (h c) -> p h c", c=512),
                                      in_=src), [wqs]))
                p.gpdma_group(items)
                def q_chain(oc, tc):
                    pp = psA[tc % 2]
                    def f_q(tc=tc, pp=pp):
                        last = None
                        for hc in range(HC):
                            last = nc.tensor.matmul(
                                pp.t[:, :512],
                                xqfm.t[:, hc * TQ + tc * 128:
                                       hc * TQ + (tc + 1) * 128],
                                wqs.t[:, hc * 512:(hc + 1) * 512],
                                start=(hc == 0), stop=(hc == HC - 1))
                        return last
                    p.op("tensor", f_q, reads=[xqfm, wqs], writes=[pp])
                    yield
                    knorm = knorm2[tc % 2]
                    yield from g_rms(pp, 512 // 128, gq_b, knorm, knorm.t,
                                     tc)
                    transpose_batch(qfm, qfm.t, (oc * 4) * TQ + tc * 128,
                                    TQ, knorm, knorm.t, 0, 512 // 128, tc)
                for tc0 in range(0, NTB, 2):
                    gens = [q_chain(oc, tc0)]
                    if tc0 + 1 < NTB:
                        gens.append(q_chain(oc, tc0 + 1))
                    drain(*gens)
            p.fence()

        with contextlib.ExitStack() as kst:
            def ksb(name, shape, dt):
                return Buf(kst.enter_context(nc.sbuf_tensor(name, shape, dt)),
                           name)
            wkv = ksb("wkv", [128, HC * KVW * 2], BF16)
            xtm = [ksb(f"xtmk{i}", [128, H], BF16) for i in (0, 1)]
            xfmc = [ksb(f"xfmc{i}", [128, HC * 128], BF16) for i in (0, 1)]
            nkvg = max(1, (H // KVW) // 2)
            items = []
            for part, wsrc in ((0, wk_d), (1, wv_d)):
                for h0 in range(0, HC, nkvg):
                    n = min(nkvg, HC - h0)
                    src = wsrc[h0 * 128:(h0 + n) * 128, :]
                    src = src.rearrange("(h p) c -> p h c", p=128)
                    off = part * HC * KVW + h0 * KVW
                    items.append((lambda src=src, off=off, n=n:
                                  nc.gpsimd.dma_start(
                                      out=wkv.t[:, off:off + n * KVW]
                                      .rearrange("p (h c) -> p h c", c=KVW),
                                      in_=src), [wkv]))
            p.gpdma_group(items)
            def kv_chain(tc):
                bx = xtm[tc % 2]
                fx = xfmc[tc % 2]
                p.op("gpsimd", lambda tc=tc, bx=bx: nc.gpsimd.dma_start(
                    out=bx.t[:, :], in_=x_d[tc * 128:(tc + 1) * 128, :]),
                    writes=[bx], dma=True)
                yield
                for hc0 in range(0, HC, 4):
                    n = min(4, HC - hc0)
                    transpose_batch(fx, fx.t, hc0 * 128, 128,
                                    bx, bx.t, hc0 * 128, n, hc0 // 4 + tc)
                    yield
                pp = psA[tc % 2]
                def f_k(fx=fx, pp=pp):
                    last = None
                    for hc in range(HC):
                        last = nc.tensor.matmul(
                            pp.t[:, :KVW],
                            fx.t[:, hc * 128:(hc + 1) * 128],
                            wkv.t[:, hc * KVW:(hc + 1) * KVW],
                            start=(hc == 0), stop=(hc == HC - 1))
                    return last
                p.op("tensor", f_k, reads=[fx, wkv], writes=[pp])
                yield
                knorm = knorm2[tc % 2]
                yield from g_rms(pp, NKV, gk_b, knorm, knorm.t, tc)
                for h0 in range(0, NKV, 4):
                    n = min(4, NKV - h0)
                    transpose_batch(kfm, kfm.t, h0 * S + tc * 128, S,
                                    knorm, knorm.t, h0 * 128, n,
                                    h0 // 4 + tc)
                    yield
                pv = psB[tc % 2]
                def f_v(fx=fx, pv=pv):
                    last = None
                    for hc in range(HC):
                        last = nc.tensor.matmul(
                            pv.t[:, :KVW],
                            fx.t[:, hc * 128:(hc + 1) * 128],
                            wkv.t[:, HC * KVW + hc * KVW:
                                  HC * KVW + (hc + 1) * KVW],
                            start=(hc == 0), stop=(hc == HC - 1))
                    return last
                p.op("tensor", f_v, reads=[fx, wkv], writes=[pv])
                yield
                p.op("vector", lambda tc=tc, pv=pv: nc.vector.tensor_copy(
                    vtm.t[:, tc * KVW:(tc + 1) * KVW], pv.t[:, :KVW]),
                    reads=[pv], writes=[vtm])
            for tc0 in range(0, S // 128, 2):
                gens = [kv_chain(tc0)]
                if tc0 + 1 < S // 128:
                    gens.append(kv_chain(tc0 + 1))
                drain(*gens)
            p.fence()
            p.op("gpsimd", lambda: nc.gpsimd.collective_compute(
                "AllGather", OP.bypass, [list(range(n_cores))],
                [qsh_d.opt()], [qfull_d.opt()]),
                reads=[qsh_b], writes=[qfull_b], cc=True)

        def ssb(name, shape, dt):
            return Buf(kvq_st.enter_context(nc.sbuf_tensor(name, shape, dt)),
                       name)
        kpos_b = ssb("kpos_t", [128, S], F32)
        p.op("gpsimd", lambda: nc.gpsimd.iota(
            kpos_b.t[:, :], [[1, S]], base=0, channel_multiplier=0,
            allow_small_or_imprecise_dtypes=True), writes=[kpos_b])
        sc_f2 = [ssb(f"sc_f{i}", [128, S], F32) for i in (0, 1)]
        af32_2 = [ssb(f"af32{i}", [128, S], F32) for i in (0, 1)]
        stat2 = [ssb(f"stat2_{i}", [128, 4], F32) for i in (0, 1)]
        msk = ssb("msk", [128, S], F32)

        with contextlib.ExitStack() as tst:
            def tsb(name, shape, dt):
                return Buf(tst.enter_context(nc.sbuf_tensor(name, shape, dt)),
                           name)
            abf2 = [tsb(f"abf{i}", [128, S], BF16) for i in (0, 1)]
            atbJ = tsb("atbJ", [128, 2 * S], BF16)
            for qi in range(NQ):
                kl = KLENS[qi]
                nkc = kl // 128
                p.op("vector", lambda qi=qi, kl=kl: nc.vector.tensor_scalar(
                    msk.t[:, :kl], kpos_b.t[:, :kl], qpos_b.t[:, qi:qi + 1],
                    -1e30, OP.is_gt, OP.mult),
                    reads=[kpos_b, qpos_b], writes=[msk])
                def h_chain(qi, h, kl, nkc):
                    kv = h // GR
                    sl = h % 2
                    sc_f = sc_f2[sl]
                    af32 = af32_2[sl]
                    stat = stat2[sl]
                    abf = abf2[sl]
                    psS = psA if sl == 0 else psC
                    for kc0 in range(0, kl, 512):
                        w = min(512, kl - kc0)
                        pp = psS[(kc0 // 512) % 2]
                        p.op("tensor",
                             lambda qi=qi, h=h, kv=kv, kc0=kc0, w=w, pp=pp:
                             nc.tensor.matmul(
                                 pp.t[:, :w],
                                 qfm.t[:, h * TQ + qi * 128:
                                       h * TQ + (qi + 1) * 128],
                                 kfm.t[:, kv * S + kc0:kv * S + kc0 + w],
                                 start=True, stop=True),
                             reads=[qfm, kfm], writes=[pp])
                        p.op("vector", lambda kc0=kc0, w=w, pp=pp, sc_f=sc_f:
                             nc.vector.tensor_add(
                                 sc_f.t[:, kc0:kc0 + w], pp.t[:, :w],
                                 msk.t[:, kc0:kc0 + w]),
                             reads=[pp, msk], writes=[sc_f])
                        yield
                    p.op("vector", lambda kl=kl, sc_f=sc_f, stat=stat:
                         nc.vector.reduce_max(
                             stat.t[:, 0:1], sc_f.t[:, :kl], AX.X,
                             negate=True),
                         reads=[sc_f], writes=[stat])
                    yield
                    p.op("scalar", lambda kl=kl, sc_f=sc_f, af32=af32,
                         stat=stat: nc.scalar.activation(
                             af32.t[:, :kl], sc_f.t[:, :kl], AF.Exp,
                             bias=stat.t[:, 0:1], accum_out=stat.t[:, 1:2]),
                         reads=[sc_f, stat, cpool], writes=[af32, stat])
                    yield
                    p.op("vector", lambda stat=stat: nc.vector.reciprocal(
                        stat.t[:, 2:3], stat.t[:, 1:2]),
                        reads=[stat], writes=[stat])
                    yield
                    p.op("scalar", lambda kl=kl, af32=af32, stat=stat,
                         abf=abf: nc.scalar.activation(
                             abf.t[:, :kl], af32.t[:, :kl], AF.Identity,
                             scale=stat.t[:, 2:3]),
                         reads=[af32, stat, cpool], writes=[abf])
                    yield
                    for c0 in range(0, nkc, 4):
                        n = min(4, nkc - c0)
                        transpose_batch(atbJ, atbJ.t,
                                        (2 * c0 + sl) * 128, 256,
                                        abf, abf.t, c0 * 128, n,
                                        c0 // 4 + sl)
                        yield
                for h0 in range(0, NH, 2):
                    drain(h_chain(qi, h0, kl, nkc),
                          h_chain(qi, h0 + 1, kl, nkc))
                    kv = h0 // GR
                    pa = psB[(h0 // 2) % 2]
                    def f_av(kv=kv, nkc=nkc, pa=pa):
                        last = None
                        for c in range(nkc):
                            last = nc.tensor.matmul(
                                pa.t[:, :256],
                                vtm.t[:, c * KVW + kv * 128:
                                      c * KVW + (kv + 1) * 128],
                                atbJ.t[:, 2 * c * 128:(2 * c + 2) * 128],
                                start=(c == 0), stop=(c == nkc - 1))
                        return last
                    p.op("tensor", f_av, reads=[vtm, atbJ], writes=[pa])
                    dsl = attn_all.t[:, (qi * NH + h0) * 128:
                                     (qi * NH + h0 + 2) * 128]
                    p.op("vector", lambda dsl=dsl, pa=pa:
                         nc.vector.tensor_copy(dsl, pa.t[:, :256]),
                         reads=[pa], writes=[attn_all])
            p.fence(exclude=("s_cc",))

        kvq_st.close()

        with contextlib.ExitStack() as ost:
            def osb(name, shape, dt):
                return Buf(ost.enter_context(nc.sbuf_tensor(name, shape, dt)),
                           name)
            scw2 = [osb(f"sc_f2{i}", [128, H], F32) for i in (0, 1)]
            afw2 = [osb(f"af32_2{i}", [128, H], F32) for i in (0, 1)]
            xres2 = [osb(f"xres{i}", [128, H], F32) for i in (0, 1)]
            wos = osb("wos", [128, (NH * D // 128) * H], BF16)
            items = []
            for h0 in range(0, NH * D // 128, 2):
                n = min(2, NH * D // 128 - h0)
                src = wo_d[h0 * 128:(h0 + n) * 128, :]
                src = src.rearrange("(h p) c -> p h c", p=128)
                items.append((lambda src=src, h0=h0, n=n:
                              nc.gpsimd.dma_start(
                                  out=wos.t[:, h0 * H:(h0 + n) * H]
                                  .rearrange("p (h c) -> p h c", c=H),
                                  in_=src), [wos]))
            p.gpdma_group(items)
            def wo_chain(qi):
                scw = scw2[qi % 2]
                afw = afw2[qi % 2]
                xres = xres2[qi % 2]
                psS = psA if qi % 2 == 0 else psC
                p.op("sync", lambda qi=qi, xres=xres: nc.sync.dma_start(
                    out=xres.t[:, :H], in_=xq_d[qi * 128:(qi + 1) * 128, :]),
                    writes=[xres], dma=True)
                yield
                for oc in range(H // 512):
                    pp = psS[oc % 2]
                    def f_wo(qi=qi, oc=oc, pp=pp):
                        last = None
                        for h in range(NH):
                            last = nc.tensor.matmul(
                                pp.t[:, :512],
                                attn_all.t[:, (qi * NH + h) * 128:
                                           (qi * NH + h + 1) * 128],
                                wos.t[:, h * H + oc * 512:
                                      h * H + (oc + 1) * 512],
                                start=(h == 0), stop=(h == NH - 1))
                        return last
                    p.op("tensor", f_wo, reads=[attn_all, wos], writes=[pp])
                    yield
                    p.op("vector", lambda oc=oc, pp=pp, scw=scw, xres=xres:
                         nc.vector.tensor_add(
                             scw.t[:, oc * 512:(oc + 1) * 512], pp.t[:, :512],
                             xres.t[:, oc * 512:(oc + 1) * 512]),
                         reads=[pp, xres], writes=[scw])
                    yield
                yield from g_layernorm(scw, afw, x1,
                                       x1.t[:, qi * H:(qi + 1) * H],
                                       lnstat2[qi % 2])
            for qi0 in range(0, NQ, 2):
                gens = [wo_chain(qi0)]
                if qi0 + 1 < NQ:
                    gens.append(wo_chain(qi0 + 1))
                drain(*gens)
            p.fence(exclude=("s_cc",))

    st_toks = []
    with contextlib.ExitStack() as fst:
        def fsb(name, shape, dt):
            return Buf(fst.enter_context(nc.sbuf_tensor(name, shape, dt)),
                       name)
        xqfm2 = fsb("xqfm2", [128, HC * TQ], BF16)
        interq = fsb("interq", [128, (IQ // 128) * TQ], BF16)
        x2acc = fsb("x2acc", [128, NTB * H], BF16)
        with contextlib.ExitStack() as f1st:
            def f1sb(name, shape, dt):
                return Buf(f1st.enter_context(
                    nc.sbuf_tensor(name, shape, dt)), name)
            NSET = min(4, NTB)
            JW = NSET * 512
            JH = JW // 2
            scrJ2 = [{k: f1sb(f"scrJ{i}_" + k, [128, JH], F32)
                      for k in "yuabcd"} for i in (0, 1)]
            HCH = HC // 2
            wb2 = [f1sb(f"wb{i}", [128, HCH * 512], BF16) for i in (0, 1)]
            ICH = max(1, (IQ // 128) // 2)
            wd2 = [f1sb(f"wdh{i}", [128, ICH * 512], BF16) for i in (0, 1)]
            s_sb = f1sb("s_sb", [128, NTB * 512], BF16)
            vjoint = f1sb("vjoint", [128, JW], F32)
            qmFB = {n: f1sb("qmFB_" + n, [128, NSET * 16], F32)
                    for n in ("amax", "z", "rem", "fl", "invs", "back")}
            qmF2 = [qm0, qmFB]
            e_sb2 = [f1sb(f"e_sb{i}", [128, 512], F32) for i in (0, 1)]
            l_sb2 = [f1sb(f"l_sb{i}", [128, 512], F32) for i in (0, 1)]
            xqa2j = [f1sb(f"xqaj{i}", [128, JH], BF16) for i in (0, 1)]
            ps4 = [psA[0], psA[1], psC[0], psC[1]]


            xw = min(H, JW)

            def x1_half(tb, sub, qmf, j, hw):
                xqa = xqa2j[j]
                yield from g_quant(scrJ2[j], vjoint, xqa,
                                   xqa.t[:, :hw], hw, 'act', qm=qmf,
                                   so=j * hw, nbo=j * (hw // 32),
                                   maps=False)
                hc0 = sub * (xw // 128) + j * (hw // 128)
                for cb in range(0, hw // 128, 4):
                    n = min(4, hw // 128 - cb)
                    transpose_batch(
                        xqfm2, xqfm2.t, (hc0 + cb) * TQ + tb * 128, TQ,
                        xqa, xqa.t, cb * 128, n, cb // 4 + j)
                    yield

            ci = 0
            for tb in range(NTB):
                for sub in range(H // xw):
                    qmf = qmF2[ci % 2]
                    p.op("vector", lambda tb=tb, sub=sub:
                         nc.vector.tensor_copy(
                             vjoint.t[:, :xw],
                             x1.t[:, tb * H + sub * xw:
                                  tb * H + (sub + 1) * xw]),
                         reads=[x1], writes=[vjoint])
                    drain(g_maps(vjoint, xw, 'act', qmf))
                    hw = xw // 2 if xw >= 1024 else xw
                    if hw < xw:
                        drain_skew([x1_half(tb, sub, qmf, 0, hw),
                                    x1_half(tb, sub, qmf, 1, hw)], skew=3)
                    else:
                        drain(x1_half(tb, sub, qmf, 0, hw))
                    ci += 1

            n5 = IQ // 512

            def ld_w(row0, half):
                eng = "sync"
                for hci in range(HCH):
                    hc = half * HCH + hci
                    p.op(eng, lambda row0=row0, hc=hc, hci=hci, eng=eng,
                         wb=wb2[half]:
                         getattr(nc, eng).dma_start_transpose(
                             out=wb.t[:, hci * 512:(hci + 1) * 512],
                             in_=qfull_d[row0:row0 + 512,
                                         hc * 128:(hc + 1) * 128]),
                         reads=[qfull_b], writes=[wb2[half]], dma=True)

            def mm_half(tb, pp, half, start, stop):
                def f(tb=tb, pp=pp, half=half, start=start, stop=stop):
                    last = None
                    for hci in range(HCH):
                        hc = half * HCH + hci
                        last = nc.tensor.matmul(
                            pp.t[:, :512],
                            xqfm2.t[:, hc * TQ + tb * 128:
                                    hc * TQ + (tb + 1) * 128],
                            wb2[half].t[:, hci * 512:(hci + 1) * 512],
                            start=(start and hci == 0),
                            stop=(stop and hci == HCH - 1))
                    return last
                p.op("tensor", f, reads=[xqfm2, wb2[half]], writes=[pp])

            assert n5 == 1
            NI = NIQ

            def gate_phase(i512):
                def gate_chain(tb):
                    pp = ps4[tb % 4]
                    mm_half(tb, pp, 0, True, False)
                    yield
                    mm_half(tb, pp, 1, False, True)
                    yield
                    yield
                    e_sb = e_sb2[tb % 2]
                    l_sb = l_sb2[tb % 2]
                    p.op("scalar", lambda pp=pp, e_sb=e_sb:
                         nc.scalar.activation(
                             e_sb.t[:, :512], pp.t[:, :512], AF.Exp,
                             scale=cap(-1.0)),
                         reads=[pp, cpool], writes=[e_sb])
                    yield
                    p.op("scalar", lambda e_sb=e_sb, l_sb=l_sb:
                         nc.scalar.activation(
                             l_sb.t[:, :512], e_sb.t[:, :512], AF.Ln,
                             bias=cap(1.0)),
                         reads=[e_sb, cpool], writes=[l_sb])
                    yield
                    p.op("scalar", lambda e_sb=e_sb, l_sb=l_sb:
                         nc.scalar.activation(
                             e_sb.t[:, :512], l_sb.t[:, :512], AF.Exp,
                             scale=cap(-1.0)),
                         reads=[l_sb, cpool], writes=[e_sb])
                    yield
                    p.op("vector", lambda tb=tb, pp=pp, e_sb=e_sb:
                         nc.vector.tensor_mul(
                             s_sb.t[:, tb * 512:(tb + 1) * 512],
                             e_sb.t[:, :512], pp.t[:, :512]),
                         reads=[e_sb, pp], writes=[s_sb])
                for w0 in range(0, NTB, 4):
                    wg = [gate_chain(tb)
                          for tb in range(w0, min(w0 + 4, NTB))]
                    for c in wg:
                        next(c)
                    if w0 + 4 >= NTB:
                        ld_w(WU0 + i512 * 512, 0)
                    for c in wg:
                        next(c)
                    if w0 + 4 >= NTB:
                        ld_w(WU0 + i512 * 512, 1)
                    for j0 in range(0, len(wg), 2):
                        drain(*wg[j0:j0 + 2])

            def ld_wd(iq, oc, half):
                for ici in range(ICH):
                    ic = half * ICH + ici
                    col = iq * IQ + ic * 128
                    cc, col_in = col // H, col % H
                    row0 = WD0 + cc * H + oc * 512
                    p.op("sync", lambda row0=row0, col_in=col_in,
                         ici=ici, wdb=wd2[half]:
                         nc.sync.dma_start_transpose(
                             out=wdb.t[:, ici * 512:(ici + 1) * 512],
                             in_=qfull_d[row0:row0 + 512,
                                         col_in:col_in + 128]),
                         reads=[qfull_b], writes=[wd2[half]], dma=True)

            def up_phase(i512):
                def up_chain(tb):
                    j = tb % 4
                    pp = ps4[j]
                    mm_half(tb, pp, 0, True, False)
                    yield
                    mm_half(tb, pp, 1, False, True)
                    yield
                    yield
                    p.op("vector", lambda tb=tb, pp=pp, j=j:
                         nc.vector.tensor_mul(
                             vjoint.t[:, j * 512:(j + 1) * 512],
                             s_sb.t[:, tb * 512:(tb + 1) * 512],
                             pp.t[:, :512]),
                         reads=[s_sb, pp], writes=[vjoint])

                def half_q(wave, qmf, j, hw):
                    xqa = xqa2j[j]
                    yield from g_quant(scrJ2[j], vjoint, xqa,
                                       xqa.t[:, :hw], hw, 'act',
                                       qm=qmf, so=j * hw,
                                       nbo=j * (hw // 32), maps=False)
                    nt = hw // 512
                    for jj in range(nt):
                        tb = wave[j * nt + jj]
                        transpose_batch(interq, interq.t,
                                        tb * 128, TQ,
                                        xqa, xqa.t, jj * 512, 4,
                                        jj + j)
                        yield

                ni = i512 + 1
                for w0 in range(0, NTB, 4):
                    wave = range(w0, min(w0 + 4, NTB))
                    wu_ = [up_chain(tb) for tb in wave]
                    for c in wu_:
                        next(c)
                    if w0 + 4 >= NTB and ni < NI:
                        ld_w(ni * 512, 0)
                    for c in wu_:
                        next(c)
                    if w0 + 4 >= NTB and ni < NI:
                        ld_w(ni * 512, 1)
                    drain(*wu_)
                    qmf = qmF2[(w0 // 4) % 2]
                    wn = len(wave)
                    drain(g_maps(vjoint, wn * 512, 'act', qmf))
                    if wn >= 2:
                        hw = wn * 512 // 2
                        drain_skew([half_q(wave, qmf, 0, hw),
                                    half_q(wave, qmf, 1, hw)],
                                   skew=3)
                    else:
                        drain(half_q(wave, qmf, 0, wn * 512))
                ld_wd(i512, 0, 0)
                ld_wd(i512, 0, 1)

            def down_phase(iq):
                def down_chain(oc, tb):
                    pp = ps4[tb % 4]
                    for half in range(2):
                        def f(tb=tb, pp=pp, half=half):
                            last = None
                            for ici in range(ICH):
                                ic = half * ICH + ici
                                last = nc.tensor.matmul(
                                    pp.t[:, :512],
                                    interq.t[:, ic * TQ + tb * 128:
                                             ic * TQ + (tb + 1) * 128],
                                    wd2[half].t[:, ici * 512:
                                                (ici + 1) * 512],
                                    start=(half == 0 and ici == 0),
                                    stop=(half == 1 and ici == ICH - 1))
                            return last
                        p.op("tensor", f, reads=[interq, wd2[half]],
                             writes=[pp])
                        yield
                    dsl = x2acc.t[:, tb * H + oc * 512:
                                  tb * H + (oc + 1) * 512]
                    if iq == 0:
                        p.op("vector", lambda dsl=dsl, pp=pp:
                             nc.vector.tensor_copy(dsl, pp.t[:, :512]),
                             reads=[pp], writes=[x2acc])
                    else:
                        p.op("vector", lambda dsl=dsl, pp=pp:
                             nc.vector.tensor_add(dsl, dsl,
                                                  pp.t[:, :512]),
                             reads=[pp, x2acc], writes=[x2acc])
                for oc in range(H // 512):
                    for w0 in range(0, NTB, 4):
                        wd_ = [down_chain(oc, tb)
                               for tb in range(w0, min(w0 + 4, NTB))]
                        for c in wd_:
                            next(c)
                        if w0 + 4 >= NTB and oc + 1 < H // 512:
                            ld_wd(iq, oc + 1, 0)
                        for c in wd_:
                            next(c)
                        if w0 + 4 >= NTB and oc + 1 < H // 512:
                            ld_wd(iq, oc + 1, 1)
                        drain(*wd_)

            ld_w(0, 0)
            ld_w(0, 1)
            gate_phase(0)
            for i in range(NI):
                up_phase(i)
                if i + 1 < NI:
                    gate_phase(i + 1)
                down_phase(i)
            p.fence(exclude=("s_cc",))

        with contextlib.ExitStack() as f2st:
            def f2sb(name, shape, dt):
                return Buf(f2st.enter_context(
                    nc.sbuf_tensor(name, shape, dt)), name)
            x2f2 = [f2sb(f"x2f{i}", [128, H], F32) for i in (0, 1)]
            lnscr2 = [f2sb(f"lnscr{i}", [128, H], F32) for i in (0, 1)]
            oln2 = [f2sb(f"oln{i}", [128, H], F32) for i in (0, 1)]

            def fin_chain(tb):
                x2f = x2f2[tb % 2]
                lnscr = lnscr2[tb % 2]
                oln = oln2[tb % 2]
                p.op("vector", lambda tb=tb, x2f=x2f: nc.vector.tensor_add(
                    x2f.t[:, :H], x1.t[:, tb * H:(tb + 1) * H],
                    x2acc.t[:, tb * H:(tb + 1) * H]),
                    reads=[x1, x2acc], writes=[x2f])
                yield
                yield from g_layernorm(x2f, lnscr, oln, oln.t[:, :H],
                                       lnstat2[tb % 2])
                tok = p.op("sync", lambda tb=tb, oln=oln: nc.sync.dma_start(
                    out=out_d[tb * 128:(tb + 1) * 128, :], in_=oln.t[:, :H]),
                    reads=[oln], dma=True)
                st_toks.append(tok)
            for tb0 in range(0, NTB, 2):
                gens = [fin_chain(tb0)]
                if tb0 + 1 < NTB:
                    gens.append(fin_chain(tb0 + 1))
                drain(*gens)
            p.fence(exclude=("s_cc",))

    p.finish(st_toks)
    p.run_block()
    stack.close()
    return nc



_NC_CACHE = {}


def build_for(S, H, NH, NKV, I, n_cores=8):
    key = (S, H, NH, NKV, I, n_cores)
    if key not in _NC_CACHE:
        _NC_CACHE[key] = build_program(S, H, NH, NKV, I, n_cores=n_cores)
    return _NC_CACHE[key]


def weight_shards(wg, wu, wd, H, n_cores):
    I = wg.shape[0]
    tiles = [wg, wu]
    for cc in range(I // H):
        tiles.append(np.ascontiguousarray(wd[:, cc * H:(cc + 1) * H]))
    full = np.concatenate(tiles, axis=0)
    rows_pc = full.shape[0] // n_cores
    return [np.ascontiguousarray(full[c * rows_pc:(c + 1) * rows_pc])
            for c in range(n_cores)]


def kernel(L0_attn_x, L0_attn_Wq, L0_attn_Wk, L0_attn_Wv, L0_attn_Wo,
           L0_attn_gq, L0_attn_gk, L0_ffn_w_gate, L0_ffn_w_up, L0_ffn_w_down):
    x = np.ascontiguousarray(L0_attn_x, dtype=np.float32)
    B, S, H = x.shape
    NB = S // 128
    I = L0_ffn_w_gate.shape[0]
    n_cores = 2 * B
    nc = build_for(S, H, 16, 4, I, n_cores)
    shards = weight_shards(
        np.ascontiguousarray(L0_ffn_w_gate, dtype=np.float32),
        np.ascontiguousarray(L0_ffn_w_up, dtype=np.float32),
        np.ascontiguousarray(L0_ffn_w_down, dtype=np.float32), H, n_cores)
    in_maps = []
    metas = []
    for c in range(n_cores):
        b, par = c // 2, c % 2
        js = qblocks_for_parity(NB, par)
        rows = np.concatenate([np.arange(j * 128, (j + 1) * 128)
                               for j in js])
        in_maps.append({
            "x": x[b],
            "xqrows": np.ascontiguousarray(x[b][rows]),
            "qpos": np.ascontiguousarray(rows.astype(np.float32)[None, :]),
            "wq": np.ascontiguousarray(L0_attn_Wq, dtype=np.float32),
            "wk": np.ascontiguousarray(L0_attn_Wk, dtype=np.float32),
            "wv": np.ascontiguousarray(L0_attn_Wv, dtype=np.float32),
            "wo": np.ascontiguousarray(L0_attn_Wo, dtype=np.float32),
            "gq": np.ascontiguousarray(L0_attn_gq, dtype=np.float32)[None, :],
            "gk": np.ascontiguousarray(L0_attn_gk, dtype=np.float32)[None, :],
            "wshard": shards[c],
        })
        metas.append((b, js))
    res = run_bass_kernel_spmd(nc, in_maps, core_ids=list(range(len(in_maps))))
    out = np.empty((B, S, H), dtype=np.float32)
    for c, r in enumerate(res.results):
        b, js = metas[c]
        o = np.asarray(r["out"])
        for qi, j in enumerate(js):
            out[b, j * 128:(j + 1) * 128] = o[qi * 128:(qi + 1) * 128]
    return out


# version 50 (final)
# speedup vs baseline: 1.0733x; 1.0041x over previous
import contextlib
import numpy as np
import concourse.bass as bass
import concourse.mybir as mybir
from concourse.bass_utils import run_bass_kernel_spmd

F32 = mybir.dt.float32
BF16 = mybir.dt.bfloat16
AF = mybir.ActivationFunctionType
OP = mybir.AluOpType
AX = mybir.AxisListType
LN2 = float(np.log(2.0))

SKIP_SELF_WAITS = False
ENGINES = ("sync", "scalar", "vector", "tensor", "gpsimd")
ESEM = {"sync": "s_ld", "scalar": "s_sc", "vector": "s_ve",
        "tensor": "s_pe", "gpsimd": "s_po"}
EDSEM = {"sync": "s_ld", "scalar": "s_scd", "gpsimd": "s_pod"}


class Buf:
    __slots__ = ("t", "w", "rs", "name")

    def __init__(self, t, name=""):
        self.t = t
        self.w = None
        self.rs = []
        self.name = name


class Prog:
    NDMA = 32

    def __init__(self, nc):
        self.nc = nc
        self.plans = {e: [] for e in ENGINES}
        self.sems = {}
        self.lastw = {e: {} for e in ENGINES}
        self.fence_toks = []
        self.n_ops = 0
        self.dma_rr = {"hw": 0, "sw": 0}

    def next_dma_sem(self, kind):
        name = f"d{kind}{self.dma_rr[kind] % self.NDMA}"
        self.dma_rr[kind] += 1
        return name

    def sem(self, name):
        if name not in self.sems:
            self.sems[name] = [self.nc.alloc_semaphore(name), 0]
        return self.sems[name]

    def fence(self, exclude=()):
        self.fence_toks = [(n, rec[1]) for n, rec in self.sems.items()
                           if rec[1] > 0 and n not in exclude]

    def op(self, eng, fn, reads=(), writes=(), dma=False, cc=False,
           extra_waits=()):
        if cc:
            sname = "s_cc"
            inc = 1
        elif dma:
            sname = self.next_dma_sem("sw" if eng == "gpsimd" else "hw")
            inc = 16
        else:
            sname = ESEM[eng]
            inc = 1
        rec = self.sem(sname)
        waits = {}

        def addw(tok):
            if tok is None:
                return
            s, v = tok
            if waits.get(s, 0) < v:
                waits[s] = v
        for b in reads:
            addw(b.w)
        for b in writes:
            addw(b.w)
            for t in b.rs:
                addw(t)
        for t in extra_waits:
            addw(t)
        for t in self.fence_toks:
            addw(t)
        rec[1] += inc
        mytok = (sname, rec[1])
        for b in writes:
            b.w = mytok
            b.rs = []
        for b in reads:
            b.rs.append(mytok)
        if SKIP_SELF_WAITS:
            own = ESEM[eng]
            waits.pop(own, None)
        wl = sorted(waits.items())
        prog = self

        def emit(wl=wl, fn=fn, sname=sname, inc=inc, eng=eng):
            e = getattr(prog.nc, eng)
            for s, v in wl:
                if prog.lastw[eng].get(s, 0) < v:
                    e.wait_ge(prog.sems[s][0], v)
                    prog.lastw[eng][s] = v
            inst = fn()
            inst.then_inc(prog.sems[sname][0], inc)
        self.plans[eng].append(emit)
        self.n_ops += 1
        return mytok

    def gpdma_group(self, items):
        sname = self.next_dma_sem("sw")
        rec = self.sem(sname)
        waits = {}

        def addw(tok):
            if tok is None:
                return
            sx, v = tok
            if waits.get(sx, 0) < v:
                waits[sx] = v
        allw = []
        for fn, writes in items:
            for b in writes:
                addw(b.w)
                for t in b.rs:
                    addw(t)
                allw.append(b)
        for t in self.fence_toks:
            addw(t)
        rec[1] += 16 * len(items)
        otok = (sname, rec[1])
        for b in allw:
            b.w = otok
            b.rs = []
        wl = sorted(waits.items())
        prog = self

        def emit(wl=wl, items=items, sname=sname):
            e = prog.nc.gpsimd
            for sx, v in wl:
                if prog.lastw["gpsimd"].get(sx, 0) < v:
                    e.wait_ge(prog.sems[sx][0], v)
                    prog.lastw["gpsimd"][sx] = v
            for fn, _w in items:
                fn().then_inc(prog.sems[sname][0], 16)
        self.plans["gpsimd"].append(emit)
        self.n_ops += len(items)
        return otok

    def finish(self, final_tokens):
        waits = {}
        for s, v in final_tokens:
            waits[s] = max(waits.get(s, 0), v)

        def emit():
            for s, v in sorted(waits.items()):
                self.nc.sync.wait_ge(self.sems[s][0], v)
        self.plans["sync"].append(emit)

    def run_block(self):
        nc = self.nc
        with nc.Block() as block:
            @block.sync
            def _(_e):
                for f in self.plans["sync"]:
                    f()

            @block.scalar
            def _(_e):
                for f in self.plans["scalar"]:
                    f()

            @block.vector
            def _(_e):
                for f in self.plans["vector"]:
                    f()

            @block.tensor
            def _(_e):
                for f in self.plans["tensor"]:
                    f()

            @block.gpsimd
            def _(_e):
                for f in self.plans["gpsimd"]:
                    f()


def qblocks_for_parity(NB, parity):
    out = []
    for k in range(NB // 2):
        if (k % 2) == parity:
            out.append(k)
            out.append(NB - 1 - k)
    return sorted(out, key=lambda j: ((j + 2) // 2, j))


def even_klen(j):
    return 128 * (((j + 1) + 1) // 2 * 2)


def build_program(S, H, NH, NKV, I, n_cores=8, eps_rms=1e-6, eps_ln=1e-5):
    D = 128
    HC = H // 128
    NB = S // 128
    NQ = NB // 2
    TQ = NQ * 128
    NTB = TQ // 128
    GR = NH // NKV
    KVW = NKV * D
    OCQ = (NH * D) // 512
    IQ = min(512, I)
    NIQ = I // IQ

    NT_WG = I // 128
    NT_WD = (I // H) * (H // 128)
    TOT_T = 2 * NT_WG + NT_WD
    TPC = TOT_T // n_cores
    assert TPC * n_cores == TOT_T and TPC % 2 == 0
    WU0 = NT_WG * 128
    WD0 = 2 * NT_WG * 128

    KLENS = [even_klen(j) for j in qblocks_for_parity(NB, 0)]
    assert KLENS == [even_klen(j) for j in qblocks_for_parity(NB, 1)]

    nc = bass.Bass("TRN2", target_bir_lowering=False,
                   detect_race_conditions=not SKIP_SELF_WAITS)
    p = Prog(nc)

    x_d = nc.dram_tensor("x", [S, H], F32, kind="ExternalInput").ap()
    xq_d = nc.dram_tensor("xqrows", [TQ, H], F32, kind="ExternalInput").ap()
    qpos_d = nc.dram_tensor("qpos", [1, TQ], F32, kind="ExternalInput").ap()
    wq_d = nc.dram_tensor("wq", [H, NH * D], F32, kind="ExternalInput").ap()
    wk_d = nc.dram_tensor("wk", [H, KVW], F32, kind="ExternalInput").ap()
    wv_d = nc.dram_tensor("wv", [H, KVW], F32, kind="ExternalInput").ap()
    wo_d = nc.dram_tensor("wo", [NH * D, H], F32, kind="ExternalInput").ap()
    gq_d = nc.dram_tensor("gq", [1, D], F32, kind="ExternalInput").ap()
    gk_d = nc.dram_tensor("gk", [1, D], F32, kind="ExternalInput").ap()
    wsh_d = nc.dram_tensor("wshard", [TPC * 128, H], F32,
                           kind="ExternalInput").ap()
    out_d = nc.dram_tensor("out", [TQ, H], F32, kind="ExternalOutput").ap()
    qsh_d = nc.dram_tensor("qshard", [TPC * 128, H], BF16,
                           kind="Internal").ap()
    qfull_d = nc.dram_tensor("qfull", [TOT_T * 128, H], BF16,
                             kind="Internal").ap()
    qsh_b = Buf(None, "qshard")
    qfull_b = Buf(None, "qfull")

    stack = contextlib.ExitStack()

    def sb(name, shape, dt, st=None):
        t = (st or stack).enter_context(nc.sbuf_tensor(name, shape, dt))
        return Buf(t, name)

    def ps(name, shape, dt):
        t = stack.enter_context(nc.psum_tensor(name, shape, dt))
        return Buf(t, name)

    cpool = sb("cpool", [128, 16], F32)
    CVALS = {}

    def cap(val):
        val = float(val)
        if val not in CVALS:
            CVALS[val] = len(CVALS)
            assert len(CVALS) <= 16
        return cpool.t[:, CVALS[val]:CVALS[val] + 1]

    ident = sb("ident", [128, 128], BF16)
    gq_b = sb("gq_t", [128, D], F32)
    gk_b = sb("gk_t", [128, D], F32)
    qpos_b = sb("qpos_t", [128, NQ], F32)
    x1 = sb("x1_all", [128, NQ * H], BF16)
    lnstat = sb("lnstat", [128, 8], F32)
    lnstat1 = sb("lnstat1", [128, 8], F32)
    lnstat2 = [lnstat, lnstat1]
    qm0 = {n: sb("qm_" + n, [128, 128], F32) for n in
           ("amax", "z", "rem", "fl", "invs", "back")}

    psA = [ps(f"psA{i}", [128, 512], F32) for i in (0, 1)]
    psB = [ps(f"psB{i}", [128, 512], F32) for i in (0, 1)]
    psT = [ps(f"psT{i}", [128, 512], BF16) for i in (0, 1)]
    psC = [ps(f"psC{i}", [128, 512], F32) for i in (0, 1)]

    def transpose_to(dst_buf, dst_ap, src_buf, src_ap, slot):
        tp = psT[slot % 2]
        p.op("tensor", lambda tp=tp, src_ap=src_ap: nc.tensor.transpose(
            tp.t[:, 0:128], src_ap, ident.t[:, :]),
            reads=[src_buf, ident], writes=[tp])
        p.op("vector", lambda tp=tp, dst_ap=dst_ap: nc.vector.tensor_copy(
            dst_ap, tp.t[:, 0:128]), reads=[tp], writes=[dst_buf])

    def ap3(t, off, stride, n):
        F = int(t.shape[1])
        return bass.AP(t, off, [[F, 128], [stride, n], [1, 128]])

    def transpose_batch(dst_buf, dst_t, dst_off, dst_stride, src_buf, src_t,
                        src_off, n, slot, src_stride=128):
        tp = psT[slot % 2]

        def f_tr(tp=tp, n=n, src_t=src_t, src_off=src_off,
                 src_stride=src_stride):
            last = None
            for i in range(n):
                Fs = int(src_t.shape[1])
                sap = bass.AP(src_t, src_off + i * src_stride,
                              [[Fs, 128], [1, 128]])
                last = nc.tensor.transpose(
                    tp.t[:, i * 128:(i + 1) * 128], sap, ident.t[:, :])
            return last
        p.op("tensor", f_tr, reads=[src_buf, ident], writes=[tp])

        def f_ev(tp=tp, n=n, dst_t=dst_t, dst_off=dst_off,
                 dst_stride=dst_stride):
            return nc.vector.tensor_copy(
                ap3(dst_t, dst_off, dst_stride, n), tp.t[:, :n * 128])
        p.op("vector", f_ev, reads=[tp], writes=[dst_buf])

    def g_maps(src, F, kind, qm):
        nb = F // 32
        src3 = src.t[:, :F].rearrange("p (n b) -> p n b", b=32)
        bias_i = cap(((2 if kind == 'w' else 5) + 64) * LN2)
        bias_b = cap(-((3 if kind == 'w' else 5) + 64) * LN2)
        am, z, rem, fl = qm["amax"], qm["z"], qm["rem"], qm["fl"]
        invs, back = qm["invs"], qm["back"]
        yield p.op("vector", lambda: nc.vector.tensor_reduce(
            am.t[:, :nb], src3, AX.X, OP.max, apply_absolute_value=True),
            reads=[src], writes=[am])
        yield p.op("vector", lambda: nc.vector.tensor_scalar_max(
            am.t[:, :nb], am.t[:, :nb], float(2.0 ** -60)),
            reads=[am], writes=[am])
        yield p.op("scalar", lambda: nc.scalar.activation(
            z.t[:, :nb], am.t[:, :nb], AF.Ln),
            reads=[am, cpool], writes=[z])
        yield p.op("vector", lambda: nc.vector.tensor_scalar(
            rem.t[:, :nb], z.t[:, :nb], 1.0 / LN2, 64.0, OP.mult, OP.add),
            reads=[z], writes=[rem])
        yield p.op("vector", lambda: nc.vector.tensor_copy(
            z.t[:, :nb], rem.t[:, :nb]), reads=[rem], writes=[z])
        for k in (64.0, 32.0, 16.0, 8.0, 4.0, 2.0, 1.0):
            yield p.op("vector", lambda k=k: nc.vector.tensor_single_scalar(
                fl.t[:, :nb], rem.t[:, :nb], k, OP.is_ge),
                reads=[rem], writes=[fl])
            yield p.op("vector", lambda k=k: nc.vector.scalar_tensor_tensor(
                rem.t[:, :nb], fl.t[:, :nb], -k, rem.t[:, :nb],
                OP.mult, OP.add), reads=[fl, rem], writes=[rem])
        yield p.op("vector", lambda: nc.vector.tensor_sub(
            fl.t[:, :nb], z.t[:, :nb], rem.t[:, :nb]),
            reads=[z, rem], writes=[fl])
        yield p.op("scalar", lambda: nc.scalar.activation(
            invs.t[:, :nb], fl.t[:, :nb], AF.Exp,
            scale=cap(-LN2), bias=bias_i), reads=[fl, cpool], writes=[invs])
        yield p.op("scalar", lambda: nc.scalar.activation(
            back.t[:, :nb], fl.t[:, :nb], AF.Exp,
            scale=cap(LN2), bias=bias_b), reads=[fl, cpool], writes=[back])

    def g_quant(scr, src, out_buf, out_ap, F, kind, qm=None,
                so=0, nbo=0, maps=True):
        if qm is None:
            qm = qm0
        nb = F // 32
        if maps:
            yield from g_maps(src, F, kind, qm)
        ibc = qm["invs"].t[:, nbo:nbo + nb].broadcast_to([128, nb, 32])
        bbc = qm["back"].t[:, nbo:nbo + nb].broadcast_to([128, nb, 32])
        s3 = src.t[:, so:so + F].rearrange("p (n b) -> p n b", b=32)
        y, u, a, b, c, d = (scr[k] for k in "yuabcd")
        y3 = y.t[:, :F].rearrange("p (n b) -> p n b", b=32)
        o3 = out_ap.rearrange("p (n b) -> p n b", b=32)
        d3 = d.t[:, :F].rearrange("p (n b) -> p n b", b=32)
        V = "vector"
        yield p.op(V, lambda: nc.vector.tensor_tensor(y3, s3, ibc, OP.mult),
             reads=[src, qm["invs"]], writes=[y])
        if kind == 'w':
            yield p.op("scalar", lambda: nc.scalar.activation(
                u.t[:, :F], y.t[:, :F], AF.Abs),
                reads=[y, cpool], writes=[u])
            yield p.op(V, lambda: nc.vector.tensor_single_scalar(
                a.t[:, :F], u.t[:, :F], 0.25, OP.is_ge),
                reads=[u], writes=[a])
            for t in (0.75, 1.25, 1.75):
                yield p.op(V, lambda t=t: nc.vector.scalar_tensor_tensor(
                    a.t[:, :F], u.t[:, :F], t, a.t[:, :F], OP.is_ge, OP.add),
                    reads=[u, a], writes=[a])
            yield p.op(V, lambda: nc.vector.tensor_single_scalar(
                b.t[:, :F], u.t[:, :F], 2.5, OP.is_ge),
                reads=[u], writes=[b])
            yield p.op(V, lambda: nc.vector.scalar_tensor_tensor(
                b.t[:, :F], u.t[:, :F], 3.5, b.t[:, :F], OP.is_ge, OP.add),
                reads=[u, b], writes=[b])
            yield p.op(V, lambda: nc.vector.tensor_single_scalar(
                c.t[:, :F], u.t[:, :F], 5.0, OP.is_ge),
                reads=[u], writes=[c])
            yield p.op(V, lambda: nc.vector.scalar_tensor_tensor(
                a.t[:, :F], b.t[:, :F], 2.0, a.t[:, :F], OP.mult, OP.add),
                reads=[b, a], writes=[a])
            yield p.op(V, lambda: nc.vector.scalar_tensor_tensor(
                a.t[:, :F], c.t[:, :F], 4.0, a.t[:, :F], OP.mult, OP.add),
                reads=[c, a], writes=[a])
            qmag = a
        else:
            yield p.op("scalar", lambda: nc.scalar.activation(
                u.t[:, :F], y.t[:, :F], AF.Abs),
                reads=[y, cpool], writes=[u])
            yield p.op(V, lambda: nc.vector.tensor_single_scalar(
                u.t[:, :F], u.t[:, :F], 61.9, OP.min),
                reads=[u], writes=[u])
            yield p.op(V, lambda: nc.vector.tensor_single_scalar(
                a.t[:, :F], u.t[:, :F], 16.0, OP.is_ge),
                reads=[u], writes=[a])
            yield p.op(V, lambda: nc.vector.tensor_single_scalar(
                b.t[:, :F], u.t[:, :F], 32.0, OP.is_ge),
                reads=[u], writes=[b])
            yield p.op(V, lambda: nc.vector.tensor_scalar(
                c.t[:, :F], a.t[:, :F], -0.5, 1.0, OP.mult, OP.add),
                reads=[a], writes=[c])
            yield p.op(V, lambda: nc.vector.scalar_tensor_tensor(
                c.t[:, :F], b.t[:, :F], -0.25, c.t[:, :F], OP.mult, OP.add),
                reads=[b, c], writes=[c])
            yield p.op(V, lambda: nc.vector.scalar_tensor_tensor(
                b.t[:, :F], b.t[:, :F], 2.0, a.t[:, :F], OP.mult, OP.add),
                reads=[a, b], writes=[b])
            yield p.op(V, lambda: nc.vector.tensor_mul(
                d.t[:, :F], c.t[:, :F], u.t[:, :F]),
                reads=[c, u], writes=[d])
            yield p.op(V, lambda: nc.vector.tensor_single_scalar(
                u.t[:, :F], d.t[:, :F], 8.0, OP.is_ge),
                reads=[d], writes=[u])
            yield p.op(V, lambda: nc.vector.scalar_tensor_tensor(
                d.t[:, :F], u.t[:, :F], -8.0, d.t[:, :F], OP.mult, OP.add),
                reads=[u, d], writes=[d])
            yield p.op(V, lambda: nc.vector.tensor_single_scalar(
                c.t[:, :F], d.t[:, :F], 4.0, OP.is_ge),
                reads=[d], writes=[c])
            yield p.op(V, lambda: nc.vector.scalar_tensor_tensor(
                d.t[:, :F], c.t[:, :F], -4.0, d.t[:, :F], OP.mult, OP.add),
                reads=[c, d], writes=[d])
            yield p.op(V, lambda: nc.vector.tensor_single_scalar(
                a.t[:, :F], d.t[:, :F], 0.5, OP.is_ge),
                reads=[d], writes=[a])
            for t in (1.5, 2.5, 3.5):
                yield p.op(V, lambda t=t: nc.vector.scalar_tensor_tensor(
                    a.t[:, :F], d.t[:, :F], t, a.t[:, :F], OP.is_ge, OP.add),
                    reads=[d, a], writes=[a])
            yield p.op(V, lambda: nc.vector.scalar_tensor_tensor(
                a.t[:, :F], c.t[:, :F], 4.0, a.t[:, :F], OP.mult, OP.add),
                reads=[c, a], writes=[a])
            yield p.op(V, lambda: nc.vector.scalar_tensor_tensor(
                a.t[:, :F], u.t[:, :F], 8.0, a.t[:, :F], OP.mult, OP.add),
                reads=[u, a], writes=[a])
            yield p.op(V, lambda: nc.vector.scalar_tensor_tensor(
                a.t[:, :F], b.t[:, :F], 1.0, a.t[:, :F], OP.add, OP.mult),
                reads=[b, a], writes=[a])
            qmag = a
        yield p.op("scalar", lambda: nc.scalar.activation(
            c.t[:, :F], y.t[:, :F], AF.Sign), reads=[y, cpool], writes=[c])
        yield p.op(V, lambda qmag=qmag: nc.vector.tensor_mul(
            d.t[:, :F], qmag.t[:, :F], c.t[:, :F]),
            reads=[qmag, c], writes=[d])
        yield p.op(V, lambda: nc.vector.tensor_tensor(o3, d3, bbc, OP.mult),
             reads=[d, qm["back"]], writes=[out_buf])

    def drain(*gens):
        gens = list(gens)
        while gens:
            keep = []
            for g in gens:
                try:
                    next(g)
                    keep.append(g)
                except StopIteration:
                    pass
            gens = keep

    def drain_skew(gens, skew=2):
        gens = list(gens)
        n = len(gens)
        live = []
        for j, g in enumerate(gens):
            ok = True
            for _ in range((n - 1 - j) * skew):
                try:
                    next(g)
                except StopIteration:
                    ok = False
                    break
            if ok:
                live.append(g)
        drain(*live)

    def emit_maps(src, F, kind, qm):
        drain(g_maps(src, F, kind, qm))

    def emit_quant(scr, src, out_buf, out_ap, F, kind, qm=None):
        drain(g_quant(scr, src, out_buf, out_ap, F, kind, qm=qm))

    def emit_quant_pool(scr, src, out_buf, out_ap, F, qm):
        nb = F // 32
        emit_maps(src, F, 'act', qm)
        ibc = qm["invs"].t[:, :nb].broadcast_to([128, nb, 32])
        bbc = qm["back"].t[:, :nb].broadcast_to([128, nb, 32])
        s3 = src.t[:, :F].rearrange("p (n b) -> p n b", b=32)
        y, u, a, b, c, d, r, xx = (scr[k] for k in "yuabcdrx")
        y3 = y.t[:, :F].rearrange("p (n b) -> p n b", b=32)
        o3 = out_ap.rearrange("p (n b) -> p n b", b=32)
        r3 = r.t[:, :F].rearrange("p (n b) -> p n b", b=32)
        G = "gpsimd"

        def po(fn, reads, writes):
            p.op(G, fn, reads=reads, writes=writes)
        po(lambda: nc.gpsimd.tensor_tensor(y3, s3, ibc, OP.mult),
           [src, qm["invs"]], [y])
        po(lambda: nc.gpsimd.tensor_single_scalar(
            u.t[:, :F], y.t[:, :F], 0.0, OP.is_ge), [y], [u])
        po(lambda: nc.gpsimd.tensor_scalar(
            u.t[:, :F], u.t[:, :F], 2.0, -1.0, OP.mult, OP.add), [u], [u])
        po(lambda: nc.gpsimd.tensor_tensor(
            u.t[:, :F], y.t[:, :F], u.t[:, :F], OP.mult), [y, u], [u])
        po(lambda: nc.gpsimd.tensor_single_scalar(
            u.t[:, :F], u.t[:, :F], 61.9, OP.min), [u], [u])
        po(lambda: nc.gpsimd.tensor_single_scalar(
            a.t[:, :F], u.t[:, :F], 16.0, OP.is_ge), [u], [a])
        po(lambda: nc.gpsimd.tensor_single_scalar(
            b.t[:, :F], u.t[:, :F], 32.0, OP.is_ge), [u], [b])
        po(lambda: nc.gpsimd.tensor_scalar(
            c.t[:, :F], a.t[:, :F], -0.5, 1.0, OP.mult, OP.add), [a], [c])
        po(lambda: nc.gpsimd.tensor_single_scalar(
            d.t[:, :F], b.t[:, :F], -0.25, OP.mult), [b], [d])
        po(lambda: nc.gpsimd.tensor_tensor(
            c.t[:, :F], c.t[:, :F], d.t[:, :F], OP.add), [c, d], [c])
        po(lambda: nc.gpsimd.tensor_tensor(
            d.t[:, :F], c.t[:, :F], u.t[:, :F], OP.mult), [c, u], [d])
        po(lambda: nc.gpsimd.tensor_single_scalar(
            xx.t[:, :F], d.t[:, :F], 8.0, OP.is_ge), [d], [xx])
        po(lambda: nc.gpsimd.tensor_single_scalar(
            c.t[:, :F], xx.t[:, :F], -8.0, OP.mult), [xx], [c])
        po(lambda: nc.gpsimd.tensor_tensor(
            d.t[:, :F], d.t[:, :F], c.t[:, :F], OP.add), [d, c], [d])
        po(lambda: nc.gpsimd.tensor_single_scalar(
            c.t[:, :F], d.t[:, :F], 4.0, OP.is_ge), [d], [c])
        po(lambda: nc.gpsimd.tensor_single_scalar(
            u.t[:, :F], c.t[:, :F], -4.0, OP.mult), [c], [u])
        po(lambda: nc.gpsimd.tensor_tensor(
            d.t[:, :F], d.t[:, :F], u.t[:, :F], OP.add), [d, u], [d])
        po(lambda: nc.gpsimd.tensor_single_scalar(
            r.t[:, :F], d.t[:, :F], 0.5, OP.is_ge), [d], [r])
        for t in (1.5, 2.5, 3.5):
            po(lambda t=t: nc.gpsimd.tensor_single_scalar(
                u.t[:, :F], d.t[:, :F], t, OP.is_ge), [d], [u])
            po(lambda: nc.gpsimd.tensor_tensor(
                r.t[:, :F], r.t[:, :F], u.t[:, :F], OP.add), [r, u], [r])
        po(lambda: nc.gpsimd.tensor_single_scalar(
            u.t[:, :F], c.t[:, :F], 4.0, OP.mult), [c], [u])
        po(lambda: nc.gpsimd.tensor_tensor(
            r.t[:, :F], r.t[:, :F], u.t[:, :F], OP.add), [r, u], [r])
        po(lambda: nc.gpsimd.tensor_single_scalar(
            u.t[:, :F], xx.t[:, :F], 8.0, OP.mult), [xx], [u])
        po(lambda: nc.gpsimd.tensor_tensor(
            r.t[:, :F], r.t[:, :F], u.t[:, :F], OP.add), [r, u], [r])
        po(lambda: nc.gpsimd.tensor_single_scalar(
            u.t[:, :F], b.t[:, :F], 2.0, OP.mult), [b], [u])
        po(lambda: nc.gpsimd.tensor_tensor(
            u.t[:, :F], u.t[:, :F], a.t[:, :F], OP.add), [u, a], [u])
        po(lambda: nc.gpsimd.tensor_single_scalar(
            u.t[:, :F], u.t[:, :F], 1.0, OP.add), [u], [u])
        po(lambda: nc.gpsimd.tensor_tensor(
            r.t[:, :F], r.t[:, :F], u.t[:, :F], OP.mult), [r, u], [r])
        po(lambda: nc.gpsimd.tensor_single_scalar(
            u.t[:, :F], y.t[:, :F], 0.0, OP.is_ge), [y], [u])
        po(lambda: nc.gpsimd.tensor_scalar(
            u.t[:, :F], u.t[:, :F], 2.0, -1.0, OP.mult, OP.add), [u], [u])
        po(lambda: nc.gpsimd.tensor_tensor(
            r.t[:, :F], r.t[:, :F], u.t[:, :F], OP.mult), [r, u], [r])
        po(lambda: nc.gpsimd.tensor_tensor(o3, r3, bbc, OP.mult),
           [r, qm["back"]], [out_buf])

    def g_layernorm(src, scratch, out_buf, out_ap, ls):
        yield p.op("vector", lambda: nc.vector.reduce_sum(
            ls.t[:, 0:1], src.t[:, :H], AX.X), reads=[src], writes=[ls])
        yield p.op("scalar", lambda: nc.scalar.activation(
            scratch.t[:, :H], src.t[:, :H], AF.Square,
            accum_out=ls.t[:, 1:2]),
            reads=[src, cpool], writes=[scratch, ls])
        yield p.op("vector", lambda: nc.vector.tensor_single_scalar(
            ls.t[:, 0:1], ls.t[:, 0:1], 1.0 / H, OP.mult),
            reads=[ls], writes=[ls])
        yield p.op("vector", lambda: nc.vector.tensor_mul(
            ls.t[:, 2:3], ls.t[:, 0:1], ls.t[:, 0:1]),
            reads=[ls], writes=[ls])
        yield p.op("vector", lambda: nc.vector.scalar_tensor_tensor(
            ls.t[:, 1:2], ls.t[:, 1:2], 1.0 / H, ls.t[:, 2:3],
            OP.mult, OP.subtract), reads=[ls], writes=[ls])
        yield p.op("scalar", lambda: nc.scalar.activation(
            ls.t[:, 3:4], ls.t[:, 1:2], AF.Ln, bias=cap(eps_ln)),
            reads=[ls, cpool], writes=[ls])
        yield p.op("scalar", lambda: nc.scalar.activation(
            ls.t[:, 3:4], ls.t[:, 3:4], AF.Exp, scale=cap(-0.5)),
            reads=[ls, cpool], writes=[ls])
        yield p.op("vector", lambda: nc.vector.scalar_tensor_tensor(
            ls.t[:, 4:5], ls.t[:, 0:1], -1.0, ls.t[:, 3:4],
            OP.mult, OP.mult), reads=[ls], writes=[ls])
        yield p.op("scalar", lambda out_ap=out_ap: nc.scalar.activation(
            out_ap, src.t[:, :H], AF.Identity,
            scale=ls.t[:, 3:4], bias=ls.t[:, 4:5]),
            reads=[src, ls, cpool], writes=[out_buf])

    def emit_layernorm(src, scratch, out_buf, out_ap):
        drain(g_layernorm(src, scratch, out_buf, out_ap, lnstat))

    for v in [-LN2, LN2, eps_ln, eps_rms, -0.5, -1.0, 1.0,
              (2 + 64) * LN2, -(3 + 64) * LN2,
              (5 + 64) * LN2, -(5 + 64) * LN2, 1.0 / D]:
        cap(v)

    def f_consts():
        last = None
        for v, col in sorted(CVALS.items(), key=lambda t: t[1]):
            last = nc.gpsimd.memset(cpool.t[:, col:col + 1], float(v))
        return last
    p.op("gpsimd", f_consts, writes=[cpool])

    p.op("gpsimd", lambda: nc.gpsimd.memset(ident.t[:, :], 1.0),
         writes=[ident])
    p.op("gpsimd", lambda: nc.gpsimd.affine_select(
        out=ident.t[:, :], in_=ident.t[:, :], compare_op=OP.is_equal,
        fill=0.0, base=0, pattern=[[-1, 128]], channel_multiplier=1),
        reads=[ident], writes=[ident])

    gq_src = bass.AP(gq_d.tensor, 0, [[0, 128], [1, D]])
    gk_src = bass.AP(gk_d.tensor, 0, [[0, 128], [1, D]])
    p.op("gpsimd", lambda: nc.gpsimd.dma_start(out=gq_b.t[:, :], in_=gq_src),
         writes=[gq_b], dma=True)
    p.op("gpsimd", lambda: nc.gpsimd.dma_start(out=gk_b.t[:, :], in_=gk_src),
         writes=[gk_b], dma=True)
    p.op("vector", lambda: nc.vector.tensor_scalar_mul(
        gq_b.t[:, :], gq_b.t[:, :], float(D) ** -0.5),
        reads=[gq_b], writes=[gq_b])
    for q in range(NQ):
        src = bass.AP(qpos_d.tensor, q * 128, [[1, 128], [128 * NQ, 1]])
        p.op("gpsimd", lambda q=q, src=src: nc.gpsimd.dma_start(
            out=qpos_b.t[:, q:q + 1], in_=src), writes=[qpos_b], dma=True)

    with contextlib.ExitStack() as wst:
        def wsb(name, shape, dt):
            t = wst.enter_context(nc.sbuf_tensor(name, shape, dt))
            return Buf(t, name)
        WW = min(1024, H)
        NWC = H // WW
        NWS = 4
        scrW4 = [{k: wsb(f"scrW{i}_" + k, [128, WW], F32)
                  for k in "yuabcd"} for i in range(NWS)]
        win = [wsb(f"win{i}", [128, WW], F32) for i in range(NWS)]
        wout = [wsb(f"wout{i}", [128, WW], BF16) for i in range(NWS)]
        qmW4 = [qm0] + [
            {n: wsb(f"qmW{i}_" + n, [128, max(32, WW // 32)], F32) for n in
             ("amax", "z", "rem", "fl", "invs", "back")}
            for i in (1, 2, 3)]

        def w_chain(ci):
            rr, hh = divmod(ci, NWC)
            j = ci % NWS
            bi, bo = win[j], wout[j]
            p.op("sync", lambda rr=rr, hh=hh, bi=bi:
                 nc.sync.dma_start(
                     out=bi.t[:, :WW],
                     in_=wsh_d[rr * 128:(rr + 1) * 128,
                               hh * WW:(hh + 1) * WW]),
                 writes=[bi], dma=True)
            yield
            yield from g_quant(scrW4[j], bi, bo, bo.t[:, :WW], WW,
                               'w', qm=qmW4[j])
            p.op("sync", lambda rr=rr, hh=hh, bo=bo:
                 nc.sync.dma_start(
                     out=qsh_d[rr * 128:(rr + 1) * 128,
                               hh * WW:(hh + 1) * WW],
                     in_=bo.t[:, :WW]),
                 reads=[bo], writes=[qsh_b], dma=True)
        NWT = TPC * NWC
        for c0 in range(0, NWT, NWS):
            drain_skew([w_chain(c) for c in range(c0, min(c0 + NWS, NWT))],
                       skew=3)
        p.fence()

    with contextlib.ExitStack() as ast:
        def asb(name, shape, dt):
            t = ast.enter_context(nc.sbuf_tensor(name, shape, dt))
            return Buf(t, name)
        sqd = asb("sqd", [128, 512], F32)
        sqd1 = asb("sqd1", [128, 512], F32)
        attn_all = asb("attn_all", [128, NQ * NH * 128], BF16)
        ssq2 = [asb(f"ssq{i}", [128, 8], F32) for i in (0, 1)]
        rstd2 = [asb(f"rstd{i}", [128, 8], F32) for i in (0, 1)]
        knorm2 = [asb(f"knorm{i}", [128, 512], BF16) for i in (0, 1)]
        sqd2 = [sqd, sqd1]
        kvq_st = ast.enter_context(contextlib.ExitStack())

        def kvqsb(name, shape, dt):
            return Buf(kvq_st.enter_context(nc.sbuf_tensor(name, shape, dt)),
                       name)
        kfm = kvqsb("kfm", [128, NKV * S], BF16)
        vtm = kvqsb("vtm", [128, (S // 128) * KVW], BF16)
        qfm = kvqsb("qfm", [128, NH * TQ], BF16)

        def g_rms(psrc, nheads, gtile, out_buf, out_ap, sl2=0):
            ssq = ssq2[sl2 % 2]
            rstd = rstd2[sl2 % 2]
            sqd_ = sqd2[sl2 % 2]
            W = nheads * 128
            yield p.op("scalar", lambda W=W, sqd_=sqd_: nc.scalar.activation(
                sqd_.t[:, :W], psrc.t[:, :W], AF.Square),
                reads=[psrc, cpool], writes=[sqd_])
            yield p.op("vector", lambda nheads=nheads, W=W, sqd_=sqd_, ssq=ssq:
                 nc.vector.tensor_reduce(
                     ssq.t[:, :nheads],
                     sqd_.t[:, :W].rearrange("p (h d) -> p h d", d=128),
                     AX.X, OP.add),
                 reads=[sqd_], writes=[ssq])
            yield p.op("scalar", lambda nheads=nheads, ssq=ssq, rstd=rstd:
                 nc.scalar.activation(
                     rstd.t[:, :nheads], ssq.t[:, :nheads], AF.Ln,
                     scale=cap(1.0 / D), bias=cap(eps_rms)),
                 reads=[ssq, cpool], writes=[rstd])
            yield p.op("scalar", lambda nheads=nheads, rstd=rstd:
                 nc.scalar.activation(
                     rstd.t[:, :nheads], rstd.t[:, :nheads], AF.Exp,
                     scale=cap(-0.5)), reads=[rstd, cpool], writes=[rstd])
            yield p.op("vector", lambda nheads=nheads, W=W, rstd=rstd, sqd_=sqd_:
                 nc.vector.tensor_tensor(
                     sqd_.t[:, :W].rearrange("p (h d) -> p h d", d=128),
                     psrc.t[:, :W].rearrange("p (h d) -> p h d", d=128),
                     rstd.t[:, :nheads].broadcast_to([128, nheads, 128]),
                     OP.mult),
                 reads=[psrc, rstd], writes=[sqd_])
            yield p.op("vector", lambda nheads=nheads, W=W, out_ap=out_ap,
                 sqd_=sqd_: nc.vector.tensor_tensor(
                     out_ap[:, :W].rearrange("p (h d) -> p h d", d=128),
                     sqd_.t[:, :W].rearrange("p (h d) -> p h d", d=128),
                     bass.AP(gtile.t, 0, [[int(gtile.t.shape[1]), 128],
                                          [0, nheads], [1, 128]]),
                     OP.mult),
                 reads=[sqd_, gtile], writes=[out_buf])

        def rms_head_cols(psrc, nheads, gtile, out_buf, out_ap, sl2=0):
            drain(g_rms(psrc, nheads, gtile, out_buf, out_ap, sl2))

        with contextlib.ExitStack() as qst:
            def qsb(name, shape, dt):
                return Buf(qst.enter_context(nc.sbuf_tensor(name, shape, dt)),
                           name)
            xqfm = qsb("xqfm", [128, HC * TQ], BF16)
            wqs = qsb("wqs", [128, HC * 512], BF16)
            xtm = [qsb("xtmq0", [128, H], BF16)] * 2
            for tc in range(NTB):
                bx = xtm[tc % 2]
                p.op("gpsimd", lambda tc=tc, bx=bx: nc.gpsimd.dma_start(
                    out=bx.t[:, :], in_=xq_d[tc * 128:(tc + 1) * 128, :]),
                    writes=[bx], dma=True)
                for hc0 in range(0, HC, 4):
                    n = min(4, HC - hc0)
                    transpose_batch(xqfm, xqfm.t, hc0 * TQ + tc * 128, TQ,
                                    bx, bx.t, hc0 * 128, n, hc0 // 4)
            HHC = max(1, HC // 4)
            for oc in range(OCQ):
                items = []
                for g2 in range(HC // HHC):
                    src = wq_d[g2 * HHC * 128:(g2 + 1) * HHC * 128,
                               oc * 512:(oc + 1) * 512]
                    src = src.rearrange("(h p) c -> p h c", p=128)
                    items.append((lambda src=src, g2=g2, HHC=HHC:
                                  nc.gpsimd.dma_start(
                                      out=wqs.t[:, g2 * HHC * 512:
                                                (g2 + 1) * HHC * 512]
                                      .rearrange("p (h c) -> p h c", c=512),
                                      in_=src), [wqs]))
                p.gpdma_group(items)
                def q_chain(oc, tc):
                    pp = psA[tc % 2]
                    def f_q(tc=tc, pp=pp):
                        last = None
                        for hc in range(HC):
                            last = nc.tensor.matmul(
                                pp.t[:, :512],
                                xqfm.t[:, hc * TQ + tc * 128:
                                       hc * TQ + (tc + 1) * 128],
                                wqs.t[:, hc * 512:(hc + 1) * 512],
                                start=(hc == 0), stop=(hc == HC - 1))
                        return last
                    p.op("tensor", f_q, reads=[xqfm, wqs], writes=[pp])
                    yield
                    knorm = knorm2[tc % 2]
                    yield from g_rms(pp, 512 // 128, gq_b, knorm, knorm.t,
                                     tc)
                    transpose_batch(qfm, qfm.t, (oc * 4) * TQ + tc * 128,
                                    TQ, knorm, knorm.t, 0, 512 // 128, tc)
                for tc0 in range(0, NTB, 2):
                    gens = [q_chain(oc, tc0)]
                    if tc0 + 1 < NTB:
                        gens.append(q_chain(oc, tc0 + 1))
                    drain(*gens)
            p.fence()

        with contextlib.ExitStack() as kst:
            def ksb(name, shape, dt):
                return Buf(kst.enter_context(nc.sbuf_tensor(name, shape, dt)),
                           name)
            wkv = ksb("wkv", [128, HC * KVW * 2], BF16)
            xtm = [ksb(f"xtmk{i}", [128, H], BF16) for i in (0, 1)]
            xfmc = [ksb(f"xfmc{i}", [128, HC * 128], BF16) for i in (0, 1)]
            nkvg = max(1, (H // KVW) // 2)
            items = []
            for part, wsrc in ((0, wk_d), (1, wv_d)):
                for h0 in range(0, HC, nkvg):
                    n = min(nkvg, HC - h0)
                    src = wsrc[h0 * 128:(h0 + n) * 128, :]
                    src = src.rearrange("(h p) c -> p h c", p=128)
                    off = part * HC * KVW + h0 * KVW
                    items.append((lambda src=src, off=off, n=n:
                                  nc.gpsimd.dma_start(
                                      out=wkv.t[:, off:off + n * KVW]
                                      .rearrange("p (h c) -> p h c", c=KVW),
                                      in_=src), [wkv]))
            p.gpdma_group(items)
            def kv_chain(tc):
                bx = xtm[tc % 2]
                fx = xfmc[tc % 2]
                p.op("gpsimd", lambda tc=tc, bx=bx: nc.gpsimd.dma_start(
                    out=bx.t[:, :], in_=x_d[tc * 128:(tc + 1) * 128, :]),
                    writes=[bx], dma=True)
                yield
                for hc0 in range(0, HC, 4):
                    n = min(4, HC - hc0)
                    transpose_batch(fx, fx.t, hc0 * 128, 128,
                                    bx, bx.t, hc0 * 128, n, hc0 // 4 + tc)
                    yield
                pp = psA[tc % 2]
                def f_k(fx=fx, pp=pp):
                    last = None
                    for hc in range(HC):
                        last = nc.tensor.matmul(
                            pp.t[:, :KVW],
                            fx.t[:, hc * 128:(hc + 1) * 128],
                            wkv.t[:, hc * KVW:(hc + 1) * KVW],
                            start=(hc == 0), stop=(hc == HC - 1))
                    return last
                p.op("tensor", f_k, reads=[fx, wkv], writes=[pp])
                yield
                knorm = knorm2[tc % 2]
                yield from g_rms(pp, NKV, gk_b, knorm, knorm.t, tc)
                for h0 in range(0, NKV, 4):
                    n = min(4, NKV - h0)
                    transpose_batch(kfm, kfm.t, h0 * S + tc * 128, S,
                                    knorm, knorm.t, h0 * 128, n,
                                    h0 // 4 + tc)
                    yield
                pv = psB[tc % 2]
                def f_v(fx=fx, pv=pv):
                    last = None
                    for hc in range(HC):
                        last = nc.tensor.matmul(
                            pv.t[:, :KVW],
                            fx.t[:, hc * 128:(hc + 1) * 128],
                            wkv.t[:, HC * KVW + hc * KVW:
                                  HC * KVW + (hc + 1) * KVW],
                            start=(hc == 0), stop=(hc == HC - 1))
                    return last
                p.op("tensor", f_v, reads=[fx, wkv], writes=[pv])
                yield
                p.op("vector", lambda tc=tc, pv=pv: nc.vector.tensor_copy(
                    vtm.t[:, tc * KVW:(tc + 1) * KVW], pv.t[:, :KVW]),
                    reads=[pv], writes=[vtm])
            for tc0 in range(0, S // 128, 2):
                gens = [kv_chain(tc0)]
                if tc0 + 1 < S // 128:
                    gens.append(kv_chain(tc0 + 1))
                drain(*gens)
            p.fence()
            p.op("gpsimd", lambda: nc.gpsimd.collective_compute(
                "AllGather", OP.bypass, [list(range(n_cores))],
                [qsh_d.opt()], [qfull_d.opt()]),
                reads=[qsh_b], writes=[qfull_b], cc=True)

        def ssb(name, shape, dt):
            return Buf(kvq_st.enter_context(nc.sbuf_tensor(name, shape, dt)),
                       name)
        kpos_b = ssb("kpos_t", [128, S], F32)
        p.op("gpsimd", lambda: nc.gpsimd.iota(
            kpos_b.t[:, :], [[1, S]], base=0, channel_multiplier=0,
            allow_small_or_imprecise_dtypes=True), writes=[kpos_b])
        sc_f2 = [ssb(f"sc_f{i}", [128, S], F32) for i in (0, 1)]
        af32_2 = [ssb(f"af32{i}", [128, S], F32) for i in (0, 1)]
        stat2 = [ssb(f"stat2_{i}", [128, 4], F32) for i in (0, 1)]
        msk = ssb("msk", [128, S], F32)

        with contextlib.ExitStack() as tst:
            def tsb(name, shape, dt):
                return Buf(tst.enter_context(nc.sbuf_tensor(name, shape, dt)),
                           name)
            abf2 = [tsb(f"abf{i}", [128, S], BF16) for i in (0, 1)]
            atbJ = tsb("atbJ", [128, 2 * S], BF16)
            for qi in range(NQ):
                kl = KLENS[qi]
                nkc = kl // 128
                p.op("vector", lambda qi=qi, kl=kl: nc.vector.tensor_scalar(
                    msk.t[:, :kl], kpos_b.t[:, :kl], qpos_b.t[:, qi:qi + 1],
                    -1e30, OP.is_gt, OP.mult),
                    reads=[kpos_b, qpos_b], writes=[msk])
                def h_chain(qi, h, kl, nkc):
                    kv = h // GR
                    sl = h % 2
                    sc_f = sc_f2[sl]
                    af32 = af32_2[sl]
                    stat = stat2[sl]
                    abf = abf2[sl]
                    psS = psA if sl == 0 else psC
                    for kc0 in range(0, kl, 512):
                        w = min(512, kl - kc0)
                        pp = psS[(kc0 // 512) % 2]
                        p.op("tensor",
                             lambda qi=qi, h=h, kv=kv, kc0=kc0, w=w, pp=pp:
                             nc.tensor.matmul(
                                 pp.t[:, :w],
                                 qfm.t[:, h * TQ + qi * 128:
                                       h * TQ + (qi + 1) * 128],
                                 kfm.t[:, kv * S + kc0:kv * S + kc0 + w],
                                 start=True, stop=True),
                             reads=[qfm, kfm], writes=[pp])
                        p.op("vector", lambda kc0=kc0, w=w, pp=pp, sc_f=sc_f:
                             nc.vector.tensor_add(
                                 sc_f.t[:, kc0:kc0 + w], pp.t[:, :w],
                                 msk.t[:, kc0:kc0 + w]),
                             reads=[pp, msk], writes=[sc_f])
                        yield
                    p.op("vector", lambda kl=kl, sc_f=sc_f, stat=stat:
                         nc.vector.reduce_max(
                             stat.t[:, 0:1], sc_f.t[:, :kl], AX.X,
                             negate=True),
                         reads=[sc_f], writes=[stat])
                    yield
                    p.op("scalar", lambda kl=kl, sc_f=sc_f, af32=af32,
                         stat=stat: nc.scalar.activation(
                             af32.t[:, :kl], sc_f.t[:, :kl], AF.Exp,
                             bias=stat.t[:, 0:1], accum_out=stat.t[:, 1:2]),
                         reads=[sc_f, stat, cpool], writes=[af32, stat])
                    yield
                    p.op("vector", lambda stat=stat: nc.vector.reciprocal(
                        stat.t[:, 2:3], stat.t[:, 1:2]),
                        reads=[stat], writes=[stat])
                    yield
                    p.op("scalar", lambda kl=kl, af32=af32, stat=stat,
                         abf=abf: nc.scalar.activation(
                             abf.t[:, :kl], af32.t[:, :kl], AF.Identity,
                             scale=stat.t[:, 2:3]),
                         reads=[af32, stat, cpool], writes=[abf])
                    yield
                    for c0 in range(0, nkc, 4):
                        n = min(4, nkc - c0)
                        transpose_batch(atbJ, atbJ.t,
                                        (2 * c0 + sl) * 128, 256,
                                        abf, abf.t, c0 * 128, n,
                                        c0 // 4 + sl)
                        yield
                for h0 in range(0, NH, 2):
                    drain(h_chain(qi, h0, kl, nkc),
                          h_chain(qi, h0 + 1, kl, nkc))
                    kv = h0 // GR
                    pa = psB[(h0 // 2) % 2]
                    def f_av(kv=kv, nkc=nkc, pa=pa):
                        last = None
                        for c in range(nkc):
                            last = nc.tensor.matmul(
                                pa.t[:, :256],
                                vtm.t[:, c * KVW + kv * 128:
                                      c * KVW + (kv + 1) * 128],
                                atbJ.t[:, 2 * c * 128:(2 * c + 2) * 128],
                                start=(c == 0), stop=(c == nkc - 1))
                        return last
                    p.op("tensor", f_av, reads=[vtm, atbJ], writes=[pa])
                    dsl = attn_all.t[:, (qi * NH + h0) * 128:
                                     (qi * NH + h0 + 2) * 128]
                    p.op("vector", lambda dsl=dsl, pa=pa:
                         nc.vector.tensor_copy(dsl, pa.t[:, :256]),
                         reads=[pa], writes=[attn_all])
            p.fence(exclude=("s_cc",))

        kvq_st.close()

        with contextlib.ExitStack() as ost:
            def osb(name, shape, dt):
                return Buf(ost.enter_context(nc.sbuf_tensor(name, shape, dt)),
                           name)
            scw2 = [osb(f"sc_f2{i}", [128, H], F32) for i in (0, 1)]
            afw2 = [osb(f"af32_2{i}", [128, H], F32) for i in (0, 1)]
            xres2 = [osb(f"xres{i}", [128, H], F32) for i in (0, 1)]
            wos = osb("wos", [128, (NH * D // 128) * H], BF16)
            items = []
            for h0 in range(0, NH * D // 128, 2):
                n = min(2, NH * D // 128 - h0)
                src = wo_d[h0 * 128:(h0 + n) * 128, :]
                src = src.rearrange("(h p) c -> p h c", p=128)
                items.append((lambda src=src, h0=h0, n=n:
                              nc.gpsimd.dma_start(
                                  out=wos.t[:, h0 * H:(h0 + n) * H]
                                  .rearrange("p (h c) -> p h c", c=H),
                                  in_=src), [wos]))
            p.gpdma_group(items)
            def wo_chain(qi):
                scw = scw2[qi % 2]
                afw = afw2[qi % 2]
                xres = xres2[qi % 2]
                psS = psA if qi % 2 == 0 else psC
                p.op("sync", lambda qi=qi, xres=xres: nc.sync.dma_start(
                    out=xres.t[:, :H], in_=xq_d[qi * 128:(qi + 1) * 128, :]),
                    writes=[xres], dma=True)
                yield
                for oc in range(H // 512):
                    pp = psS[oc % 2]
                    def f_wo(qi=qi, oc=oc, pp=pp):
                        last = None
                        for h in range(NH):
                            last = nc.tensor.matmul(
                                pp.t[:, :512],
                                attn_all.t[:, (qi * NH + h) * 128:
                                           (qi * NH + h + 1) * 128],
                                wos.t[:, h * H + oc * 512:
                                      h * H + (oc + 1) * 512],
                                start=(h == 0), stop=(h == NH - 1))
                        return last
                    p.op("tensor", f_wo, reads=[attn_all, wos], writes=[pp])
                    yield
                    p.op("vector", lambda oc=oc, pp=pp, scw=scw, xres=xres:
                         nc.vector.tensor_add(
                             scw.t[:, oc * 512:(oc + 1) * 512], pp.t[:, :512],
                             xres.t[:, oc * 512:(oc + 1) * 512]),
                         reads=[pp, xres], writes=[scw])
                    yield
                yield from g_layernorm(scw, afw, x1,
                                       x1.t[:, qi * H:(qi + 1) * H],
                                       lnstat2[qi % 2])
            for qi0 in range(0, NQ, 2):
                gens = [wo_chain(qi0)]
                if qi0 + 1 < NQ:
                    gens.append(wo_chain(qi0 + 1))
                drain(*gens)
            p.fence(exclude=("s_cc",))

    st_toks = []
    with contextlib.ExitStack() as fst:
        def fsb(name, shape, dt):
            return Buf(fst.enter_context(nc.sbuf_tensor(name, shape, dt)),
                       name)
        xqfm2 = fsb("xqfm2", [128, HC * TQ], BF16)
        interq = fsb("interq", [128, (IQ // 128) * TQ], BF16)
        x2acc = fsb("x2acc", [128, NTB * H], BF16)
        with contextlib.ExitStack() as f1st:
            def f1sb(name, shape, dt):
                return Buf(f1st.enter_context(
                    nc.sbuf_tensor(name, shape, dt)), name)
            NSET = min(4, NTB)
            JW = NSET * 512
            JH = JW // 2
            scrJ2 = [{k: f1sb(f"scrJ{i}_" + k, [128, JH], F32)
                      for k in "yuabcd"} for i in (0, 1)]
            HCH = HC // 2
            wb2 = [f1sb(f"wb{i}", [128, HCH * 512], BF16) for i in (0, 1)]
            ICH = max(1, (IQ // 128) // 2)
            wd2 = [f1sb(f"wdh{i}", [128, ICH * 512], BF16) for i in (0, 1)]
            s_sb = f1sb("s_sb", [128, NTB * 512], BF16)
            vjoint = f1sb("vjoint", [128, JW], F32)
            qmFB = {n: f1sb("qmFB_" + n, [128, NSET * 16], F32)
                    for n in ("amax", "z", "rem", "fl", "invs", "back")}
            qmF2 = [qm0, qmFB]
            e_sb2 = [f1sb(f"e_sb{i}", [128, 512], F32) for i in (0, 1)]
            l_sb2 = [f1sb(f"l_sb{i}", [128, 512], F32) for i in (0, 1)]
            xqa2j = [f1sb(f"xqaj{i}", [128, JH], BF16) for i in (0, 1)]
            ps4 = [psA[0], psA[1], psC[0], psC[1]]


            xw = min(H, JW)

            def x1_half(tb, sub, qmf, j, hw):
                xqa = xqa2j[j]
                yield from g_quant(scrJ2[j], vjoint, xqa,
                                   xqa.t[:, :hw], hw, 'act', qm=qmf,
                                   so=j * hw, nbo=j * (hw // 32),
                                   maps=False)
                hc0 = sub * (xw // 128) + j * (hw // 128)
                for cb in range(0, hw // 128, 4):
                    n = min(4, hw // 128 - cb)
                    transpose_batch(
                        xqfm2, xqfm2.t, (hc0 + cb) * TQ + tb * 128, TQ,
                        xqa, xqa.t, cb * 128, n, cb // 4 + j)
                    yield

            ci = 0
            for tb in range(NTB):
                for sub in range(H // xw):
                    qmf = qmF2[ci % 2]
                    p.op("vector", lambda tb=tb, sub=sub:
                         nc.vector.tensor_copy(
                             vjoint.t[:, :xw],
                             x1.t[:, tb * H + sub * xw:
                                  tb * H + (sub + 1) * xw]),
                         reads=[x1], writes=[vjoint])
                    drain(g_maps(vjoint, xw, 'act', qmf))
                    hw = xw // 2 if xw >= 1024 else xw
                    if hw < xw:
                        drain_skew([x1_half(tb, sub, qmf, 0, hw),
                                    x1_half(tb, sub, qmf, 1, hw)], skew=3)
                    else:
                        drain(x1_half(tb, sub, qmf, 0, hw))
                    ci += 1

            n5 = IQ // 512

            def ld_w(row0, half):
                eng = "sync"
                for hci in range(HCH):
                    hc = half * HCH + hci
                    p.op(eng, lambda row0=row0, hc=hc, hci=hci, eng=eng,
                         wb=wb2[half]:
                         getattr(nc, eng).dma_start_transpose(
                             out=wb.t[:, hci * 512:(hci + 1) * 512],
                             in_=qfull_d[row0:row0 + 512,
                                         hc * 128:(hc + 1) * 128]),
                         reads=[qfull_b], writes=[wb2[half]], dma=True)

            def mm_half(tb, pp, half, start, stop):
                def f(tb=tb, pp=pp, half=half, start=start, stop=stop):
                    last = None
                    for hci in range(HCH):
                        hc = half * HCH + hci
                        last = nc.tensor.matmul(
                            pp.t[:, :512],
                            xqfm2.t[:, hc * TQ + tb * 128:
                                    hc * TQ + (tb + 1) * 128],
                            wb2[half].t[:, hci * 512:(hci + 1) * 512],
                            start=(start and hci == 0),
                            stop=(stop and hci == HCH - 1))
                    return last
                p.op("tensor", f, reads=[xqfm2, wb2[half]], writes=[pp])

            assert n5 == 1
            NI = NIQ

            def gate_phase(i512):
                def gate_chain(tb):
                    pp = ps4[tb % 4]
                    mm_half(tb, pp, 0, True, False)
                    yield
                    mm_half(tb, pp, 1, False, True)
                    yield
                    yield
                    e_sb = e_sb2[tb % 2]
                    l_sb = l_sb2[tb % 2]
                    p.op("scalar", lambda pp=pp, e_sb=e_sb:
                         nc.scalar.activation(
                             e_sb.t[:, :512], pp.t[:, :512], AF.Exp,
                             scale=cap(-1.0)),
                         reads=[pp, cpool], writes=[e_sb])
                    yield
                    p.op("scalar", lambda e_sb=e_sb, l_sb=l_sb:
                         nc.scalar.activation(
                             l_sb.t[:, :512], e_sb.t[:, :512], AF.Ln,
                             bias=cap(1.0)),
                         reads=[e_sb, cpool], writes=[l_sb])
                    yield
                    p.op("scalar", lambda e_sb=e_sb, l_sb=l_sb:
                         nc.scalar.activation(
                             e_sb.t[:, :512], l_sb.t[:, :512], AF.Exp,
                             scale=cap(-1.0)),
                         reads=[l_sb, cpool], writes=[e_sb])
                    yield
                    p.op("vector", lambda tb=tb, pp=pp, e_sb=e_sb:
                         nc.vector.tensor_mul(
                             s_sb.t[:, tb * 512:(tb + 1) * 512],
                             e_sb.t[:, :512], pp.t[:, :512]),
                         reads=[e_sb, pp], writes=[s_sb])
                for w0 in range(0, NTB, 4):
                    wg = [gate_chain(tb)
                          for tb in range(w0, min(w0 + 4, NTB))]
                    for c in wg:
                        next(c)
                    if w0 + 4 >= NTB:
                        ld_w(WU0 + i512 * 512, 0)
                    for c in wg:
                        next(c)
                    if w0 + 4 >= NTB:
                        ld_w(WU0 + i512 * 512, 1)
                    for j0 in range(0, len(wg), 2):
                        drain(*wg[j0:j0 + 2])

            def ld_wd(iq, oc, half):
                for ici in range(ICH):
                    ic = half * ICH + ici
                    col = iq * IQ + ic * 128
                    cc, col_in = col // H, col % H
                    row0 = WD0 + cc * H + oc * 512
                    p.op("sync", lambda row0=row0, col_in=col_in,
                         ici=ici, wdb=wd2[half]:
                         nc.sync.dma_start_transpose(
                             out=wdb.t[:, ici * 512:(ici + 1) * 512],
                             in_=qfull_d[row0:row0 + 512,
                                         col_in:col_in + 128]),
                         reads=[qfull_b], writes=[wd2[half]], dma=True)

            def up_phase(i512):
                def up_chain(tb):
                    j = tb % 4
                    pp = ps4[j]
                    mm_half(tb, pp, 0, True, False)
                    yield
                    mm_half(tb, pp, 1, False, True)
                    yield
                    yield
                    p.op("vector", lambda tb=tb, pp=pp, j=j:
                         nc.vector.tensor_mul(
                             vjoint.t[:, j * 512:(j + 1) * 512],
                             s_sb.t[:, tb * 512:(tb + 1) * 512],
                             pp.t[:, :512]),
                         reads=[s_sb, pp], writes=[vjoint])

                def half_q(wave, qmf, j, hw):
                    xqa = xqa2j[j]
                    yield from g_quant(scrJ2[j], vjoint, xqa,
                                       xqa.t[:, :hw], hw, 'act',
                                       qm=qmf, so=j * hw,
                                       nbo=j * (hw // 32), maps=False)
                    nt = hw // 512
                    for jj in range(nt):
                        tb = wave[j * nt + jj]
                        transpose_batch(interq, interq.t,
                                        tb * 128, TQ,
                                        xqa, xqa.t, jj * 512, 4,
                                        jj + j)
                        yield

                ni = i512 + 1
                for w0 in range(0, NTB, 4):
                    wave = range(w0, min(w0 + 4, NTB))
                    wu_ = [up_chain(tb) for tb in wave]
                    for c in wu_:
                        next(c)
                    if w0 + 4 >= NTB and ni < NI:
                        ld_w(ni * 512, 0)
                    for c in wu_:
                        next(c)
                    if w0 + 4 >= NTB and ni < NI:
                        ld_w(ni * 512, 1)
                    drain(*wu_)
                    qmf = qmF2[(w0 // 4) % 2]
                    wn = len(wave)
                    drain(g_maps(vjoint, wn * 512, 'act', qmf))
                    if wn >= 2:
                        hw = wn * 512 // 2
                        drain_skew([half_q(wave, qmf, 0, hw),
                                    half_q(wave, qmf, 1, hw)],
                                   skew=3)
                    else:
                        drain(half_q(wave, qmf, 0, wn * 512))
                ld_wd(i512, 0, 0)
                ld_wd(i512, 0, 1)

            def down_phase(iq):
                def down_chain(oc, tb):
                    pp = ps4[tb % 4]
                    for half in range(2):
                        def f(tb=tb, pp=pp, half=half):
                            last = None
                            for ici in range(ICH):
                                ic = half * ICH + ici
                                last = nc.tensor.matmul(
                                    pp.t[:, :512],
                                    interq.t[:, ic * TQ + tb * 128:
                                             ic * TQ + (tb + 1) * 128],
                                    wd2[half].t[:, ici * 512:
                                                (ici + 1) * 512],
                                    start=(half == 0 and ici == 0),
                                    stop=(half == 1 and ici == ICH - 1))
                            return last
                        p.op("tensor", f, reads=[interq, wd2[half]],
                             writes=[pp])
                        yield
                    dsl = x2acc.t[:, tb * H + oc * 512:
                                  tb * H + (oc + 1) * 512]
                    if iq == 0:
                        p.op("vector", lambda dsl=dsl, pp=pp:
                             nc.vector.tensor_copy(dsl, pp.t[:, :512]),
                             reads=[pp], writes=[x2acc])
                    else:
                        p.op("vector", lambda dsl=dsl, pp=pp:
                             nc.vector.tensor_add(dsl, dsl,
                                                  pp.t[:, :512]),
                             reads=[pp, x2acc], writes=[x2acc])
                for oc in range(H // 512):
                    for w0 in range(0, NTB, 4):
                        wd_ = [down_chain(oc, tb)
                               for tb in range(w0, min(w0 + 4, NTB))]
                        for c in wd_:
                            next(c)
                        if w0 + 4 >= NTB and oc + 1 < H // 512:
                            ld_wd(iq, oc + 1, 0)
                        for c in wd_:
                            next(c)
                        if w0 + 4 >= NTB and oc + 1 < H // 512:
                            ld_wd(iq, oc + 1, 1)
                        drain(*wd_)

            ld_w(0, 0)
            ld_w(0, 1)
            gate_phase(0)
            for i in range(NI):
                up_phase(i)
                if i + 1 < NI:
                    gate_phase(i + 1)
                down_phase(i)
            p.fence(exclude=("s_cc",))

        with contextlib.ExitStack() as f2st:
            def f2sb(name, shape, dt):
                return Buf(f2st.enter_context(
                    nc.sbuf_tensor(name, shape, dt)), name)
            x2f2 = [f2sb(f"x2f{i}", [128, H], F32) for i in range(4)]
            lnscr2 = [f2sb(f"lnscr{i}", [128, H], F32) for i in range(4)]
            oln2 = [f2sb(f"oln{i}", [128, H], F32) for i in range(4)]
            lnstat4 = [lnstat2[0], lnstat2[1],
                       f2sb("lnstat2", [128, 8], F32),
                       f2sb("lnstat3", [128, 8], F32)]

            def fin_chain(tb):
                x2f = x2f2[tb % 4]
                lnscr = lnscr2[tb % 4]
                oln = oln2[tb % 4]
                p.op("vector", lambda tb=tb, x2f=x2f: nc.vector.tensor_add(
                    x2f.t[:, :H], x1.t[:, tb * H:(tb + 1) * H],
                    x2acc.t[:, tb * H:(tb + 1) * H]),
                    reads=[x1, x2acc], writes=[x2f])
                yield
                yield from g_layernorm(x2f, lnscr, oln, oln.t[:, :H],
                                       lnstat4[tb % 4])
                tok = p.op("sync", lambda tb=tb, oln=oln: nc.sync.dma_start(
                    out=out_d[tb * 128:(tb + 1) * 128, :], in_=oln.t[:, :H]),
                    reads=[oln], dma=True)
                st_toks.append(tok)
            for tb0 in range(0, NTB, 4):
                gens = [fin_chain(tb) for tb in
                        range(tb0, min(tb0 + 4, NTB))]
                drain_skew(gens, skew=2)
            p.fence(exclude=("s_cc",))

    p.finish(st_toks)
    p.run_block()
    stack.close()
    return nc



_NC_CACHE = {}


def build_for(S, H, NH, NKV, I, n_cores=8):
    key = (S, H, NH, NKV, I, n_cores)
    if key not in _NC_CACHE:
        _NC_CACHE[key] = build_program(S, H, NH, NKV, I, n_cores=n_cores)
    return _NC_CACHE[key]


def weight_shards(wg, wu, wd, H, n_cores):
    I = wg.shape[0]
    tiles = [wg, wu]
    for cc in range(I // H):
        tiles.append(np.ascontiguousarray(wd[:, cc * H:(cc + 1) * H]))
    full = np.concatenate(tiles, axis=0)
    rows_pc = full.shape[0] // n_cores
    return [np.ascontiguousarray(full[c * rows_pc:(c + 1) * rows_pc])
            for c in range(n_cores)]


def kernel(L0_attn_x, L0_attn_Wq, L0_attn_Wk, L0_attn_Wv, L0_attn_Wo,
           L0_attn_gq, L0_attn_gk, L0_ffn_w_gate, L0_ffn_w_up, L0_ffn_w_down):
    x = np.ascontiguousarray(L0_attn_x, dtype=np.float32)
    B, S, H = x.shape
    NB = S // 128
    I = L0_ffn_w_gate.shape[0]
    n_cores = 2 * B
    nc = build_for(S, H, 16, 4, I, n_cores)
    shards = weight_shards(
        np.ascontiguousarray(L0_ffn_w_gate, dtype=np.float32),
        np.ascontiguousarray(L0_ffn_w_up, dtype=np.float32),
        np.ascontiguousarray(L0_ffn_w_down, dtype=np.float32), H, n_cores)
    in_maps = []
    metas = []
    for c in range(n_cores):
        b, par = c // 2, c % 2
        js = qblocks_for_parity(NB, par)
        rows = np.concatenate([np.arange(j * 128, (j + 1) * 128)
                               for j in js])
        in_maps.append({
            "x": x[b],
            "xqrows": np.ascontiguousarray(x[b][rows]),
            "qpos": np.ascontiguousarray(rows.astype(np.float32)[None, :]),
            "wq": np.ascontiguousarray(L0_attn_Wq, dtype=np.float32),
            "wk": np.ascontiguousarray(L0_attn_Wk, dtype=np.float32),
            "wv": np.ascontiguousarray(L0_attn_Wv, dtype=np.float32),
            "wo": np.ascontiguousarray(L0_attn_Wo, dtype=np.float32),
            "gq": np.ascontiguousarray(L0_attn_gq, dtype=np.float32)[None, :],
            "gk": np.ascontiguousarray(L0_attn_gk, dtype=np.float32)[None, :],
            "wshard": shards[c],
        })
        metas.append((b, js))
    res = run_bass_kernel_spmd(nc, in_maps, core_ids=list(range(len(in_maps))))
    out = np.empty((B, S, H), dtype=np.float32)
    for c, r in enumerate(res.results):
        b, js = metas[c]
        o = np.asarray(r["out"])
        for qi, j in enumerate(js):
            out[b, j * 128:(j + 1) * 128] = o[qi * 128:(qi + 1) * 128]
    return out
